# version 03
# baseline (speedup 1.0000x reference)
import contextlib

import numpy as np

import concourse.bass as bass
import concourse.mybir as mybir
from concourse.bass_utils import run_bass_kernel_spmd

F32 = mybir.dt.float32
BF16 = mybir.dt.bfloat16
AF = mybir.ActivationFunctionType
ALU = mybir.AluOpType
AX = mybir.AxisListType.X

D = 2048
NCORE = 8
NBATCH = 2
SEQ = 2048
NTOK = NBATCH * SEQ
L = 1024
HID = [1024, 2048, 3072, 4096, 1024, 2048, 3072, 4096]
COST_LAMBDA = 1e-7
LNEPS = 1e-5
NKH, NVH, DK, DV = 16, 32, 128, 128
KCONV = 4
RMSEPS = 1e-6
KEY_DIM = NKH * DK
VALUE_DIM = NVH * DV
CONV_DIM = 2 * KEY_DIM + VALUE_DIM
SC, CH = 512, 128
NCH = SC // CH
FB = 512
NEG_BIG = -1e30
PANEL = 256


def panelw(name):
    if name == "L0_attn_W_qkvz":
        return 128
    K = WMATS[name][0]
    return 128 if K > 2048 else 256


def qkvz_perm(bw):
    kh, r = divmod(bw, 6)
    if r == 0:
        return kh
    if r == 1:
        return 16 + kh
    if r in (2, 3):
        return 32 + 2 * kh + (r - 2)
    return 64 + 2 * kh + (r - 4)

WMATS = {
    "L0_attn_W_qkvz": (D, 12288),
    "L0_attn_W_ba": (D, 64),
    "L0_attn_W_out": (VALUE_DIM, D),
    "L0_ffn_down_w": (D, L),
    "L0_ffn_router_w": (D, 8),
    "L0_ffn_up_w": (L, D),
    "L0_ffn_core_w": (D, D),
    "L0_ffn_shared_w1": (L, L),
    "L0_ffn_shared_w2": (L, L),
}
for _e, _h in enumerate(HID):
    WMATS[f"L0_ffn_e{_e}_w1"] = (L, _h)
    WMATS[f"L0_ffn_e{_e}_w2"] = (_h, L)

VECS = {
    "L0_attn_conv_w": CONV_DIM * KCONV,
    "L0_attn_dt_bias": NVH,
    "L0_attn_A_log": NVH,
    "L0_attn_norm_w": DV,
    "L0_ffn_down_b": L,
    "L0_ffn_up_b": D,
    "L0_ffn_router_b": 8,
    "L0_ffn_shared_b1": L,
    "L0_ffn_shared_b2": L,
    "L0_ffn_core_b": D,
}
for _e, _h in enumerate(HID):
    VECS[f"L0_ffn_e{_e}_b1"] = _h
    VECS[f"L0_ffn_e{_e}_b2"] = L


def ceil_div(a, b):
    return -(-a // b)


class Builder:

    ENGS = ("t", "v", "s", "g", "y")

    NDMASEM = 8

    def __init__(self, nc):
        self.nc = nc
        self.es = contextlib.ExitStack()
        self.stream = {e: [] for e in self.ENGS}
        self.sems = {}
        names = ["t", "v", "s", "g", "y"]
        for q in ("dy", "ds", "dg"):
            for i in range(self.NDMASEM):
                names.append(f"{q}{i}")
        for name in names:
            self.sems[name] = self.es.enter_context(nc.semaphore("sem_" + name))
        self.cnt = {name: 0 for name in self.sems}
        self.rr = {"dy": 0, "ds": 0, "dg": 0}
        self.waited = {e: {} for e in self.ENGS}
        self.track = {}
        self.same_engine_sync = True

    def _acc(self, key, lo, hi, pt, write):
        ivs = self.track.setdefault(key, [])
        deps = []
        if write:
            keep = []
            nlo, nhi = lo, hi
            for iv in ivs:
                if iv[1] <= lo or iv[0] >= hi:
                    keep.append(iv)
                else:
                    if iv[2] is not None:
                        deps.append(iv[2])
                    deps.extend(iv[3])
                    nlo = min(nlo, iv[0])
                    nhi = max(nhi, iv[1])
            keep.append([nlo, nhi, pt, []])
            self.track[key] = keep
        else:
            hit = False
            for iv in ivs:
                if iv[1] <= lo or iv[0] >= hi:
                    continue
                hit = True
                if iv[2] is not None:
                    deps.append(iv[2])
                iv[3].append(pt)
            if not hit:
                ivs.append([lo, hi, None, [pt]])
        return deps

    def emit(self, eng, fn, reads=(), writes=(), dma=False, group_n=1):
        if dma:
            q = "d" + eng
            sem = f"{q}{self.rr[q]}"
            self.rr[q] = (self.rr[q] + 1) % self.NDMASEM
            inc = 16
        else:
            sem = eng
            inc = 1
        pt = (sem, self.cnt[sem] + inc)
        wpt = (sem, self.cnt[sem] + inc * group_n)
        deps = []
        if dma and self.cnt[sem] > 0:
            deps.append((sem, self.cnt[sem]))
        for r in reads:
            deps += self._acc(r[0], r[1], r[2], pt, False)
        for w in writes:
            deps += self._acc(w[0], w[1], w[2], wpt, True)
        cnt_before = pt[1] - inc
        self.cnt[sem] = pt[1]
        wd = self.waited[eng]
        bys = {}
        for (s, v) in deps:
            if s == sem and v > cnt_before:
                continue
            if s == eng and not self.same_engine_sync:
                continue
            if wd.get(s, 0) >= v:
                continue
            bys[s] = max(bys.get(s, 0), v)
        for s, v in bys.items():
            wd[s] = v
        self.stream[eng].append((list(bys.items()), fn, sem, inc))
        return pt

    def barrier(self, engs=None):
        engs = engs or self.ENGS
        tgt = dict(self.cnt)
        for e in engs:
            waits = []
            for s, v in tgt.items():
                if v > 0 and self.waited[e].get(s, 0) < v:
                    self.waited[e][s] = v
                    waits.append((s, v))
            if waits:
                self.stream[e].append((waits, None, None, 0))

    def finalize(self):
        sems = self.sems
        with self.nc.Block() as block:
            def mk(eng_key):
                def body(e):
                    for waits, fn, sem, inc in self.stream[eng_key]:
                        for s, v in waits:
                            e.wait_ge(sems[s], v)
                        if fn is not None:
                            fn(e).then_inc(sems[sem], inc)
                return body
            block.tensor(mk("t"))
            block.vector(mk("v"))
            block.scalar(mk("s"))
            block.gpsimd(mk("g"))
            block.sync(mk("y"))


def build_program(parts=("ffn1", "attn", "ffn2")):
    nc = bass.Bass("TRN2", target_bir_lowering=False)
    B = Builder(nc)
    es = B.es

    x_in = nc.dram_tensor("L0_attn_x", [D, NTOK], F32, kind="ExternalInput").ap()
    out = nc.dram_tensor("OUT", [D, NTOK], F32, kind="ExternalOutput").ap()
    wmat = {n: nc.dram_tensor(n, list(s), F32, kind="ExternalInput").ap()
            for n, s in WMATS.items()}
    wvec = {n: nc.dram_tensor(n, [s], F32, kind="ExternalInput").ap()
            for n, s in VECS.items()}
    wscr = {n: nc.dram_tensor("scr_" + n, [K * N], BF16, kind="Internal").ap()
            for n, (K, N) in WMATS.items()}
    xA = nc.dram_tensor("xA", [D, NTOK], F32, kind="Internal").ap()
    xB = nc.dram_tensor("xB", [D, NTOK], F32, kind="Internal").ap()
    rowscr = nc.dram_tensor("rowscr", [16384], F32, kind="Internal").ap()
    zscr = nc.dram_tensor("zscr", [NVH * 128 * SC], BF16, kind="Internal").ap()
    dbg = nc.dram_tensor("DBG", [128, 2048], F32, kind="Internal").ap()

    def sb(name, p, w, dt):
        return es.enter_context(nc.sbuf_tensor(name, [p, w], dt))

    bigA = sb("bigA", 128, 38912, BF16)
    bigB = sb("bigB", 128, 8192, BF16)
    bigC = sb("bigC", 128, 8192, BF16)
    bigD = sb("bigD", 128, 8192, BF16)
    bigE = sb("bigE", 128, 4096, F32)
    S_sb = sb("S_sb", 128, 4096, F32)
    xfst = sb("xfst", 128, 1024, F32)
    hist = sb("hist", 128, 192, F32)
    cols = sb("cols", 128, 384, F32)
    ones2d = sb("ones2d", 128, 128, F32)
    gb1c = sb("gb1c", 128, 176, F32)
    gb1cb = sb("gb1cb", 128, 176, BF16)
    gb1c_raw = sb("gb1c_raw", 128, 176, F32)
    identf = sb("identf", 128, 128, F32)
    identb = sb("identb", 128, 128, BF16)
    mask_t = sb("mask_t", 128, 128, BF16)
    mask_i = sb("mask_i", 128, 128, BF16)
    mask_sn = sb("mask_sn", 128, 128, BF16)
    onesf = sb("onesf", 1, 128, F32)
    onesb = sb("onesb", 1, 512, BF16)
    onescb = sb("onescb", 128, 1, BF16)
    selt = sb("selt", 8, 1024, BF16)
    rowA = sb("rowA", 1, 2048, F32)
    rowB = sb("rowB", 1, 1024, F32)
    rowsb = sb("rowsb", 1, 3328, BF16)
    D8pm = sb("D8pm", 8, 1024, BF16)
    w8f = sb("w8f", 128, 64, F32)
    w8tb = sb("w8tb", 8, 512, BF16)
    wbt = sb("wbt", 128, 512, BF16)
    stat = rowA
    bg = sb("bg", 128, 512, F32)
    bg2 = sb("bg2", 128, 256, F32)
    tokc = sb("tokc", 128, 1024, F32)
    prob = sb("prob", 128, 64, F32)
    iot = sb("iot", 128, 128, F32)

    ps = [es.enter_context(nc.psum_tensor(f"ps{i}", [128, 512], F32)) for i in range(6)]
    pbf = [es.enter_context(nc.psum_tensor(f"pbf{i}", [128, 1024], BF16)) for i in range(2)]

    def R(t, lo, hi):
        return (t, lo, hi)

    A, Bk, C, Dk, E = "bigA", "bigB", "bigC", "bigD", "bigE"
    PS = [f"ps{i}" for i in range(6)]
    PB = ["pbf0", "pbf1"]

    def dma(eng, out_ap, in_ap, reads, writes):
        return B.emit(eng, lambda e: e.dma_start(out_ap, in_ap), reads=reads,
                      writes=writes, dma=True)

    def vop(fn, reads, writes):
        return B.emit("v", fn, reads=reads, writes=writes)

    def sop(fn, reads, writes):
        return B.emit("s", fn, reads=reads, writes=writes)

    def gop(fn, reads, writes):
        return B.emit("g", fn, reads=reads, writes=writes)

    def top(fn, reads, writes, group_n=1):
        return B.emit("t", fn, reads=reads, writes=writes, group_n=group_n)

    def xiv(xkey, j, t0, w=512):
        return (xkey, j * NTOK + t0, j * NTOK + t0 + w)

    def setup_constants():
        gop(lambda e: e.iota(iot[:, :], [[1, 128]], channel_multiplier=-1,
                             allow_small_or_imprecise_dtypes=True),
            [], [R("iot", 0, 128)])
        gop(lambda e: e.memset(onesf[:, :], 1.0), [], [R("onesf", 0, 128)])
        gop(lambda e: e.memset(onesb[:, :], 1.0), [], [R("onesb", 0, 512)])
        gop(lambda e: e.memset(onescb[:, :], 1.0), [], [R("onescb", 0, 1)])
        gop(lambda e: e.memset(ones2d[:, :], 1.0), [], [R("ones2d", 0, 128)])
        vop(lambda e: e.tensor_single_scalar(identf[:, :], iot[:, :], 0.0, ALU.is_equal),
            [R("iot", 0, 128)], [R("identf", 0, 128)])
        vop(lambda e: e.tensor_copy(identb[:, :], identf[:, :]),
            [R("identf", 0, 128)], [R("identb", 0, 128)])
        vop(lambda e: e.tensor_single_scalar(mask_t[:, :], iot[:, :], 0.0, ALU.is_gt),
            [R("iot", 0, 128)], [R("mask_t", 0, 128)])
        vop(lambda e: e.tensor_single_scalar(mask_i[:, :], iot[:, :], 0.0, ALU.is_ge),
            [R("iot", 0, 128)], [R("mask_i", 0, 128)])
        vop(lambda e: e.tensor_scalar(mask_sn[:, :], iot[:, :], 0.0, -1.0,
                                      ALU.is_lt, ALU.mult),
            [R("iot", 0, 128)], [R("mask_sn", 0, 128)])
        gop(lambda e: e.iota(bigE[0:8, 0:128], [[0, 128]], channel_multiplier=1,
                             allow_small_or_imprecise_dtypes=True),
            [], [R(E, 0, 128)])
        for ee in range(8):
            vop(lambda e, ee=ee: e.tensor_single_scalar(
                    selt[0:8, ee * 128:(ee + 1) * 128], bigE[0:8, 0:128],
                    float(ee), ALU.is_equal),
                [R(E, 0, 128)], [R("selt", ee * 128, (ee + 1) * 128)])

    _colptr = [0]
    CI = {}

    def colify(row_t, row_key, off, n, dst_t, dst_name, dst_off, stride=1):
        pbk = ps[5]
        if stride == 1:
            src_ap = row_t[0:1, off:off + n]
        else:
            src_ap = bass.AP(row_t, off, [[row_t.shape[1], 1], [stride, n]])
        top(lambda e: e.matmul(pbk[:, 0:n], onesf[0:1, 0:128], src_ap,
                               start=True, stop=True),
            [R(row_key, 0, row_t.shape[1]), R("onesf", 0, 128)], [R("ps5", 0, 512)])
        vop(lambda e: e.tensor_mul(prob[:, 0:n] if n <= 64 else bigE[:, 0:n],
                                   pbk[:, 0:n], identf[:, 0:n]),
            [R("ps5", 0, 512), R("identf", 0, 128)],
            [R("prob", 0, 64) if n <= 64 else R(E, 0, 128)])
        srct = prob if n <= 64 else bigE
        srck = R("prob", 0, 64) if n <= 64 else R(E, 0, 128)
        vop(lambda e: e.reduce_sum(dst_t[:, dst_off:dst_off + 1], srct[:, 0:n], axis=AX),
            [srck], [(dst_name, dst_off, dst_off + 1)])

    def ccol(row_t, row_key, off, n, stride=1):
        idx = _colptr[0]
        _colptr[0] += 1
        colify(row_t, row_key, off, n, cols, "cols", idx, stride)
        return idx

    def load_row(dst_off, name, lo, hi, row_tensor, row_key):
        dma("y", row_tensor[0:1, dst_off:dst_off + (hi - lo)], wvec[name][lo:hi],
            [], [(row_key, dst_off, dst_off + (hi - lo))])

    def setup_vectors():
        load_row(0, "L0_ffn_down_b", 0, L, rowB, "rowB")
        CI["lneps"] = _colptr[0]; _colptr[0] += 1
        CI["rmseps"] = _colptr[0]; _colptr[0] += 1
        CI["one"] = _colptr[0]; _colptr[0] += 1
        vop(lambda e: e.memset(cols[:, CI["lneps"]:CI["lneps"] + 1], LNEPS),
            [], [R("cols", CI["lneps"], CI["lneps"] + 1)])
        vop(lambda e: e.memset(cols[:, CI["rmseps"]:CI["rmseps"] + 1], RMSEPS),
            [], [R("cols", CI["rmseps"], CI["rmseps"] + 1)])
        vop(lambda e: e.memset(cols[:, CI["one"]:CI["one"] + 1], 1.0),
            [], [R("cols", CI["one"], CI["one"] + 1)])
        CI["down_b"] = _colptr[0]
        for j in range(8):
            ccol(rowB, "rowB", j * 128, 128)
        for half in range(2):
            load_row(0, "L0_ffn_up_b", half * 1024, (half + 1) * 1024, rowB, "rowB")
            load_row(0, "L0_ffn_core_b", half * 1024, (half + 1) * 1024, rowA, "rowA")
            vop(lambda e, half=half: e.tensor_add(
                    rowsb[0:1, half * 1024:(half + 1) * 1024],
                    rowB[0:1, 0:1024], rowA[0:1, 0:1024]),
                [R("rowB", 0, 1024), R("rowA", 0, 1024)],
                [R("rowsb", half * 1024, (half + 1) * 1024)])
        load_row(0, "L0_attn_dt_bias", 0, NVH, rowB, "rowB")
        load_row(32, "L0_attn_A_log", 0, NVH, rowB, "rowB")
        load_row(64, "L0_attn_norm_w", 0, DV, rowB, "rowB")
        CI["normw"] = ccol(rowB, "rowB", 64, 128)
        CI["dtbB"], CI["neB"] = 320, 352
        top(lambda e: e.matmul(ps[5][:, 0:32], onesf[0:1, 0:128], rowB[0:1, 0:32],
                               start=True, stop=True),
            [R("rowB", 0, 32), R("onesf", 0, 128)], [R("ps5", 0, 512)])
        vop(lambda e: e.tensor_copy(cols[:, 320:352], ps[5][:, 0:32]),
            [R("ps5", 0, 512)], [R("cols", 320, 352)])
        sop(lambda e: e.activation(rowB[0:1, 64:96], rowB[0:1, 32:64], AF.Exp),
            [R("rowB", 32, 64)], [R("rowB", 64, 96)])
        vop(lambda e: e.tensor_scalar_mul(rowB[0:1, 96:128], rowB[0:1, 64:96], -1.0),
            [R("rowB", 64, 96)], [R("rowB", 96, 128)])
        top(lambda e: e.matmul(ps[5][:, 0:32], onesf[0:1, 0:128], rowB[0:1, 96:128],
                               start=True, stop=True),
            [R("rowB", 96, 128), R("onesf", 0, 128)], [R("ps5", 0, 512)])
        vop(lambda e: e.tensor_copy(cols[:, 352:384], ps[5][:, 0:32]),
            [R("ps5", 0, 512)], [R("cols", 352, 384)])
        CI["b1"] = {}
        ptr = 0
        for ee in range(9):
            h = HID[ee] if ee < 8 else L
            nm = f"L0_ffn_e{ee}_b1" if ee < 8 else "L0_ffn_shared_b1"
            CI["b1"][ee] = ptr
            for half in range(ceil_div(h, 1024)):
                lo = half * 1024
                w = min(1024, h - lo)
                load_row(0, nm, lo, lo + w, rowB, "rowB")
                for j in range(w // 128):
                    colify(rowB, "rowB", j * 128, 128, gb1c_raw, "gb1c_raw", ptr)
                    ptr += 1
        CI["sb1"] = CI["b1"][8]
        sop(lambda e, ptr=ptr: e.activation(gb1c[:, 0:ptr], gb1c_raw[:, 0:ptr], AF.Gelu),
            [R("gb1c_raw", 0, ptr)], [R("gb1c", 0, ptr)])
        vop(lambda e, ptr=ptr: e.tensor_copy(gb1cb[:, 0:ptr], gb1c[:, 0:ptr]),
            [R("gb1c", 0, ptr)], [R("gb1cb", 0, ptr)])
        CI["conv"] = {}
        for piece in range(32):
            load_row(0, "L0_attn_conv_w", piece * 1024, (piece + 1) * 1024, rowB, "rowB")
            for tt in range(2):
                for j in range(4):
                    tile = piece * 2 + tt
                    CI["conv"][(tile, j)] = _colptr[0]
                    ccol(rowB, "rowB", tt * 512 + j, 128, stride=4)
        load_row(0, "L0_ffn_router_b", 0, 8, rowB, "rowB")
        for ee in range(8):
            cst = COST_LAMBDA * (L * HID[ee] + HID[ee] * L)
            vop(lambda e, ee=ee, cst=cst: e.memset(rowB[0:1, 8 + ee:9 + ee], float(cst)),
                [], [R("rowB", 8, 16)])
        vop(lambda e: e.tensor_sub(rowB[0:1, 16:24], rowB[0:1, 0:8], rowB[0:1, 8:16]),
            [R("rowB", 0, 16)], [R("rowB", 16, 24)])
        vop(lambda e: e.tensor_copy(rowsb[0:1, 3072:3080], rowB[0:1, 16:24]),
            [R("rowB", 16, 24)], [R("rowsb", 3072, 3080)])

    CASTOFF = 24576

    def cast_weight(name, post=None, scale_col=None, scale_imm=None):
        K, N = WMATS[name]
        PW = panelw(name)
        src, dst = wmat[name], wscr[name]
        sname = "scr_" + name
        pieces = ceil_div(N, 1024)
        for ks in range(K // 128):
            for pi in range(pieces):
                p0 = pi * 1024
                w = min(1024, N - p0)
                slot = (ks * pieces + pi) % 2
                cb = bigA[:, CASTOFF + slot * 1024: CASTOFF + slot * 1024 + w]
                ckey = (A, CASTOFF + slot * 1024, CASTOFF + slot * 1024 + w)
                dma("g", cb, src[ks * 128:(ks + 1) * 128, p0:p0 + w], [], [ckey])
                if scale_col is not None:
                    vop(lambda e, cb=cb, sc=scale_col: e.tensor_scalar_mul(
                            cb, cb, cols[:, sc:sc + 1]),
                        [ckey, R("cols", scale_col, scale_col + 1)], [ckey])
                if scale_imm is not None:
                    vop(lambda e, cb=cb, si=scale_imm: e.tensor_scalar_mul(cb, cb, si),
                        [ckey], [ckey])
                if post is not None:
                    post(cb, ckey, ks, p0, w)
                nk = K // 128
                for pp in range(p0 // PW, ceil_div(p0 + w, PW)):
                    pw = min(PW, N - pp * PW)
                    dpp = qkvz_perm(pp) if name == "L0_attn_W_qkvz" else pp
                    base = dpp * PW * K + ks * pw
                    dma("y", bass.AP(dst.tensor, base, [[nk * pw, 128], [1, pw]]),
                        cb[:, pp * PW - p0: pp * PW - p0 + pw],
                        [ckey], [(sname, dpp * PW * K, (dpp + 1) * PW * K)])

    def load_panel(name, pp, dest_t, dest_off, dkey, eng="s"):
        K, N = WMATS[name]
        PW = panelw(name)
        pw = min(PW, N - pp * PW)
        base = pp * PW * K
        nk = K // 128
        dma(eng, dest_t[:, dest_off: dest_off + nk * pw],
            wscr[name][base: base + K * pw].rearrange("(p kj) -> p kj", p=128),
            [("scr_" + name, base, base + K * pw)], [dkey])
        return pw

    def make_w2_post(ee):
        h = HID[ee]
        base = CI["b1"][ee]
        nk = h // 128

        def post(cb, ckey, ks, p0, w):
            for half in range(2):
                bank = 3 + half
                top(lambda e, cb=cb, half=half, bank=bank, ks=ks: e.matmul(
                        ps[bank][0:1, 0:512], gb1cb[:, base + ks: base + ks + 1],
                        cb[:, half * 512: half * 512 + 512],
                        start=(ks == 0), stop=(ks == nk - 1)),
                    [ckey, R("gb1cb", base + ks, base + ks + 1)],
                    [(f"ps{bank}", 0, 512)] if ks == 0 else [],
                    group_n=(2 * nk - 1) if ks == 0 else 1)
        return post

    def setup_cast():
        cast_weight("L0_ffn_down_w")
        cast_weight("L0_ffn_router_w")
        for ee in range(8):
            cast_weight(f"L0_ffn_e{ee}_w1")
            load_row(0, f"L0_ffn_e{ee}_b2", 0, L, rowB, "rowB")
            cast_weight(f"L0_ffn_e{ee}_w2", post=make_w2_post(ee))
            for half in range(2):
                bank = 3 + half
                lo = half * 512
                co = 1024 + half * 512
                if ee == 0:
                    vop(lambda e, bank=bank, co=co, lo=lo: e.tensor_add(
                            rowA[0:1, co:co + 512], rowB[0:1, lo:lo + 512],
                            ps[bank][0:1, 0:512]),
                        [(f"ps{bank}", 0, 512), R("rowB", lo, lo + 512)],
                        [R("rowA", co, co + 512)])
                else:
                    vop(lambda e, bank=bank, co=co: e.tensor_add(
                            rowA[0:1, co:co + 512], rowA[0:1, co:co + 512],
                            ps[bank][0:1, 0:512]),
                        [(f"ps{bank}", 0, 512), R("rowA", co, co + 512)],
                        [R("rowA", co, co + 512)])
                    vop(lambda e, co=co, lo=lo: e.tensor_add(
                            rowA[0:1, co:co + 512], rowA[0:1, co:co + 512],
                            rowB[0:1, lo:lo + 512]),
                        [R("rowB", lo, lo + 512), R("rowA", co, co + 512)],
                        [R("rowA", co, co + 512)])
                vop(lambda e, bank=bank, lo=lo: e.tensor_scalar_mul(
                        rowB[0:1, lo:lo + 512], ps[bank][0:1, 0:512], -1.0),
                    [(f"ps{bank}", 0, 512)],
                    [R("rowB", lo, lo + 512)])
            dma("y", rowscr[ee * 1024:(ee + 1) * 1024], rowB[0:1, 0:1024],
                [R("rowB", 0, 1024)], [("rowscr", ee * 1024, (ee + 1) * 1024)])
        dma("g", D8pm[:, :], rowscr[0:8192].rearrange("(p f) -> p f", p=8),
            [("rowscr", 0, 8192)], [R("D8pm", 0, 1024)])
        load_row(0, "L0_ffn_shared_b2", 0, L, rowB, "rowB")
        vop(lambda e: e.scalar_tensor_tensor(rowA[0:1, 1024:2048], rowB[0:1, 0:1024],
                                             0.1, rowA[0:1, 1024:2048], ALU.mult, ALU.add),
            [R("rowB", 0, 1024), R("rowA", 1024, 2048)], [R("rowA", 1024, 2048)])
        vop(lambda e: e.tensor_copy(rowsb[0:1, 2048:3072], rowA[0:1, 1024:2048]),
            [R("rowA", 1024, 2048)], [R("rowsb", 2048, 3072)])
        cast_weight("L0_ffn_shared_w1")
        cast_weight("L0_ffn_shared_w2", scale_imm=0.1)
        cast_weight("L0_ffn_up_w")
        cast_weight("L0_ffn_core_w")
        cast_weight("L0_attn_W_qkvz")
        cast_weight("L0_attn_W_ba")
        cast_weight("L0_attn_W_out", scale_col=CI["normw"])

    def layernorm(src_hbm, src_key, t0, dst_t, dst_off, dst_key):
        for k in range(16):
            dma("g", bigD[:, k * 512:(k + 1) * 512],
                src_hbm[k * 128:(k + 1) * 128, t0:t0 + 512],
                [xiv(src_key, k, t0)],
                [(Dk, k * 512, (k + 1) * 512)])
        for k in range(16):
            top(lambda e, k=k: e.matmul(ps[4][0:1, 0:512], onescb[:, 0:1],
                                        bigD[:, k * 512:(k + 1) * 512],
                                        start=(k == 0), stop=(k == 15)),
                [(Dk, k * 512, (k + 1) * 512), R("onescb", 0, 1)],
                [("ps4", 0, 512)] if k == 0 else [], group_n=16 if k == 0 else 1)
        for k in range(16):
            vop(lambda e, k=k: e.tensor_mul(wbt[:, 0:512], bigD[:, k * 512:(k + 1) * 512],
                                            bigD[:, k * 512:(k + 1) * 512]),
                [(Dk, k * 512, (k + 1) * 512)], [R("wbt", 0, 512)])
            top(lambda e, k=k: e.matmul(ps[5][0:1, 0:512], onescb[:, 0:1], wbt[:, 0:512],
                                        start=(k == 0), stop=(k == 15)),
                [R("wbt", 0, 512), R("onescb", 0, 1)],
                [("ps5", 0, 512)] if k == 0 else [], group_n=16 if k == 0 else 1)
        vop(lambda e: e.tensor_scalar_mul(stat[0:1, 0:512], ps[4][0:1, 0:512], 1.0 / D),
            [("ps4", 0, 512)], [R("rowA", 0, 512)])
        vop(lambda e: e.tensor_scalar_mul(stat[0:1, 512:1024], ps[5][0:1, 0:512], 1.0 / D),
            [("ps5", 0, 512)], [R("rowA", 512, 1024)])
        vop(lambda e: e.tensor_mul(stat[0:1, 1024:1536], stat[0:1, 0:512], stat[0:1, 0:512]),
            [R("rowA", 0, 512)], [R("rowA", 1024, 1536)])
        vop(lambda e: e.tensor_sub(stat[0:1, 512:1024], stat[0:1, 512:1024],
                                   stat[0:1, 1024:1536]),
            [R("rowA", 512, 1024), R("rowA", 1024, 1536)], [R("rowA", 512, 1024)])
        sop(lambda e: e.activation(stat[0:1, 512:1024], stat[0:1, 512:1024], AF.Ln,
                                   bias=cols[0:1, CI["lneps"]:CI["lneps"] + 1]),
            [R("rowA", 512, 1024), R("cols", CI["lneps"], CI["lneps"] + 1)],
            [R("rowA", 512, 1024)])
        sop(lambda e: e.activation(stat[0:1, 512:1024], stat[0:1, 512:1024], AF.Exp,
                                   scale=-0.5),
            [R("rowA", 512, 1024)], [R("rowA", 512, 1024)])
        vop(lambda e: e.tensor_mul(stat[0:1, 0:512], stat[0:1, 0:512], stat[0:1, 512:1024]),
            [R("rowA", 0, 512), R("rowA", 512, 1024)], [R("rowA", 0, 512)])
        top(lambda e: e.matmul(ps[4][:, 0:512], onesf[0:1, 0:128], stat[0:1, 512:1024],
                               start=True, stop=True),
            [R("rowA", 512, 1024), R("onesf", 0, 128)], [("ps4", 0, 512)])
        top(lambda e: e.matmul(ps[5][:, 0:512], onesf[0:1, 0:128], stat[0:1, 0:512],
                               start=True, stop=True),
            [R("rowA", 0, 512), R("onesf", 0, 128)], [("ps5", 0, 512)])
        for k in range(16):
            vop(lambda e, k=k: e.tensor_mul(bigE[:, 0:512], bigD[:, k * 512:(k + 1) * 512],
                                            ps[4][:, 0:512]),
                [(Dk, k * 512, (k + 1) * 512), ("ps4", 0, 512)], [(E, 0, 512)])
            vop(lambda e, k=k: e.tensor_sub(dst_t[:, dst_off + k * 512: dst_off + (k + 1) * 512],
                                            bigE[:, 0:512], ps[5][:, 0:512]),
                [(E, 0, 512), ("ps5", 0, 512)],
                [(dst_key, dst_off + k * 512, dst_off + (k + 1) * 512)])

    def ffn(src_hbm, src_key, dst_hbm, dst_key):
        TOFF, GCT, OM = 16384, 26624, 34816
        for blk in range(NTOK // FB):
            t0 = blk * FB
            layernorm(src_hbm, src_key, t0, bigA, TOFF, A)
            for k in range(16):
                sop(lambda e, k=k: e.activation(bigA[:, GCT + k * 512: GCT + (k + 1) * 512],
                                                bigA[:, TOFF + k * 512: TOFF + (k + 1) * 512],
                                                AF.Gelu),
                    [(A, TOFF + k * 512, TOFF + (k + 1) * 512)],
                    [(A, GCT + k * 512, GCT + (k + 1) * 512)])
            for pp in range(4):
                slot = pp % 2
                dkey = (Bk, slot * 4096, slot * 4096 + 4096)
                load_panel("L0_ffn_down_w", pp, bigB, slot * 4096, dkey)
                for half in range(2):
                    j = pp * 2 + half
                    bank = j % 2
                    for k in range(16):
                        top(lambda e, slot=slot, half=half, k=k, bank=bank: e.matmul(
                                ps[bank][:, 0:512],
                                bigB[:, slot * 4096 + k * 256 + half * 128:
                                     slot * 4096 + k * 256 + half * 128 + 128],
                                bigA[:, TOFF + k * 512: TOFF + (k + 1) * 512],
                                start=(k == 0), stop=(k == 15)),
                            [dkey, (A, TOFF + k * 512, TOFF + (k + 1) * 512)],
                            [(f"ps{bank}", 0, 512)] if k == 0 else [],
                            group_n=16 if k == 0 else 1)
                    bcol = CI["down_b"] + j
                    sop(lambda e, j=j, bank=bank, bcol=bcol: e.activation(
                            bigC[:, j * 512:(j + 1) * 512], ps[bank][:, 0:512], AF.Gelu,
                            bias=cols[:, bcol:bcol + 1]),
                        [(f"ps{bank}", 0, 512), R("cols", bcol, bcol + 1)],
                        [(C, j * 512, (j + 1) * 512)])
            rkey = (Bk, 0, 128)
            load_panel("L0_ffn_router_w", 0, bigB, 0, rkey)
            for sub in range(4):
                for k in range(16):
                    top(lambda e, sub=sub, k=k: e.matmul(
                            ps[2][:, 0:8],
                            bigA[:, TOFF + k * 512 + sub * 128: TOFF + k * 512 + sub * 128 + 128],
                            bigB[:, k * 8: k * 8 + 8],
                            start=(k == 0), stop=False),
                        [rkey, (A, TOFF + k * 512 + sub * 128,
                                TOFF + k * 512 + sub * 128 + 128)],
                        [("ps2", 0, 512)] if k == 0 else [],
                        group_n=17 if k == 0 else 1)
                top(lambda e: e.matmul(ps[2][:, 0:8], onesb[0:1, 0:128],
                                       rowsb[0:1, 3072:3080], start=False, stop=True),
                    [R("rowsb", 3072, 3080), R("onesb", 0, 128)], [])
                vop(lambda e: e.reduce_max(prob[:, 8:9], ps[2][:, 0:8], axis=AX, negate=True),
                    [("ps2", 0, 512)], [R("prob", 8, 9)])
                sop(lambda e: e.activation(prob[:, 0:8], ps[2][:, 0:8], AF.Exp,
                                           bias=prob[:, 8:9], accum_out=prob[:, 9:10]),
                    [("ps2", 0, 512), R("prob", 8, 9)],
                    [R("prob", 0, 8), R("prob", 9, 10)])
                vop(lambda e: e.reciprocal(prob[:, 10:11], prob[:, 9:10]),
                    [R("prob", 9, 10)], [R("prob", 10, 11)])
                vop(lambda e: e.tensor_scalar_mul(prob[:, 0:8], prob[:, 0:8], prob[:, 10:11]),
                    [R("prob", 0, 8), R("prob", 10, 11)], [R("prob", 0, 8)])
                vop(lambda e: e.max(prob[:, 16:24], prob[:, 0:8]),
                    [R("prob", 0, 8)], [R("prob", 16, 24)])
                vop(lambda e: e.memset(prob[:, 24:32], NEG_BIG), [], [R("prob", 24, 32)])
                vop(lambda e: e.tensor_copy(prob[:, 24:25], prob[:, 16:17]),
                    [R("prob", 16, 24)], [R("prob", 24, 32)])
                vop(lambda e: e.match_replace(prob[:, 32:40], prob[:, 24:32], prob[:, 0:8],
                                              NEG_BIG),
                    [R("prob", 24, 32), R("prob", 0, 8)], [R("prob", 32, 40)])
                vop(lambda e: e.tensor_tensor(prob[:, 40:48], prob[:, 0:8], prob[:, 32:40],
                                              ALU.not_equal),
                    [R("prob", 0, 8), R("prob", 32, 40)], [R("prob", 40, 48)])
                vop(lambda e: e.tensor_copy(prob[:, 24:25], prob[:, 17:18]),
                    [R("prob", 16, 24)], [R("prob", 24, 32)])
                vop(lambda e: e.match_replace(prob[:, 48:56], prob[:, 24:32], prob[:, 32:40],
                                              NEG_BIG),
                    [R("prob", 24, 32), R("prob", 32, 40)], [R("prob", 48, 56)])
                vop(lambda e: e.tensor_tensor(prob[:, 56:64], prob[:, 32:40], prob[:, 48:56],
                                              ALU.not_equal),
                    [R("prob", 32, 40), R("prob", 48, 56)], [R("prob", 56, 64)])
                vop(lambda e: e.tensor_sub(prob[:, 11:12], prob[:, 17:18], prob[:, 16:17]),
                    [R("prob", 16, 24)], [R("prob", 11, 12)])
                sop(lambda e: e.activation(prob[:, 12:13], prob[:, 11:12], AF.Exp),
                    [R("prob", 11, 12)], [R("prob", 12, 13)])
                vop(lambda e: e.tensor_scalar_add(prob[:, 13:14], prob[:, 12:13], 1.0),
                    [R("prob", 12, 13)], [R("prob", 13, 14)])
                vop(lambda e: e.reciprocal(prob[:, 13:14], prob[:, 13:14]),
                    [R("prob", 13, 14)], [R("prob", 13, 14)])
                vop(lambda e: e.tensor_mul(prob[:, 14:15], prob[:, 12:13], prob[:, 13:14]),
                    [R("prob", 12, 13), R("prob", 13, 14)], [R("prob", 14, 15)])
                vop(lambda e: e.tensor_scalar_mul(prob[:, 40:48], prob[:, 40:48],
                                                  prob[:, 13:14]),
                    [R("prob", 40, 48), R("prob", 13, 14)], [R("prob", 40, 48)])
                vop(lambda e, sub=sub: e.scalar_tensor_tensor(
                        w8f[:, sub * 8:(sub + 1) * 8], prob[:, 56:64], prob[:, 14:15],
                        prob[:, 40:48], ALU.mult, ALU.add),
                    [R("prob", 56, 64), R("prob", 40, 48), R("prob", 14, 15)],
                    [R("w8f", sub * 8, (sub + 1) * 8)])
                top(lambda e, sub=sub: e.transpose(ps[3][0:8, sub * 128:(sub + 1) * 128],
                                                   w8f[:, sub * 8:(sub + 1) * 8],
                                                   identf[:, :]),
                    [R("w8f", sub * 8, (sub + 1) * 8), R("identf", 0, 128)],
                    [("ps3", 0, 512)])
            vop(lambda e: e.tensor_copy(w8tb[0:8, 0:512], ps[3][0:8, 0:512]),
                [("ps3", 0, 512)], [R("w8tb", 0, 512)])
            for ee in range(9):
                h = HID[ee] if ee < 8 else L
                nht = h // 128
                w1n = f"L0_ffn_e{ee}_w1" if ee < 8 else "L0_ffn_shared_w1"
                w2n = f"L0_ffn_e{ee}_w2" if ee < 8 else "L0_ffn_shared_w2"
                if ee < 8:
                    top(lambda e, ee=ee: e.matmul(ps[2][:, 0:512],
                                                  selt[0:8, ee * 128:(ee + 1) * 128],
                                                  w8tb[0:8, 0:512], start=True, stop=True),
                        [R("selt", ee * 128, (ee + 1) * 128), R("w8tb", 0, 512)],
                        [("ps2", 0, 512)])
                    vop(lambda e: e.tensor_copy(wbt[:, 0:512], ps[2][:, 0:512]),
                        [("ps2", 0, 512)], [R("wbt", 0, 512)])
                for pp in range(h // PANEL):
                    slot = pp % 2
                    dkey = (Bk, slot * 4096, slot * 4096 + 2048)
                    load_panel(w1n, pp, bigB, slot * 4096, dkey)
                    for half in range(2):
                        hj = pp * 2 + half
                        bank = hj % 2
                        for k in range(8):
                            top(lambda e, slot=slot, half=half, k=k, bank=bank: e.matmul(
                                    ps[bank][:, 0:512],
                                    bigB[:, slot * 4096 + k * 256 + half * 128:
                                         slot * 4096 + k * 256 + half * 128 + 128],
                                    bigC[:, k * 512:(k + 1) * 512],
                                    start=(k == 0), stop=(k == 7)),
                                [dkey, (C, k * 512, (k + 1) * 512)],
                                [(f"ps{bank}", 0, 512)] if k == 0 else [],
                                group_n=8 if k == 0 else 1)
                        gsl = (A, hj * 512, (hj + 1) * 512)
                        bcol = CI["b1"][ee] + hj
                        sop(lambda e, hj=hj, bank=bank, bcol=bcol: e.activation(
                                bigA[:, hj * 512:(hj + 1) * 512], ps[bank][:, 0:512],
                                AF.Gelu, bias=gb1c_raw[:, bcol:bcol + 1]),
                            [(f"ps{bank}", 0, 512), R("gb1c_raw", bcol, bcol + 1)], [gsl])
                        if ee < 8:
                            vop(lambda e, hj=hj: e.tensor_mul(
                                    bigA[:, hj * 512:(hj + 1) * 512],
                                    bigA[:, hj * 512:(hj + 1) * 512], wbt[:, 0:512]),
                                [gsl, R("wbt", 0, 512)], [gsl])
                w2pw = panelw(w2n)
                for pp in range(L // w2pw):
                    slot = pp % 2
                    pkey = (Bk, slot * 4096, slot * 4096 + nht * w2pw)
                    load_panel(w2n, pp, bigB, slot * 4096, pkey)
                    for half in range(w2pw // 128):
                        j = pp * (w2pw // 128) + half
                        bank = 2 + (j % 2)
                        for k in range(nht):
                            top(lambda e, slot=slot, half=half, k=k, bank=bank,
                                       w2pw=w2pw, nht=nht: e.matmul(
                                    ps[bank][:, 0:512],
                                    bigB[:, slot * 4096 + k * w2pw + half * 128:
                                         slot * 4096 + k * w2pw + half * 128 + 128],
                                    bigA[:, k * 512:(k + 1) * 512],
                                    start=(k == 0), stop=(k == nht - 1)),
                                [pkey, (A, k * 512, (k + 1) * 512)],
                                [(f"ps{bank}", 0, 512)] if k == 0 else [],
                                group_n=nht if k == 0 else 1)
                        if ee == 0:
                            vop(lambda e, j=j, bank=bank: e.tensor_copy(
                                    bigE[:, j * 512:(j + 1) * 512], ps[bank][:, 0:512]),
                                [(f"ps{bank}", 0, 512)], [(E, j * 512, (j + 1) * 512)])
                        else:
                            vop(lambda e, j=j, bank=bank: e.tensor_add(
                                    bigE[:, j * 512:(j + 1) * 512],
                                    bigE[:, j * 512:(j + 1) * 512], ps[bank][:, 0:512]),
                                [(f"ps{bank}", 0, 512), (E, j * 512, (j + 1) * 512)],
                                [(E, j * 512, (j + 1) * 512)])
            for j in range(8):
                top(lambda e, j=j: e.matmul(ps[2][:, 0:512], D8pm[0:8, j * 128:(j + 1) * 128],
                                            w8tb[0:8, 0:512], start=True, stop=False),
                    [R("D8pm", j * 128, (j + 1) * 128), R("w8tb", 0, 512)],
                    [("ps2", 0, 512)], group_n=2)
                top(lambda e, j=j: e.matmul(ps[2][:, 0:512],
                                            rowsb[0:1, 2048 + j * 128: 2048 + (j + 1) * 128],
                                            onesb[0:1, 0:512], start=False, stop=True),
                    [R("rowsb", 2048 + j * 128, 2048 + (j + 1) * 128), R("onesb", 0, 512)],
                    [])
                vop(lambda e, j=j: e.tensor_add(bigA[:, OM + j * 512: OM + (j + 1) * 512],
                                                bigE[:, j * 512:(j + 1) * 512],
                                                ps[2][:, 0:512]),
                    [(E, j * 512, (j + 1) * 512), ("ps2", 0, 512)],
                    [(A, OM + j * 512, OM + (j + 1) * 512)])
            for grp in range(8):
                ukey = (Bk, 0, 2048)
                ckey = (Bk, 2048, 6144)
                load_panel("L0_ffn_up_w", grp, bigB, 0, ukey)
                load_panel("L0_ffn_core_w", grp, bigB, 2048, ckey)
                for half in range(2):
                    j = grp * 2 + half
                    bank = j % 2
                    for k in range(8):
                        top(lambda e, half=half, k=k, bank=bank: e.matmul(
                                ps[bank][:, 0:512],
                                bigB[:, k * 256 + half * 128: k * 256 + half * 128 + 128],
                                bigA[:, OM + k * 512: OM + (k + 1) * 512],
                                start=(k == 0), stop=False),
                            [ukey, (A, OM + k * 512, OM + (k + 1) * 512)],
                            [(f"ps{bank}", 0, 512)] if k == 0 else [],
                            group_n=25 if k == 0 else 1)
                    for k in range(16):
                        top(lambda e, half=half, k=k, bank=bank: e.matmul(
                                ps[bank][:, 0:512],
                                bigB[:, 2048 + k * 256 + half * 128:
                                     2048 + k * 256 + half * 128 + 128],
                                bigA[:, GCT + k * 512: GCT + (k + 1) * 512],
                                start=False, stop=False),
                            [ckey, (A, GCT + k * 512, GCT + (k + 1) * 512)], [])
                    top(lambda e, j=j, bank=bank: e.matmul(
                            ps[bank][:, 0:512], rowsb[0:1, j * 128:(j + 1) * 128],
                            onesb[0:1, 0:512], start=False, stop=True),
                        [R("rowsb", j * 128, (j + 1) * 128), R("onesb", 0, 512)], [])
                    xk = ("xfst", 0, 512)
                    dma("y", xfst[:, 0:512],
                        src_hbm[j * 128:(j + 1) * 128, t0:t0 + 512],
                        [xiv(src_key, j, t0)], [xk])
                    ykey = ("xfst", 512, 1024)
                    vop(lambda e, bank=bank: e.scalar_tensor_tensor(
                            xfst[:, 512:1024], ps[bank][:, 0:512], 0.5,
                            xfst[:, 0:512], ALU.mult, ALU.add),
                        [(f"ps{bank}", 0, 512), xk], [ykey])
                    dma("y", dst_hbm[j * 128:(j + 1) * 128, t0:t0 + 512],
                        xfst[:, 512:1024], [ykey], [xiv(dst_key, j, t0)])


    def attn(src_hbm, src_key, dst_hbm, dst_key):
        fslot_ctr = [0]

        def fslot():
            i = fslot_ctr[0] % 20
            fslot_ctr[0] += 1
            lo = 1536 + i * 128
            return bigE[:, lo:lo + 128], (E, lo, lo + 128)

        bslot_ctr = [0]

        def bslot():
            i = bslot_ctr[0] % 16
            bslot_ctr[0] += 1
            lo = 6144 + i * 128
            return bigD[:, lo:lo + 128], (Dk, lo, lo + 128)

        pslot_ctr = [0]

        def pslot():
            i = pslot_ctr[0] % 16
            pslot_ctr[0] += 1
            t = pbf[i // 8]
            lo = (i % 8) * 128
            return t[:, lo:lo + 128], (PB[i // 8], 0, 1024)

        def qtile(kh, c):
            lo = kh * 512 + c * 128
            return bigA[:, lo:lo + 128], (A, lo, lo + 128)

        def ktile(kh, c):
            lo = (16 + kh) * 512 + c * 128
            return bigA[:, lo:lo + 128], (A, lo, lo + 128)

        def vtile(h, c):
            lo = (32 + h) * 512 + c * 128
            return bigA[:, lo:lo + 128], (A, lo, lo + 128)

        for b in range(NBATCH):
            vop(lambda e: e.memset(S_sb[:, :], 0.0), [], [R("S_sb", 0, 4096)])
            vop(lambda e: e.memset(hist[:, :], 0.0), [], [R("hist", 0, 192)])
            for sc in range(SEQ // SC):
                t0 = b * SEQ + sc * SC
                layernorm(src_hbm, src_key, t0, bigC, 0, C)
                for pp in range(96):
                    slot = pp % 2
                    dkey = (Bk, slot * 4096, slot * 4096 + 2048)
                    load_panel("L0_attn_W_qkvz", pp, bigB, slot * 4096, dkey)
                    if True:
                        n = pp
                        bank = n % 2
                        for k in range(16):
                            top(lambda e, slot=slot, k=k, bank=bank: e.matmul(
                                    ps[bank][:, 0:512],
                                    bigB[:, slot * 4096 + k * 128:
                                         slot * 4096 + k * 128 + 128],
                                    bigC[:, k * 512:(k + 1) * 512],
                                    start=(k == 0), stop=(k == 15)),
                                [dkey, (C, k * 512, (k + 1) * 512)],
                                [(f"ps{bank}", 0, 512)] if k == 0 else [],
                                group_n=16 if k == 0 else 1)
                        if n < 64:
                            vop(lambda e, n=n: e.tensor_copy(bigE[:, 0:3],
                                                             hist[:, n * 3:(n + 1) * 3]),
                                [R("hist", n * 3, (n + 1) * 3)], [(E, 0, 3)])
                            vop(lambda e, bank=bank: e.tensor_copy(bigE[:, 3:515],
                                                                   ps[bank][:, 0:512]),
                                [(f"ps{bank}", 0, 512)], [(E, 3, 515)])
                            vop(lambda e, n=n: e.tensor_copy(hist[:, n * 3:(n + 1) * 3],
                                                             bigE[:, 512:515]),
                                [(E, 512, 515)], [R("hist", n * 3, (n + 1) * 3)])
                            c0 = CI["conv"][(n, 0)]
                            vop(lambda e, c0=c0: e.tensor_scalar_mul(
                                    bigE[:, 1536:2048], bigE[:, 0:512],
                                    cols[:, c0:c0 + 1]),
                                [(E, 0, 515), R("cols", c0, c0 + 1)], [(E, 1536, 2048)])
                            for j in range(1, 4):
                                cj = CI["conv"][(n, j)]
                                vop(lambda e, j=j, cj=cj: e.scalar_tensor_tensor(
                                        bigE[:, 1536:2048], bigE[:, j:j + 512],
                                        cols[:, cj:cj + 1], bigE[:, 1536:2048],
                                        ALU.mult, ALU.add),
                                    [(E, 0, 515), R("cols", cj, cj + 1), (E, 1536, 2048)],
                                    [(E, 1536, 2048)])
                            sop(lambda e, n=n: e.activation(
                                    bigA[:, n * 512:(n + 1) * 512], bigE[:, 1536:2048],
                                    AF.Silu),
                                [(E, 1536, 2048)], [(A, n * 512, (n + 1) * 512)])
                        else:
                            zt = n - 64
                            sop(lambda e, bank=bank: e.activation(
                                    wbt[:, 0:512], ps[bank][:, 0:512], AF.Silu),
                                [(f"ps{bank}", 0, 512)], [R("wbt", 0, 512)])
                            dma("y", zscr[zt * 65536:(zt + 1) * 65536]
                                .rearrange("(p j) -> p j", p=128), wbt[:, 0:512],
                                [R("wbt", 0, 512)], [("zscr", zt * 65536, (zt + 1) * 65536)])
                bakey = (Bk, 0, 1024)
                load_panel("L0_attn_W_ba", 0, bigB, 0, bakey)
                for k in range(16):
                    top(lambda e, k=k: e.matmul(
                            ps[2][0:64, 0:512], bigB[:, k * 64: k * 64 + 64],
                            bigC[:, k * 512:(k + 1) * 512],
                            start=(k == 0), stop=(k == 15)),
                        [bakey, (C, k * 512, (k + 1) * 512)],
                        [("ps2", 0, 512)] if k == 0 else [], group_n=16 if k == 0 else 1)
                vop(lambda e: e.tensor_copy(bigE[0:64, 0:512], ps[2][0:64, 0:512]),
                    [("ps2", 0, 512)], [(E, 0, 512)])
                for c in range(NCH):
                    tb = c * 256
                    top(lambda e, c=c: e.transpose(ps[3][:, 0:64],
                                                   bigE[0:64, c * 128:(c + 1) * 128],
                                                   identf[0:64, 0:64]),
                        [(E, 0, 512)], [("ps3", 0, 512)])
                    for par in range(2):
                        vop(lambda e, par=par: e.tensor_copy(
                                bass.AP(bigE, 1408 + par, [[4096, 128], [2, 16]]),
                                bass.AP(ps[3], par, [[512, 128], [4, 16]])),
                            [("ps3", 0, 512)], [(E, 1408, 1440)])
                        vop(lambda e, par=par: e.tensor_copy(
                                bass.AP(bigE, 1440 + par, [[4096, 128], [2, 16]]),
                                bass.AP(ps[3], 2 + par, [[512, 128], [4, 16]])),
                            [("ps3", 0, 512)], [(E, 1440, 1472)])
                    sop(lambda e, tb=tb: e.activation(tokc[:, tb:tb + 32],
                                                      bigE[:, 1408:1440], AF.Sigmoid),
                        [(E, 1408, 1440)], [R("tokc", tb, tb + 32)])
                    vop(lambda e, tb=tb: e.tensor_add(bigE[:, 1536:1568],
                                                      bigE[:, 1440:1472], cols[:, 320:352]),
                        [(E, 1440, 1472), R("cols", 320, 352)], [(E, 1536, 1568)])
                    sop(lambda e: e.activation(bigE[:, 1568:1600], bigE[:, 1536:1568],
                                               AF.Abs),
                        [(E, 1536, 1568)], [(E, 1568, 1600)])
                    sop(lambda e: e.activation(bigE[:, 1600:1632], bigE[:, 1568:1600],
                                               AF.Exp, scale=-1.0),
                        [(E, 1568, 1600)], [(E, 1600, 1632)])
                    sop(lambda e: e.activation(bigE[:, 1600:1632], bigE[:, 1600:1632],
                                               AF.Ln, bias=cols[:, CI["one"]:CI["one"] + 1]),
                        [(E, 1600, 1632), R("cols", CI["one"], CI["one"] + 1)],
                        [(E, 1600, 1632)])
                    vop(lambda e: e.tensor_scalar_max(bigE[:, 1536:1568],
                                                      bigE[:, 1536:1568], 0.0),
                        [(E, 1536, 1568)], [(E, 1536, 1568)])
                    vop(lambda e: e.tensor_add(bigE[:, 1536:1568], bigE[:, 1536:1568],
                                               bigE[:, 1600:1632]),
                        [(E, 1536, 1632)], [(E, 1536, 1568)])
                    vop(lambda e: e.tensor_mul(bigE[:, 1568:1600], bigE[:, 1536:1568],
                                               cols[:, 352:384]),
                        [(E, 1536, 1568), R("cols", 352, 384)], [(E, 1568, 1600)])
                    top(lambda e: e.transpose(ps[4][0:32, 0:128], bigE[:, 1568:1600],
                                              identf[:, :]),
                        [(E, 1568, 1600)], [("ps4", 0, 512)])
                    vop(lambda e: e.tensor_copy(bg2[0:32, 0:128], ps[4][0:32, 0:128]),
                        [("ps4", 0, 512)], [R("bg2", 0, 128)])
                    vop(lambda e: e.tensor_tensor_scan(bg2[0:32, 128:256],
                                                       bg2[0:32, 0:128],
                                                       bg2[0:32, 0:128], 0.0,
                                                       ALU.add, ALU.bypass),
                        [R("bg2", 0, 128)], [R("bg2", 128, 256)])
                    top(lambda e: e.transpose(ps[4][:, 128:160], bg2[0:32, 128:256],
                                              identf[0:32, 0:32]),
                        [R("bg2", 128, 256)], [("ps4", 0, 512)])
                    vop(lambda e, tb=tb: e.tensor_copy(tokc[:, tb + 32:tb + 64],
                                                       ps[4][:, 128:160]),
                        [("ps4", 0, 512)], [R("tokc", tb + 32, tb + 64)])
                    vop(lambda e, tb=tb: e.tensor_scalar_mul(tokc[:, tb + 64:tb + 96],
                                                             tokc[:, tb:tb + 32], -1.0),
                        [R("tokc", tb, tb + 32)], [R("tokc", tb + 64, tb + 96)])
                    sop(lambda e, tb=tb: e.activation(tokc[:, tb + 96:tb + 128],
                                                      tokc[:, tb + 32:tb + 64], AF.Exp),
                        [R("tokc", tb + 32, tb + 64)], [R("tokc", tb + 96, tb + 128)])
                    vop(lambda e: e.tensor_scalar_mul(bigE[0:32, 0:128],
                                                      ones2d[0:32, 0:128],
                                                      bg2[0:32, 255:256]),
                        [R("ones2d", 0, 128), R("bg2", 128, 256)], [(E, 0, 128)])
                    top(lambda e: e.transpose(ps[4][:, 160:192], bigE[0:32, 0:128],
                                              identf[0:32, 0:32]),
                        [(E, 0, 128)], [("ps4", 0, 512)])
                    sop(lambda e, tb=tb: e.activation(tokc[:, tb + 160:tb + 192],
                                                      ps[4][:, 160:192], AF.Exp),
                        [("ps4", 0, 512)], [R("tokc", tb + 160, tb + 192)])
                    vop(lambda e, tb=tb: e.tensor_sub(bigE[:, 1536:1568],
                                                      ps[4][:, 160:192],
                                                      tokc[:, tb + 32:tb + 64]),
                        [("ps4", 0, 512), R("tokc", tb + 32, tb + 64)], [(E, 1536, 1568)])
                    sop(lambda e, tb=tb: e.activation(tokc[:, tb + 128:tb + 160],
                                                      bigE[:, 1536:1568], AF.Exp),
                        [(E, 1536, 1568)], [R("tokc", tb + 128, tb + 160)])
                    sop(lambda e, tb=tb: e.activation(bigE[:, 1536:1568],
                                                      tokc[:, tb:tb + 32], AF.Ln),
                        [R("tokc", tb, tb + 32)], [(E, 1536, 1568)])
                    vop(lambda e, tb=tb: e.tensor_sub(tokc[:, tb + 192:tb + 224],
                                                      bigE[:, 1536:1568],
                                                      tokc[:, tb + 32:tb + 64]),
                        [(E, 1536, 1568), R("tokc", tb + 32, tb + 64)],
                        [R("tokc", tb + 192, tb + 224)])
                for kt in range(32):
                    fac = (1.0 / DK) if kt < 16 else (DK ** -0.5)
                    tkey = (A, kt * 512, (kt + 1) * 512)
                    vop(lambda e, kt=kt: e.tensor_mul(wbt[:, 0:512],
                                                      bigA[:, kt * 512:(kt + 1) * 512],
                                                      bigA[:, kt * 512:(kt + 1) * 512]),
                        [tkey], [R("wbt", 0, 512)])
                    top(lambda e: e.matmul(ps[4][0:1, 0:512], onescb[:, 0:1],
                                           wbt[:, 0:512], start=True, stop=True),
                        [R("wbt", 0, 512), R("onescb", 0, 1)], [("ps4", 0, 512)])
                    vop(lambda e: e.tensor_scalar_mul(rowB[0:1, 0:512],
                                                      ps[4][0:1, 0:512], 1.0 / DK),
                        [("ps4", 0, 512)], [R("rowB", 0, 512)])
                    sop(lambda e: e.activation(rowB[0:1, 0:512], rowB[0:1, 0:512],
                                               AF.Ln, bias=cols[0:1, CI["rmseps"]:CI["rmseps"] + 1]),
                        [R("rowB", 0, 512), R("cols", CI["rmseps"], CI["rmseps"] + 1)],
                        [R("rowB", 0, 512)])
                    sop(lambda e: e.activation(rowB[0:1, 0:512], rowB[0:1, 0:512],
                                               AF.Exp, scale=-0.5),
                        [R("rowB", 0, 512)], [R("rowB", 0, 512)])
                    vop(lambda e, fac=fac: e.tensor_scalar_mul(rowB[0:1, 0:512],
                                                               rowB[0:1, 0:512], fac),
                        [R("rowB", 0, 512)], [R("rowB", 0, 512)])
                    top(lambda e: e.matmul(ps[5][:, 0:512], onesf[0:1, 0:128],
                                           rowB[0:1, 0:512], start=True, stop=True),
                        [R("rowB", 0, 512), R("onesf", 0, 128)], [("ps5", 0, 512)])
                    vop(lambda e, kt=kt: e.tensor_mul(bigA[:, kt * 512:(kt + 1) * 512],
                                                      bigA[:, kt * 512:(kt + 1) * 512],
                                                      ps[5][:, 0:512]),
                        [tkey, ("ps5", 0, 512)], [tkey])
                for c in range(NCH):
                    tb = c * 256
                    ccol_ = lambda off, hh: tokc[:, tb + off + hh: tb + off + hh + 1]
                    ccolk = lambda off, hh: ("tokc", tb + off + hh, tb + off + hh + 1)
                    for h in range(NVH):
                        kh = h // 2
                        kap, kkey = ktile(kh, c)
                        qap, qkey = qtile(kh, c)
                        vap, vkey = vtile(h, c)
                        if h % 2 == 0:
                            top(lambda e, kap=kap: e.matmul(ps[1][:, 0:128], kap, kap,
                                                            start=True, stop=True),
                                [kkey], [("ps1", 0, 512)])
                            top(lambda e, kap=kap, qap=qap: e.matmul(
                                    ps[1][:, 128:256], kap, qap, start=True, stop=True),
                                [kkey, qkey], [("ps1", 0, 512)])
                            pb, pbk = pslot()
                            top(lambda e, kap=kap, pb=pb: e.transpose(pb, kap, identb[:, :]),
                                [kkey, R("identb", 0, 128)], [pbk])
                            vop(lambda e, pb=pb, kh=kh: e.tensor_copy(
                                    bigD[:, 4096 + kh * 128: 4096 + (kh + 1) * 128], pb),
                                [pbk], [(Dk, 4096 + kh * 128, 4096 + (kh + 1) * 128)])
                        ktokk = (Dk, 4096 + kh * 128, 4096 + (kh + 1) * 128)
                        sV1, kV1 = fslot()
                        cap_cum = ccol_(32, h)
                        vop(lambda e, sV1=sV1, cap=cap_cum: e.tensor_scalar_mul(
                                sV1, ones2d[:, :], cap),
                            [R("ones2d", 0, 128), ccolk(32, h)], [kV1])
                        top(lambda e, sV1=sV1: e.transpose(ps[0][:, 0:128], sV1,
                                                           identf[:, :]),
                            [kV1, R("identf", 0, 128)], [("ps0", 0, 512)])
                        sM, kM = fslot()
                        vop(lambda e, sM=sM, cap=cap_cum: e.tensor_scalar(
                                sM, ps[0][:, 0:128], cap, 0.0,
                                ALU.subtract, ALU.min),
                            [("ps0", 0, 512), ccolk(32, h)], [kM])
                        sE, kE = fslot()
                        sop(lambda e, sM=sM, sE=sE: e.activation(sE, sM, AF.Exp),
                            [kM], [kE])
                        sV2, kV2 = fslot()
                        cap_cmb = ccol_(192, h)
                        vop(lambda e, sV2=sV2, cap=cap_cmb: e.tensor_scalar_mul(
                                sV2, ones2d[:, :], cap),
                            [R("ones2d", 0, 128), ccolk(192, h)], [kV2])
                        top(lambda e, sV2=sV2: e.transpose(ps[0][:, 128:256], sV2,
                                                           identf[:, :]),
                            [kV2, R("identf", 0, 128)], [("ps0", 0, 512)])
                        sM2, kM2 = fslot()
                        vop(lambda e, sM2=sM2, cap=cap_cum: e.tensor_scalar(
                                sM2, ps[0][:, 128:256], cap, 0.0,
                                ALU.add, ALU.min),
                            [("ps0", 0, 512), ccolk(32, h)], [kM2])
                        sE2, kE2 = fslot()
                        sop(lambda e, sM2=sM2, sE2=sE2: e.activation(sE2, sM2, AF.Exp),
                            [kM2], [kE2])
                        sA1, kA1 = fslot()
                        vop(lambda e, sA1=sA1, sE=sE: e.tensor_mul(sA1, sE,
                                                                   ps[1][:, 0:128]),
                            [kE, ("ps1", 0, 512)], [kA1])
                        nT, nTk = bslot()
                        cap_nb = ccol_(64, h)
                        vop(lambda e, sA1=sA1, nT=nT, cap=cap_nb: e.scalar_tensor_tensor(
                                nT, sA1, cap, mask_t[:, :], ALU.mult, ALU.mult),
                            [kA1, ccolk(64, h), R("mask_t", 0, 128)], [nTk])
                        sA2, kA2 = fslot()
                        vop(lambda e, sA2=sA2, sE2=sE2: e.tensor_mul(sA2, sE2,
                                                                     ps[1][:, 0:128]),
                            [kE2, ("ps1", 0, 512)], [kA2])
                        nS, nSk = bslot()
                        vop(lambda e, sA2=sA2, nS=nS: e.tensor_mul(nS, sA2,
                                                                   mask_sn[:, :]),
                            [kA2, R("mask_sn", 0, 128)], [nSk])
                        sB1, kB1 = fslot()
                        vop(lambda e, sB1=sB1, sE=sE: e.tensor_mul(sB1, sE,
                                                                   ps[1][:, 128:256]),
                            [kE, ("ps1", 0, 512)], [kB1])
                        btk = (Dk, h * 128, (h + 1) * 128)
                        vop(lambda e, sB1=sB1, h=h: e.tensor_mul(
                                bigD[:, h * 128:(h + 1) * 128], sB1, mask_i[:, :]),
                            [kB1, R("mask_i", 0, 128)], [btk])
                        zzlo = h * 256
                        zzk = (C, zzlo, zzlo + 256)
                        pb2, pbk2 = pslot()
                        top(lambda e, vap=vap, pb2=pb2: e.transpose(pb2, vap, identb[:, :]),
                            [vkey, R("identb", 0, 128)], [pbk2])
                        vop(lambda e, pb2=pb2, zzlo=zzlo: e.tensor_copy(
                                bigC[:, zzlo:zzlo + 128], pb2),
                            [pbk2], [(C, zzlo, zzlo + 128)])
                        cap_gam = ccol_(96, h)
                        vop(lambda e, zzlo=zzlo, kh=kh, cap=cap_gam: e.tensor_scalar_mul(
                                bigC[:, zzlo + 128:zzlo + 256],
                                bigD[:, 4096 + kh * 128: 4096 + (kh + 1) * 128],
                                cap),
                            [ktokk, ccolk(96, h)], [(C, zzlo + 128, zzlo + 256)])
                        curT, curTk = nT, nTk
                        curS, curSk = nS, nSk
                        for j in range(7):
                            top(lambda e, curT=curT, zzlo=zzlo: e.matmul(
                                    ps[2][:, 0:256], curT,
                                    bigC[:, zzlo:zzlo + 256], start=True, stop=True),
                                [curTk, zzk], [("ps2", 0, 512)])
                            vop(lambda e, zzlo=zzlo: e.tensor_add(
                                    bigC[:, zzlo:zzlo + 256],
                                    bigC[:, zzlo:zzlo + 256], ps[2][:, 0:256]),
                                [("ps2", 0, 512), zzk], [zzk])
                            if j < 6:
                                top(lambda e, curS=curS, curT=curT: e.matmul(
                                        ps[3][:, 0:128], curS, curT,
                                        start=True, stop=True),
                                    [curSk, curTk], [("ps3", 0, 512)])
                                top(lambda e, curS=curS, curT=curT: e.matmul(
                                        ps[3][:, 128:256], curT, curS,
                                        start=True, stop=True),
                                    [curTk, curSk], [("ps3", 0, 512)])
                                nT2, nT2k = bslot()
                                nS2, nS2k = bslot()
                                sop(lambda e, nT2=nT2: e.copy(nT2, ps[3][:, 0:128]),
                                    [("ps3", 0, 512)], [nT2k])
                                vop(lambda e, nS2=nS2: e.tensor_copy(nS2,
                                                                     ps[3][:, 128:256]),
                                    [("ps3", 0, 512)], [nS2k])
                                curT, curTk = nT2, nT2k
                                curS, curSk = nS2, nS2k
                    for h in range(NVH):
                        kh = h // 2
                        qap, qkey = qtile(kh, c)
                        zzlo = h * 256
                        zzk = (C, zzlo, zzlo + 256)
                        skey = ("S_sb", h * 128, (h + 1) * 128)
                        btk = (Dk, h * 128, (h + 1) * 128)
                        ktokk = (Dk, 4096 + kh * 128, 4096 + (kh + 1) * 128)
                        fb = 2 if h % 2 == 0 else 3
                        ob = 4 if h % 2 == 0 else 5
                        zr = bigB[:, 2048 + (h % 16) * 128: 2048 + (h % 16 + 1) * 128]
                        zrk = (Bk, 2048 + (h % 16) * 128, 2048 + (h % 16 + 1) * 128)
                        dma("y", zr, zscr[h * 65536:(h + 1) * 65536]
                            .rearrange("(p j) -> p j", p=128)[:, c * 128:(c + 1) * 128],
                            [("zscr", h * 65536, (h + 1) * 65536)], [zrk])
                        if c > 0:
                            sbf = bigB[:, (h % 16) * 128:(h % 16 + 1) * 128]
                            sbfk = (Bk, (h % 16) * 128, (h % 16 + 1) * 128)
                            vop(lambda e, sbf=sbf, h=h: e.tensor_copy(
                                    sbf, S_sb[:, h * 128:(h + 1) * 128]),
                                [skey], [sbfk])
                            ek, ekk = bslot()
                            pb3, pbk3 = pslot()
                            top(lambda e, pb3=pb3, zzlo=zzlo: e.transpose(
                                    pb3, bigC[:, zzlo + 128:zzlo + 256], identb[:, :]),
                                [zzk, R("identb", 0, 128)], [pbk3])
                            vop(lambda e, ek=ek, pb3=pb3: e.tensor_copy(ek, pb3),
                                [pbk3], [ekk])
                            top(lambda e, ek=ek, sbf=sbf, fb=fb: e.matmul(
                                    ps[fb][:, 0:128], ek, sbf, start=True, stop=True),
                                [ekk, sbfk], [(f"ps{fb}", 0, 512)])
                            top(lambda e, qap=qap, sbf=sbf, fb=fb: e.matmul(
                                    ps[fb][:, 128:256], qap, sbf, start=True, stop=True),
                                [qkey, sbfk], [(f"ps{fb}", 0, 512)])
                        sT, kT = fslot()
                        if c > 0:
                            vop(lambda e, sT=sT, zzlo=zzlo, fb=fb: e.tensor_sub(
                                    sT, bigC[:, zzlo:zzlo + 128], ps[fb][:, 0:128]),
                                [zzk, (f"ps{fb}", 0, 512)], [kT])
                        else:
                            vop(lambda e, sT=sT, zzlo=zzlo: e.tensor_copy(
                                    sT, bigC[:, zzlo:zzlo + 128]),
                                [zzk], [kT])
                        dl, dlk = bslot()
                        cap_beta = ccol_(0, h)
                        vop(lambda e, dl=dl, sT=sT, cap=cap_beta: e.tensor_scalar_mul(
                                dl, sT, cap),
                            [kT, ccolk(0, h)], [dlk])
                        top(lambda e, dl=dl, h=h, ob=ob: e.matmul(
                                ps[ob][:, 0:128], bigD[:, h * 128:(h + 1) * 128], dl,
                                start=True, stop=True),
                            [btk, dlk], [(f"ps{ob}", 0, 512)])
                        sO, kO = fslot()
                        if c > 0:
                            cap_gam2 = ccol_(96, h)
                            vop(lambda e, sO=sO, cap=cap_gam2, fb=fb: e.tensor_scalar_mul(
                                    sO, ps[fb][:, 128:256], cap),
                                [(f"ps{fb}", 0, 512), ccolk(96, h)], [kO])
                            vop(lambda e, sO=sO, ob=ob: e.tensor_add(
                                    sO, sO, ps[ob][:, 0:128]),
                                [(f"ps{ob}", 0, 512), kO], [kO])
                        else:
                            vop(lambda e, sO=sO, ob=ob: e.tensor_copy(
                                    sO, ps[ob][:, 0:128]),
                                [(f"ps{ob}", 0, 512)], [kO])
                        sSq, kSq = fslot()
                        sop(lambda e, sO=sO, sSq=sSq: e.activation(
                                sSq, sO, AF.Square, accum_out=prob[:, 32:33]),
                            [kO], [kSq, R("prob", 32, 33)])
                        sop(lambda e: e.activation(prob[:, 33:34], prob[:, 32:33],
                                                   AF.Ln, scale=1.0 / DV,
                                                   bias=cols[:, CI["rmseps"]:CI["rmseps"] + 1]),
                            [R("prob", 32, 33), R("cols", CI["rmseps"], CI["rmseps"] + 1)],
                            [R("prob", 33, 34)])
                        sop(lambda e: e.activation(prob[:, 34:35], prob[:, 33:34],
                                                   AF.Exp, scale=-0.5),
                            [R("prob", 33, 34)], [R("prob", 34, 35)])
                        onb, onbk = bslot()
                        vop(lambda e, onb=onb, sO=sO: e.tensor_scalar_mul(
                                onb, sO, prob[:, 34:35]),
                            [kO, R("prob", 34, 35)], [onbk])
                        pb4, pbk4 = pslot()
                        top(lambda e, onb=onb, pb4=pb4: e.transpose(pb4, onb, identb[:, :]),
                            [onbk, R("identb", 0, 128)], [pbk4])
                        gmap = (16 + h) if h < 16 else (h - 16)
                        glo = gmap * 512 + c * 128
                        vop(lambda e, pb4=pb4, glo=glo, zr=zr: e.tensor_mul(
                                bigA[:, glo:glo + 128], zr, pb4),
                            [pbk4, zrk], [(A, glo, glo + 128)])
                        kb, kbk = bslot()
                        cap_dec = ccol_(128, h)
                        vop(lambda e, kb=kb, kh=kh, cap=cap_dec: e.tensor_scalar_mul(
                                kb, bigD[:, 4096 + kh * 128: 4096 + (kh + 1) * 128],
                                cap),
                            [ktokk, ccolk(128, h)], [kbk])
                        top(lambda e, kb=kb, dl=dl, ob=ob: e.matmul(
                                ps[ob][:, 128:256], kb, dl, start=True, stop=True),
                            [kbk, dlk], [(f"ps{ob}", 0, 512)])
                        if b == 1 and sc == 3 and c == 3 and h == 0:
                            vop(lambda e: e.tensor_copy(bigE[:, 2560:2688],
                                                        S_sb[:, 0:128]),
                                [skey], [(E, 2560, 2688)])
                            dma("y", dbg[:, 0:128], bigE[:, 2560:2688],
                                [(E, 2560, 2688)], [("DBG", 0, 128)])
                            vop(lambda e, sT=sT: e.tensor_copy(bigE[:, 2560:2688], sT),
                                [kT], [(E, 2560, 2688)])
                            dma("y", dbg[:, 128:256], bigE[:, 2560:2688],
                                [(E, 2560, 2688)], [("DBG", 128, 256)])
                            vop(lambda e, dl=dl: e.tensor_copy(bigE[:, 2560:2688], dl),
                                [dlk], [(E, 2560, 2688)])
                            dma("y", dbg[:, 256:384], bigE[:, 2560:2688],
                                [(E, 2560, 2688)], [("DBG", 256, 384)])
                            vop(lambda e, fb=fb: e.tensor_copy(bigE[:, 2560:2688],
                                                               ps[fb][:, 128:256]),
                                [(f"ps{fb}", 0, 512)], [(E, 2560, 2688)])
                            dma("y", dbg[:, 384:512], bigE[:, 2560:2688],
                                [(E, 2560, 2688)], [("DBG", 384, 512)])
                            vop(lambda e, sO=sO: e.tensor_copy(bigE[:, 2560:2688], sO),
                                [kO], [(E, 2560, 2688)])
                            dma("y", dbg[:, 512:640], bigE[:, 2560:2688],
                                [(E, 2560, 2688)], [("DBG", 512, 640)])
                            vop(lambda e, onb=onb: e.tensor_copy(bigE[:, 2560:2688], onb),
                                [onbk], [(E, 2560, 2688)])
                            dma("y", dbg[:, 640:768], bigE[:, 2560:2688],
                                [(E, 2560, 2688)], [("DBG", 640, 768)])
                            vop(lambda e, zzlo=zzlo: e.tensor_copy(
                                    bigE[:, 2560:2816], bigC[:, zzlo:zzlo + 256]),
                                [zzk], [(E, 2560, 2816)])
                            dma("y", dbg[:, 768:1024], bigE[:, 2560:2816],
                                [(E, 2560, 2816)], [("DBG", 768, 1024)])
                        if c == 0:
                            vop(lambda e, h=h, ob=ob: e.tensor_copy(
                                    S_sb[:, h * 128:(h + 1) * 128], ps[ob][:, 128:256]),
                                [(f"ps{ob}", 0, 512)], [skey])
                        else:
                            cap_gc = ccol_(160, h)
                            vop(lambda e, h=h, ob=ob, cap=cap_gc: e.scalar_tensor_tensor(
                                    S_sb[:, h * 128:(h + 1) * 128],
                                    S_sb[:, h * 128:(h + 1) * 128], cap,
                                    ps[ob][:, 128:256], ALU.mult, ALU.add),
                                [skey, ccolk(160, h), (f"ps{ob}", 0, 512)], [skey])
                for pp2 in range(16):
                    slot = pp2 % 2
                    wkey = (Bk, slot * 4096, slot * 4096 + 4096)
                    load_panel("L0_attn_W_out", pp2, bigB, slot * 4096, wkey)
                    j = pp2
                    bank = j % 2
                    for k in range(32):
                        gk = (16 + k) if k < 16 else (k - 16)
                        top(lambda e, slot=slot, k=k, bank=bank, gk=gk: e.matmul(
                                ps[bank][:, 0:512],
                                bigB[:, slot * 4096 + k * 128:
                                     slot * 4096 + k * 128 + 128],
                                bigA[:, gk * 512:(gk + 1) * 512],
                                start=(k == 0), stop=(k == 31)),
                            [wkey, (A, gk * 512, (gk + 1) * 512)],
                            [(f"ps{bank}", 0, 512)] if k == 0 else [],
                            group_n=32 if k == 0 else 1)
                    xk = ("xfst", 0, 512)
                    dma("y", xfst[:, 0:512],
                        src_hbm[j * 128:(j + 1) * 128, t0:t0 + 512],
                        [xiv(src_key, j, t0)], [xk])
                    ykey = ("xfst", 512, 1024)
                    vop(lambda e, bank=bank: e.tensor_add(
                            xfst[:, 512:1024], ps[bank][:, 0:512], xfst[:, 0:512]),
                        [(f"ps{bank}", 0, 512), xk], [ykey])
                    dma("y", dst_hbm[j * 128:(j + 1) * 128, t0:t0 + 512],
                        xfst[:, 512:1024], [ykey], [xiv(dst_key, j, t0)])

    setup_constants()
    setup_vectors()
    setup_cast()
    B.barrier()
    seqs = []
    if "ffn1" in parts:
        seqs.append(("ffn1", x_in, "x_in"))
    chain = {"ffn1": (xA, "xA"), "attn": (xB, "xB")}
    srcs = {"ffn1": (x_in, "x_in"), "attn": (xA, "xA"), "ffn2": (xB, "xB")}
    plist = [p for p in ("ffn1", "attn", "ffn2") if p in parts]
    for i, p in enumerate(plist):
        src_hbm, src_key = srcs[p]
        if i + 1 < len(plist):
            dst_hbm, dst_key = srcs[plist[i + 1]]
        else:
            dst_hbm, dst_key = out, "out"
        if p in ("ffn1", "ffn2"):
            ffn(src_hbm, src_key, dst_hbm, dst_key)
        else:
            attn(src_hbm, src_key, dst_hbm, dst_key)
        B.barrier()
    for s_ in list(B.sems):
        if s_ == "y":
            continue
        v = B.cnt[s_]
        if v > 0 and B.waited["y"].get(s_, 0) < v:
            B.waited["y"][s_] = v
            B.stream["y"].append(([(s_, v)], None, None, 0))
    B.finalize()
    mybir.codegen_inst_isa_subclasses(nc)
    return nc


def attn(*a, **k):
    raise NotImplementedError("attn emitted later")


def kernel(**inputs):
    x = inputs["L0_attn_x"]
    nc = build_program()
    in_maps = []
    for c in range(NCORE):
        m = {}
        xc = x[2 * c: 2 * c + 2]
        m["L0_attn_x"] = np.ascontiguousarray(xc.reshape(NTOK, D).T)
        for n in WMATS:
            m[n] = np.ascontiguousarray(inputs[n])
        for n in VECS:
            m[n] = np.ascontiguousarray(inputs[n]).reshape(-1)
        in_maps.append(m)
    res = run_bass_kernel_spmd(nc, in_maps, core_ids=list(range(NCORE)))
    outs = []
    for c in range(NCORE):
        o = res.results[c]["OUT"]
        outs.append(o.T.reshape(2, SEQ, D))
    return np.concatenate(outs, axis=0)


# version 04
# speedup vs baseline: 1.0958x; 1.0958x over previous
import contextlib

import numpy as np

import concourse.bass as bass
import concourse.mybir as mybir
from concourse.bass_utils import run_bass_kernel_spmd

F32 = mybir.dt.float32
BF16 = mybir.dt.bfloat16
AF = mybir.ActivationFunctionType
ALU = mybir.AluOpType
AX = mybir.AxisListType.X

D = 2048
NCORE = 8
NBATCH = 2
SEQ = 2048
NTOK = NBATCH * SEQ
L = 1024
HID = [1024, 2048, 3072, 4096, 1024, 2048, 3072, 4096]
COST_LAMBDA = 1e-7
LNEPS = 1e-5
NKH, NVH, DK, DV = 16, 32, 128, 128
KCONV = 4
RMSEPS = 1e-6
KEY_DIM = NKH * DK
VALUE_DIM = NVH * DV
CONV_DIM = 2 * KEY_DIM + VALUE_DIM
SC, CH = 512, 128
NCH = SC // CH
FB = 512
NEG_BIG = -1e30
PANEL = 256


def panelw(name):
    if name in ("L0_attn_W_qkvz", "L0_ffn_up_w", "L0_ffn_core_w"):
        return 128
    K = WMATS[name][0]
    return 128 if K > 2048 else 256


def qkvz_perm(bw):
    kh, r = divmod(bw, 6)
    if r == 0:
        return kh
    if r == 1:
        return 16 + kh
    if r in (2, 3):
        return 32 + 2 * kh + (r - 2)
    return 64 + 2 * kh + (r - 4)

WMATS = {
    "L0_attn_W_qkvz": (D, 12288),
    "L0_attn_W_ba": (D, 64),
    "L0_attn_W_out": (VALUE_DIM, D),
    "L0_ffn_down_w": (D, L),
    "L0_ffn_router_w": (D, 8),
    "L0_ffn_up_w": (L, D),
    "L0_ffn_core_w": (D, D),
    "L0_ffn_shared_w1": (L, L),
    "L0_ffn_shared_w2": (L, L),
}
for _e, _h in enumerate(HID):
    WMATS[f"L0_ffn_e{_e}_w1"] = (L, _h)
    WMATS[f"L0_ffn_e{_e}_w2"] = (_h, L)

VECS = {
    "L0_attn_conv_w": CONV_DIM * KCONV,
    "L0_attn_dt_bias": NVH,
    "L0_attn_A_log": NVH,
    "L0_attn_norm_w": DV,
    "L0_ffn_down_b": L,
    "L0_ffn_up_b": D,
    "L0_ffn_router_b": 8,
    "L0_ffn_shared_b1": L,
    "L0_ffn_shared_b2": L,
    "L0_ffn_core_b": D,
}
for _e, _h in enumerate(HID):
    VECS[f"L0_ffn_e{_e}_b1"] = _h
    VECS[f"L0_ffn_e{_e}_b2"] = L


def ceil_div(a, b):
    return -(-a // b)


class Builder:

    ENGS = ("t", "v", "s", "g", "y")

    NDMASEM = 8

    def __init__(self, nc):
        self.nc = nc
        self.es = contextlib.ExitStack()
        self.stream = {e: [] for e in self.ENGS}
        self.sems = {}
        names = ["t", "v", "s", "g", "y"]
        for q in ("dy", "ds", "dg"):
            for i in range(self.NDMASEM):
                names.append(f"{q}{i}")
        for name in names:
            self.sems[name] = self.es.enter_context(nc.semaphore("sem_" + name))
        self.cnt = {name: 0 for name in self.sems}
        self.rr = {"dy": 0, "ds": 0, "dg": 0}
        self.waited = {e: {} for e in self.ENGS}
        self.track = {}
        self.same_engine_sync = True

    def _acc(self, key, lo, hi, pt, write):
        ivs = self.track.setdefault(key, [])
        deps = []
        if write:
            keep = []
            nlo, nhi = lo, hi
            for iv in ivs:
                if iv[1] <= lo or iv[0] >= hi:
                    keep.append(iv)
                else:
                    if iv[2] is not None:
                        deps.append(iv[2])
                    deps.extend(iv[3])
                    nlo = min(nlo, iv[0])
                    nhi = max(nhi, iv[1])
            keep.append([nlo, nhi, pt, []])
            self.track[key] = keep
        else:
            hit = False
            for iv in ivs:
                if iv[1] <= lo or iv[0] >= hi:
                    continue
                hit = True
                if iv[2] is not None:
                    deps.append(iv[2])
                iv[3].append(pt)
            if not hit:
                ivs.append([lo, hi, None, [pt]])
        return deps

    def emit(self, eng, fn, reads=(), writes=(), dma=False, group_n=1):
        if dma:
            q = "d" + eng
            sem = f"{q}{self.rr[q]}"
            self.rr[q] = (self.rr[q] + 1) % self.NDMASEM
            inc = 16
        else:
            sem = eng
            inc = 1
        pt = (sem, self.cnt[sem] + inc)
        wpt = (sem, self.cnt[sem] + inc * group_n)
        deps = []
        if dma and self.cnt[sem] > 0:
            deps.append((sem, self.cnt[sem]))
        for r in reads:
            deps += self._acc(r[0], r[1], r[2], pt, False)
        for w in writes:
            deps += self._acc(w[0], w[1], w[2], wpt, True)
        cnt_before = pt[1] - inc
        self.cnt[sem] = pt[1]
        wd = self.waited[eng]
        bys = {}
        for (s, v) in deps:
            if s == sem and v > cnt_before:
                continue
            if s == eng and not self.same_engine_sync:
                continue
            if wd.get(s, 0) >= v:
                continue
            bys[s] = max(bys.get(s, 0), v)
        for s, v in bys.items():
            wd[s] = v
        self.stream[eng].append((list(bys.items()), fn, sem, inc))
        return pt

    def barrier(self, engs=None):
        engs = engs or self.ENGS
        tgt = dict(self.cnt)
        for e in engs:
            waits = []
            for s, v in tgt.items():
                if v > 0 and self.waited[e].get(s, 0) < v:
                    self.waited[e][s] = v
                    waits.append((s, v))
            if waits:
                self.stream[e].append((waits, None, None, 0))

    def finalize(self):
        sems = self.sems
        with self.nc.Block() as block:
            def mk(eng_key):
                def body(e):
                    for waits, fn, sem, inc in self.stream[eng_key]:
                        for s, v in waits:
                            e.wait_ge(sems[s], v)
                        if fn is not None:
                            fn(e).then_inc(sems[sem], inc)
                return body
            block.tensor(mk("t"))
            block.vector(mk("v"))
            block.scalar(mk("s"))
            block.gpsimd(mk("g"))
            block.sync(mk("y"))


def build_program(parts=("ffn1", "attn", "ffn2")):
    nc = bass.Bass("TRN2", target_bir_lowering=False)
    B = Builder(nc)
    es = B.es

    x_in = nc.dram_tensor("L0_attn_x", [D, NTOK], F32, kind="ExternalInput").ap()
    out = nc.dram_tensor("OUT", [D, NTOK], F32, kind="ExternalOutput").ap()
    wmat = {n: nc.dram_tensor(n, list(s), F32, kind="ExternalInput").ap()
            for n, s in WMATS.items()}
    wvec = {n: nc.dram_tensor(n, [s], F32, kind="ExternalInput").ap()
            for n, s in VECS.items()}
    wscr = {n: nc.dram_tensor("scr_" + n, [K * N], BF16, kind="Internal").ap()
            for n, (K, N) in WMATS.items()}
    xA = nc.dram_tensor("xA", [D, NTOK], F32, kind="Internal").ap()
    xB = nc.dram_tensor("xB", [D, NTOK], F32, kind="Internal").ap()
    rowscr = nc.dram_tensor("rowscr", [16384], F32, kind="Internal").ap()
    zscr = nc.dram_tensor("zscr", [NVH * 128 * SC], BF16, kind="Internal").ap()
    dbg = nc.dram_tensor("DBG", [128, 2048], F32, kind="Internal").ap()

    def sb(name, p, w, dt):
        return es.enter_context(nc.sbuf_tensor(name, [p, w], dt))

    bigA = sb("bigA", 128, 38912, BF16)
    bigB = sb("bigB", 128, 8192, BF16)
    bigC = sb("bigC", 128, 8192, BF16)
    bigD = sb("bigD", 128, 8192, BF16)
    bigE = sb("bigE", 128, 4096, F32)
    S_sb = sb("S_sb", 128, 4096, F32)
    xfst = sb("xfst", 128, 1024, F32)
    hist = sb("hist", 128, 192, F32)
    cols = sb("cols", 128, 384, F32)
    ones2d = sb("ones2d", 128, 128, F32)
    gb1c = sb("gb1c", 128, 176, F32)
    gb1cb = sb("gb1cb", 128, 176, BF16)
    gb1c_raw = sb("gb1c_raw", 128, 176, F32)
    identf = sb("identf", 128, 128, F32)
    identb = sb("identb", 128, 128, BF16)
    mask_t = sb("mask_t", 128, 128, BF16)
    mask_i = sb("mask_i", 128, 128, BF16)
    mask_sn = sb("mask_sn", 128, 128, BF16)
    onesf = sb("onesf", 1, 128, F32)
    onesb = sb("onesb", 1, 512, BF16)
    onescb = sb("onescb", 128, 1, BF16)
    selt = sb("selt", 8, 1024, BF16)
    rowA = sb("rowA", 1, 2048, F32)
    rowB = sb("rowB", 1, 1024, F32)
    rowsb = sb("rowsb", 1, 3328, BF16)
    D8pm = sb("D8pm", 8, 1024, BF16)
    w8f = sb("w8f", 128, 64, F32)
    w8tb = sb("w8tb", 8, 512, BF16)
    wbt = sb("wbt", 128, 512, BF16)
    stat = rowA
    bg = sb("bg", 128, 512, F32)
    bg2 = sb("bg2", 128, 256, F32)
    tokc = sb("tokc", 128, 1024, F32)
    prob = sb("prob", 128, 64, F32)
    iot = sb("iot", 128, 128, F32)

    ps = [es.enter_context(nc.psum_tensor(f"ps{i}", [128, 512], F32)) for i in range(6)]
    pbf = [es.enter_context(nc.psum_tensor(f"pbf{i}", [128, 1024], BF16)) for i in range(2)]

    def R(t, lo, hi):
        return (t, lo, hi)

    A, Bk, C, Dk, E = "bigA", "bigB", "bigC", "bigD", "bigE"
    PS = [f"ps{i}" for i in range(6)]
    PB = ["pbf0", "pbf1"]

    def dma(eng, out_ap, in_ap, reads, writes):
        return B.emit(eng, lambda e: e.dma_start(out_ap, in_ap), reads=reads,
                      writes=writes, dma=True)

    def vop(fn, reads, writes):
        return B.emit("v", fn, reads=reads, writes=writes)

    def sop(fn, reads, writes):
        return B.emit("s", fn, reads=reads, writes=writes)

    def gop(fn, reads, writes):
        return B.emit("g", fn, reads=reads, writes=writes)

    def top(fn, reads, writes, group_n=1):
        return B.emit("t", fn, reads=reads, writes=writes, group_n=group_n)

    def xiv(xkey, j, t0, w=512):
        return (xkey, j * NTOK + t0, j * NTOK + t0 + w)

    def setup_constants():
        gop(lambda e: e.iota(iot[:, :], [[1, 128]], channel_multiplier=-1,
                             allow_small_or_imprecise_dtypes=True),
            [], [R("iot", 0, 128)])
        gop(lambda e: e.memset(onesf[:, :], 1.0), [], [R("onesf", 0, 128)])
        gop(lambda e: e.memset(onesb[:, :], 1.0), [], [R("onesb", 0, 512)])
        gop(lambda e: e.memset(onescb[:, :], 1.0), [], [R("onescb", 0, 1)])
        gop(lambda e: e.memset(ones2d[:, :], 1.0), [], [R("ones2d", 0, 128)])
        vop(lambda e: e.tensor_single_scalar(identf[:, :], iot[:, :], 0.0, ALU.is_equal),
            [R("iot", 0, 128)], [R("identf", 0, 128)])
        vop(lambda e: e.tensor_copy(identb[:, :], identf[:, :]),
            [R("identf", 0, 128)], [R("identb", 0, 128)])
        vop(lambda e: e.tensor_single_scalar(mask_t[:, :], iot[:, :], 0.0, ALU.is_gt),
            [R("iot", 0, 128)], [R("mask_t", 0, 128)])
        vop(lambda e: e.tensor_single_scalar(mask_i[:, :], iot[:, :], 0.0, ALU.is_ge),
            [R("iot", 0, 128)], [R("mask_i", 0, 128)])
        vop(lambda e: e.tensor_scalar(mask_sn[:, :], iot[:, :], 0.0, -1.0,
                                      ALU.is_lt, ALU.mult),
            [R("iot", 0, 128)], [R("mask_sn", 0, 128)])
        gop(lambda e: e.iota(bigE[0:8, 0:128], [[0, 128]], channel_multiplier=1,
                             allow_small_or_imprecise_dtypes=True),
            [], [R(E, 0, 128)])
        for ee in range(8):
            vop(lambda e, ee=ee: e.tensor_single_scalar(
                    selt[0:8, ee * 128:(ee + 1) * 128], bigE[0:8, 0:128],
                    float(ee), ALU.is_equal),
                [R(E, 0, 128)], [R("selt", ee * 128, (ee + 1) * 128)])

    _colptr = [0]
    CI = {}

    def colify(row_t, row_key, off, n, dst_t, dst_name, dst_off, stride=1):
        pbk = ps[5]
        if stride == 1:
            src_ap = row_t[0:1, off:off + n]
        else:
            src_ap = bass.AP(row_t, off, [[row_t.shape[1], 1], [stride, n]])
        top(lambda e: e.matmul(pbk[:, 0:n], onesf[0:1, 0:128], src_ap,
                               start=True, stop=True),
            [R(row_key, 0, row_t.shape[1]), R("onesf", 0, 128)], [R("ps5", 0, 512)])
        vop(lambda e: e.tensor_mul(prob[:, 0:n] if n <= 64 else bigE[:, 0:n],
                                   pbk[:, 0:n], identf[:, 0:n]),
            [R("ps5", 0, 512), R("identf", 0, 128)],
            [R("prob", 0, 64) if n <= 64 else R(E, 0, 128)])
        srct = prob if n <= 64 else bigE
        srck = R("prob", 0, 64) if n <= 64 else R(E, 0, 128)
        vop(lambda e: e.reduce_sum(dst_t[:, dst_off:dst_off + 1], srct[:, 0:n], axis=AX),
            [srck], [(dst_name, dst_off, dst_off + 1)])

    def ccol(row_t, row_key, off, n, stride=1):
        idx = _colptr[0]
        _colptr[0] += 1
        colify(row_t, row_key, off, n, cols, "cols", idx, stride)
        return idx

    def load_row(dst_off, name, lo, hi, row_tensor, row_key):
        dma("y", row_tensor[0:1, dst_off:dst_off + (hi - lo)], wvec[name][lo:hi],
            [], [(row_key, dst_off, dst_off + (hi - lo))])

    def setup_vectors():
        load_row(0, "L0_ffn_down_b", 0, L, rowB, "rowB")
        CI["lneps"] = _colptr[0]; _colptr[0] += 1
        CI["rmseps"] = _colptr[0]; _colptr[0] += 1
        CI["one"] = _colptr[0]; _colptr[0] += 1
        vop(lambda e: e.memset(cols[:, CI["lneps"]:CI["lneps"] + 1], LNEPS),
            [], [R("cols", CI["lneps"], CI["lneps"] + 1)])
        vop(lambda e: e.memset(cols[:, CI["rmseps"]:CI["rmseps"] + 1], RMSEPS),
            [], [R("cols", CI["rmseps"], CI["rmseps"] + 1)])
        vop(lambda e: e.memset(cols[:, CI["one"]:CI["one"] + 1], 1.0),
            [], [R("cols", CI["one"], CI["one"] + 1)])
        CI["down_b"] = _colptr[0]
        for j in range(8):
            ccol(rowB, "rowB", j * 128, 128)
        for half in range(2):
            load_row(0, "L0_ffn_up_b", half * 1024, (half + 1) * 1024, rowB, "rowB")
            load_row(0, "L0_ffn_core_b", half * 1024, (half + 1) * 1024, rowA, "rowA")
            vop(lambda e, half=half: e.tensor_add(
                    rowsb[0:1, half * 1024:(half + 1) * 1024],
                    rowB[0:1, 0:1024], rowA[0:1, 0:1024]),
                [R("rowB", 0, 1024), R("rowA", 0, 1024)],
                [R("rowsb", half * 1024, (half + 1) * 1024)])
        load_row(0, "L0_attn_dt_bias", 0, NVH, rowB, "rowB")
        load_row(32, "L0_attn_A_log", 0, NVH, rowB, "rowB")
        load_row(64, "L0_attn_norm_w", 0, DV, rowB, "rowB")
        CI["normw"] = ccol(rowB, "rowB", 64, 128)
        CI["dtbB"], CI["neB"] = 320, 352
        top(lambda e: e.matmul(ps[5][:, 0:32], onesf[0:1, 0:128], rowB[0:1, 0:32],
                               start=True, stop=True),
            [R("rowB", 0, 32), R("onesf", 0, 128)], [R("ps5", 0, 512)])
        vop(lambda e: e.tensor_copy(cols[:, 320:352], ps[5][:, 0:32]),
            [R("ps5", 0, 512)], [R("cols", 320, 352)])
        sop(lambda e: e.activation(rowB[0:1, 64:96], rowB[0:1, 32:64], AF.Exp),
            [R("rowB", 32, 64)], [R("rowB", 64, 96)])
        vop(lambda e: e.tensor_scalar_mul(rowB[0:1, 96:128], rowB[0:1, 64:96], -1.0),
            [R("rowB", 64, 96)], [R("rowB", 96, 128)])
        top(lambda e: e.matmul(ps[5][:, 0:32], onesf[0:1, 0:128], rowB[0:1, 96:128],
                               start=True, stop=True),
            [R("rowB", 96, 128), R("onesf", 0, 128)], [R("ps5", 0, 512)])
        vop(lambda e: e.tensor_copy(cols[:, 352:384], ps[5][:, 0:32]),
            [R("ps5", 0, 512)], [R("cols", 352, 384)])
        CI["b1"] = {}
        ptr = 0
        for ee in range(9):
            h = HID[ee] if ee < 8 else L
            nm = f"L0_ffn_e{ee}_b1" if ee < 8 else "L0_ffn_shared_b1"
            CI["b1"][ee] = ptr
            for half in range(ceil_div(h, 1024)):
                lo = half * 1024
                w = min(1024, h - lo)
                load_row(0, nm, lo, lo + w, rowB, "rowB")
                for j in range(w // 128):
                    colify(rowB, "rowB", j * 128, 128, gb1c_raw, "gb1c_raw", ptr)
                    ptr += 1
        CI["sb1"] = CI["b1"][8]
        sop(lambda e, ptr=ptr: e.activation(gb1c[:, 0:ptr], gb1c_raw[:, 0:ptr], AF.Gelu),
            [R("gb1c_raw", 0, ptr)], [R("gb1c", 0, ptr)])
        vop(lambda e, ptr=ptr: e.tensor_copy(gb1cb[:, 0:ptr], gb1c[:, 0:ptr]),
            [R("gb1c", 0, ptr)], [R("gb1cb", 0, ptr)])
        CI["conv"] = {}
        for piece in range(32):
            load_row(0, "L0_attn_conv_w", piece * 1024, (piece + 1) * 1024, rowB, "rowB")
            for tt in range(2):
                for j in range(4):
                    tile = piece * 2 + tt
                    CI["conv"][(tile, j)] = _colptr[0]
                    ccol(rowB, "rowB", tt * 512 + j, 128, stride=4)
        load_row(0, "L0_ffn_router_b", 0, 8, rowB, "rowB")
        for ee in range(8):
            cst = COST_LAMBDA * (L * HID[ee] + HID[ee] * L)
            vop(lambda e, ee=ee, cst=cst: e.memset(rowB[0:1, 8 + ee:9 + ee], float(cst)),
                [], [R("rowB", 8, 16)])
        vop(lambda e: e.tensor_sub(rowB[0:1, 16:24], rowB[0:1, 0:8], rowB[0:1, 8:16]),
            [R("rowB", 0, 16)], [R("rowB", 16, 24)])
        vop(lambda e: e.tensor_copy(rowsb[0:1, 3072:3080], rowB[0:1, 16:24]),
            [R("rowB", 16, 24)], [R("rowsb", 3072, 3080)])

    CASTOFF = 24576

    def cast_weight(name, post=None, scale_col=None, scale_imm=None):
        K, N = WMATS[name]
        PW = panelw(name)
        src, dst = wmat[name], wscr[name]
        sname = "scr_" + name
        pieces = ceil_div(N, 1024)
        for ks in range(K // 128):
            for pi in range(pieces):
                p0 = pi * 1024
                w = min(1024, N - p0)
                slot = (ks * pieces + pi) % 2
                cb = bigA[:, CASTOFF + slot * 1024: CASTOFF + slot * 1024 + w]
                ckey = (A, CASTOFF + slot * 1024, CASTOFF + slot * 1024 + w)
                dma("g", cb, src[ks * 128:(ks + 1) * 128, p0:p0 + w], [], [ckey])
                if scale_col is not None:
                    vop(lambda e, cb=cb, sc=scale_col: e.tensor_scalar_mul(
                            cb, cb, cols[:, sc:sc + 1]),
                        [ckey, R("cols", scale_col, scale_col + 1)], [ckey])
                if scale_imm is not None:
                    vop(lambda e, cb=cb, si=scale_imm: e.tensor_scalar_mul(cb, cb, si),
                        [ckey], [ckey])
                if post is not None:
                    post(cb, ckey, ks, p0, w)
                nk = K // 128
                for pp in range(p0 // PW, ceil_div(p0 + w, PW)):
                    pw = min(PW, N - pp * PW)
                    dpp = qkvz_perm(pp) if name == "L0_attn_W_qkvz" else pp
                    base = dpp * PW * K + ks * pw
                    dma("y", bass.AP(dst.tensor, base, [[nk * pw, 128], [1, pw]]),
                        cb[:, pp * PW - p0: pp * PW - p0 + pw],
                        [ckey], [(sname, dpp * PW * K, (dpp + 1) * PW * K)])

    def load_panel(name, pp, dest_t, dest_off, dkey, eng="s"):
        K, N = WMATS[name]
        PW = panelw(name)
        pw = min(PW, N - pp * PW)
        base = pp * PW * K
        nk = K // 128
        dma(eng, dest_t[:, dest_off: dest_off + nk * pw],
            wscr[name][base: base + K * pw].rearrange("(p kj) -> p kj", p=128),
            [("scr_" + name, base, base + K * pw)], [dkey])
        return pw

    def make_w2_post(ee):
        h = HID[ee]
        base = CI["b1"][ee]
        nk = h // 128

        def post(cb, ckey, ks, p0, w):
            for half in range(2):
                bank = 3 + half
                top(lambda e, cb=cb, half=half, bank=bank, ks=ks: e.matmul(
                        ps[bank][0:1, 0:512], gb1cb[:, base + ks: base + ks + 1],
                        cb[:, half * 512: half * 512 + 512],
                        start=(ks == 0), stop=(ks == nk - 1)),
                    [ckey, R("gb1cb", base + ks, base + ks + 1)],
                    [(f"ps{bank}", 0, 512)] if ks == 0 else [],
                    group_n=(2 * nk - 1) if ks == 0 else 1)
        return post

    def setup_cast():
        cast_weight("L0_ffn_down_w")
        cast_weight("L0_ffn_router_w")
        for ee in range(8):
            cast_weight(f"L0_ffn_e{ee}_w1")
            load_row(0, f"L0_ffn_e{ee}_b2", 0, L, rowB, "rowB")
            cast_weight(f"L0_ffn_e{ee}_w2", post=make_w2_post(ee))
            for half in range(2):
                bank = 3 + half
                lo = half * 512
                co = 1024 + half * 512
                if ee == 0:
                    vop(lambda e, bank=bank, co=co, lo=lo: e.tensor_add(
                            rowA[0:1, co:co + 512], rowB[0:1, lo:lo + 512],
                            ps[bank][0:1, 0:512]),
                        [(f"ps{bank}", 0, 512), R("rowB", lo, lo + 512)],
                        [R("rowA", co, co + 512)])
                else:
                    vop(lambda e, bank=bank, co=co: e.tensor_add(
                            rowA[0:1, co:co + 512], rowA[0:1, co:co + 512],
                            ps[bank][0:1, 0:512]),
                        [(f"ps{bank}", 0, 512), R("rowA", co, co + 512)],
                        [R("rowA", co, co + 512)])
                    vop(lambda e, co=co, lo=lo: e.tensor_add(
                            rowA[0:1, co:co + 512], rowA[0:1, co:co + 512],
                            rowB[0:1, lo:lo + 512]),
                        [R("rowB", lo, lo + 512), R("rowA", co, co + 512)],
                        [R("rowA", co, co + 512)])
                vop(lambda e, bank=bank, lo=lo: e.tensor_scalar_mul(
                        rowB[0:1, lo:lo + 512], ps[bank][0:1, 0:512], -1.0),
                    [(f"ps{bank}", 0, 512)],
                    [R("rowB", lo, lo + 512)])
            dma("y", rowscr[ee * 1024:(ee + 1) * 1024], rowB[0:1, 0:1024],
                [R("rowB", 0, 1024)], [("rowscr", ee * 1024, (ee + 1) * 1024)])
        dma("g", D8pm[:, :], rowscr[0:8192].rearrange("(p f) -> p f", p=8),
            [("rowscr", 0, 8192)], [R("D8pm", 0, 1024)])
        load_row(0, "L0_ffn_shared_b2", 0, L, rowB, "rowB")
        vop(lambda e: e.scalar_tensor_tensor(rowA[0:1, 1024:2048], rowB[0:1, 0:1024],
                                             0.1, rowA[0:1, 1024:2048], ALU.mult, ALU.add),
            [R("rowB", 0, 1024), R("rowA", 1024, 2048)], [R("rowA", 1024, 2048)])
        vop(lambda e: e.tensor_copy(rowsb[0:1, 2048:3072], rowA[0:1, 1024:2048]),
            [R("rowA", 1024, 2048)], [R("rowsb", 2048, 3072)])
        cast_weight("L0_ffn_shared_w1")
        cast_weight("L0_ffn_shared_w2", scale_imm=0.1)
        cast_weight("L0_ffn_up_w")
        cast_weight("L0_ffn_core_w")
        cast_weight("L0_attn_W_qkvz")
        cast_weight("L0_attn_W_ba")
        cast_weight("L0_attn_W_out", scale_col=CI["normw"])

    def layernorm(src_hbm, src_key, t0, dst_t, dst_off, dst_key):
        for k in range(16):
            dma("g", bigD[:, k * 512:(k + 1) * 512],
                src_hbm[k * 128:(k + 1) * 128, t0:t0 + 512],
                [xiv(src_key, k, t0)],
                [(Dk, k * 512, (k + 1) * 512)])
        for k in range(16):
            top(lambda e, k=k: e.matmul(ps[4][0:1, 0:512], onescb[:, 0:1],
                                        bigD[:, k * 512:(k + 1) * 512],
                                        start=(k == 0), stop=(k == 15)),
                [(Dk, k * 512, (k + 1) * 512), R("onescb", 0, 1)],
                [("ps4", 0, 512)] if k == 0 else [], group_n=16 if k == 0 else 1)
        for k in range(16):
            vop(lambda e, k=k: e.tensor_mul(wbt[:, 0:512], bigD[:, k * 512:(k + 1) * 512],
                                            bigD[:, k * 512:(k + 1) * 512]),
                [(Dk, k * 512, (k + 1) * 512)], [R("wbt", 0, 512)])
            top(lambda e, k=k: e.matmul(ps[5][0:1, 0:512], onescb[:, 0:1], wbt[:, 0:512],
                                        start=(k == 0), stop=(k == 15)),
                [R("wbt", 0, 512), R("onescb", 0, 1)],
                [("ps5", 0, 512)] if k == 0 else [], group_n=16 if k == 0 else 1)
        vop(lambda e: e.tensor_scalar_mul(stat[0:1, 0:512], ps[4][0:1, 0:512], 1.0 / D),
            [("ps4", 0, 512)], [R("rowA", 0, 512)])
        vop(lambda e: e.tensor_scalar_mul(stat[0:1, 512:1024], ps[5][0:1, 0:512], 1.0 / D),
            [("ps5", 0, 512)], [R("rowA", 512, 1024)])
        vop(lambda e: e.tensor_mul(stat[0:1, 1024:1536], stat[0:1, 0:512], stat[0:1, 0:512]),
            [R("rowA", 0, 512)], [R("rowA", 1024, 1536)])
        vop(lambda e: e.tensor_sub(stat[0:1, 512:1024], stat[0:1, 512:1024],
                                   stat[0:1, 1024:1536]),
            [R("rowA", 512, 1024), R("rowA", 1024, 1536)], [R("rowA", 512, 1024)])
        sop(lambda e: e.activation(stat[0:1, 512:1024], stat[0:1, 512:1024], AF.Ln,
                                   bias=cols[0:1, CI["lneps"]:CI["lneps"] + 1]),
            [R("rowA", 512, 1024), R("cols", CI["lneps"], CI["lneps"] + 1)],
            [R("rowA", 512, 1024)])
        sop(lambda e: e.activation(stat[0:1, 512:1024], stat[0:1, 512:1024], AF.Exp,
                                   scale=-0.5),
            [R("rowA", 512, 1024)], [R("rowA", 512, 1024)])
        vop(lambda e: e.tensor_mul(stat[0:1, 0:512], stat[0:1, 0:512], stat[0:1, 512:1024]),
            [R("rowA", 0, 512), R("rowA", 512, 1024)], [R("rowA", 0, 512)])
        top(lambda e: e.matmul(ps[4][:, 0:512], onesf[0:1, 0:128], stat[0:1, 512:1024],
                               start=True, stop=True),
            [R("rowA", 512, 1024), R("onesf", 0, 128)], [("ps4", 0, 512)])
        top(lambda e: e.matmul(ps[5][:, 0:512], onesf[0:1, 0:128], stat[0:1, 0:512],
                               start=True, stop=True),
            [R("rowA", 0, 512), R("onesf", 0, 128)], [("ps5", 0, 512)])
        for k in range(16):
            vop(lambda e, k=k: e.tensor_mul(bigE[:, 0:512], bigD[:, k * 512:(k + 1) * 512],
                                            ps[4][:, 0:512]),
                [(Dk, k * 512, (k + 1) * 512), ("ps4", 0, 512)], [(E, 0, 512)])
            vop(lambda e, k=k: e.tensor_sub(dst_t[:, dst_off + k * 512: dst_off + (k + 1) * 512],
                                            bigE[:, 0:512], ps[5][:, 0:512]),
                [(E, 0, 512), ("ps5", 0, 512)],
                [(dst_key, dst_off + k * 512, dst_off + (k + 1) * 512)])

    def ffn(src_hbm, src_key, dst_hbm, dst_key):
        TOFF, GCT, OM = 16384, 26624, 34816
        for blk in range(NTOK // FB):
            t0 = blk * FB
            layernorm(src_hbm, src_key, t0, bigA, TOFF, A)
            for k in range(16):
                sop(lambda e, k=k: e.activation(bigA[:, GCT + k * 512: GCT + (k + 1) * 512],
                                                bigA[:, TOFF + k * 512: TOFF + (k + 1) * 512],
                                                AF.Gelu),
                    [(A, TOFF + k * 512, TOFF + (k + 1) * 512)],
                    [(A, GCT + k * 512, GCT + (k + 1) * 512)])
            for pp in range(4):
                slot = pp % 2
                dkey = (Bk, slot * 4096, slot * 4096 + 4096)
                load_panel("L0_ffn_down_w", pp, bigB, slot * 4096, dkey)
                for half in range(2):
                    j = pp * 2 + half
                    bank = j % 2
                    for k in range(16):
                        top(lambda e, slot=slot, half=half, k=k, bank=bank: e.matmul(
                                ps[bank][:, 0:512],
                                bigB[:, slot * 4096 + k * 256 + half * 128:
                                     slot * 4096 + k * 256 + half * 128 + 128],
                                bigA[:, TOFF + k * 512: TOFF + (k + 1) * 512],
                                start=(k == 0), stop=(k == 15)),
                            [dkey, (A, TOFF + k * 512, TOFF + (k + 1) * 512)],
                            [(f"ps{bank}", 0, 512)] if k == 0 else [],
                            group_n=16 if k == 0 else 1)
                    bcol = CI["down_b"] + j
                    sop(lambda e, j=j, bank=bank, bcol=bcol: e.activation(
                            bigC[:, j * 512:(j + 1) * 512], ps[bank][:, 0:512], AF.Gelu,
                            bias=cols[:, bcol:bcol + 1]),
                        [(f"ps{bank}", 0, 512), R("cols", bcol, bcol + 1)],
                        [(C, j * 512, (j + 1) * 512)])
            rkey = (Bk, 0, 128)
            load_panel("L0_ffn_router_w", 0, bigB, 0, rkey)
            for sub in range(4):
                for k in range(16):
                    top(lambda e, sub=sub, k=k: e.matmul(
                            ps[2][:, 0:8],
                            bigA[:, TOFF + k * 512 + sub * 128: TOFF + k * 512 + sub * 128 + 128],
                            bigB[:, k * 8: k * 8 + 8],
                            start=(k == 0), stop=False),
                        [rkey, (A, TOFF + k * 512 + sub * 128,
                                TOFF + k * 512 + sub * 128 + 128)],
                        [("ps2", 0, 512)] if k == 0 else [],
                        group_n=17 if k == 0 else 1)
                top(lambda e: e.matmul(ps[2][:, 0:8], onesb[0:1, 0:128],
                                       rowsb[0:1, 3072:3080], start=False, stop=True),
                    [R("rowsb", 3072, 3080), R("onesb", 0, 128)], [])
                vop(lambda e: e.reduce_max(prob[:, 8:9], ps[2][:, 0:8], axis=AX, negate=True),
                    [("ps2", 0, 512)], [R("prob", 8, 9)])
                sop(lambda e: e.activation(prob[:, 0:8], ps[2][:, 0:8], AF.Exp,
                                           bias=prob[:, 8:9], accum_out=prob[:, 9:10]),
                    [("ps2", 0, 512), R("prob", 8, 9)],
                    [R("prob", 0, 8), R("prob", 9, 10)])
                vop(lambda e: e.reciprocal(prob[:, 10:11], prob[:, 9:10]),
                    [R("prob", 9, 10)], [R("prob", 10, 11)])
                vop(lambda e: e.tensor_scalar_mul(prob[:, 0:8], prob[:, 0:8], prob[:, 10:11]),
                    [R("prob", 0, 8), R("prob", 10, 11)], [R("prob", 0, 8)])
                vop(lambda e: e.max(prob[:, 16:24], prob[:, 0:8]),
                    [R("prob", 0, 8)], [R("prob", 16, 24)])
                vop(lambda e: e.memset(prob[:, 24:32], NEG_BIG), [], [R("prob", 24, 32)])
                vop(lambda e: e.tensor_copy(prob[:, 24:25], prob[:, 16:17]),
                    [R("prob", 16, 24)], [R("prob", 24, 32)])
                vop(lambda e: e.match_replace(prob[:, 32:40], prob[:, 24:32], prob[:, 0:8],
                                              NEG_BIG),
                    [R("prob", 24, 32), R("prob", 0, 8)], [R("prob", 32, 40)])
                vop(lambda e: e.tensor_tensor(prob[:, 40:48], prob[:, 0:8], prob[:, 32:40],
                                              ALU.not_equal),
                    [R("prob", 0, 8), R("prob", 32, 40)], [R("prob", 40, 48)])
                vop(lambda e: e.tensor_copy(prob[:, 24:25], prob[:, 17:18]),
                    [R("prob", 16, 24)], [R("prob", 24, 32)])
                vop(lambda e: e.match_replace(prob[:, 48:56], prob[:, 24:32], prob[:, 32:40],
                                              NEG_BIG),
                    [R("prob", 24, 32), R("prob", 32, 40)], [R("prob", 48, 56)])
                vop(lambda e: e.tensor_tensor(prob[:, 56:64], prob[:, 32:40], prob[:, 48:56],
                                              ALU.not_equal),
                    [R("prob", 32, 40), R("prob", 48, 56)], [R("prob", 56, 64)])
                vop(lambda e: e.tensor_sub(prob[:, 11:12], prob[:, 17:18], prob[:, 16:17]),
                    [R("prob", 16, 24)], [R("prob", 11, 12)])
                sop(lambda e: e.activation(prob[:, 12:13], prob[:, 11:12], AF.Exp),
                    [R("prob", 11, 12)], [R("prob", 12, 13)])
                vop(lambda e: e.tensor_scalar_add(prob[:, 13:14], prob[:, 12:13], 1.0),
                    [R("prob", 12, 13)], [R("prob", 13, 14)])
                vop(lambda e: e.reciprocal(prob[:, 13:14], prob[:, 13:14]),
                    [R("prob", 13, 14)], [R("prob", 13, 14)])
                vop(lambda e: e.tensor_mul(prob[:, 14:15], prob[:, 12:13], prob[:, 13:14]),
                    [R("prob", 12, 13), R("prob", 13, 14)], [R("prob", 14, 15)])
                vop(lambda e: e.tensor_scalar_mul(prob[:, 40:48], prob[:, 40:48],
                                                  prob[:, 13:14]),
                    [R("prob", 40, 48), R("prob", 13, 14)], [R("prob", 40, 48)])
                vop(lambda e, sub=sub: e.scalar_tensor_tensor(
                        w8f[:, sub * 8:(sub + 1) * 8], prob[:, 56:64], prob[:, 14:15],
                        prob[:, 40:48], ALU.mult, ALU.add),
                    [R("prob", 56, 64), R("prob", 40, 48), R("prob", 14, 15)],
                    [R("w8f", sub * 8, (sub + 1) * 8)])
                top(lambda e, sub=sub: e.transpose(ps[3][0:8, sub * 128:(sub + 1) * 128],
                                                   w8f[:, sub * 8:(sub + 1) * 8],
                                                   identf[:, :]),
                    [R("w8f", sub * 8, (sub + 1) * 8), R("identf", 0, 128)],
                    [("ps3", 0, 512)])
            vop(lambda e: e.tensor_copy(w8tb[0:8, 0:512], ps[3][0:8, 0:512]),
                [("ps3", 0, 512)], [R("w8tb", 0, 512)])
            for ee in range(9):
                h = HID[ee] if ee < 8 else L
                nht = h // 128
                w1n = f"L0_ffn_e{ee}_w1" if ee < 8 else "L0_ffn_shared_w1"
                w2n = f"L0_ffn_e{ee}_w2" if ee < 8 else "L0_ffn_shared_w2"
                if ee < 8:
                    top(lambda e, ee=ee: e.matmul(ps[2][:, 0:512],
                                                  selt[0:8, ee * 128:(ee + 1) * 128],
                                                  w8tb[0:8, 0:512], start=True, stop=True),
                        [R("selt", ee * 128, (ee + 1) * 128), R("w8tb", 0, 512)],
                        [("ps2", 0, 512)])
                    vop(lambda e: e.tensor_copy(wbt[:, 0:512], ps[2][:, 0:512]),
                        [("ps2", 0, 512)], [R("wbt", 0, 512)])
                for pp in range(h // PANEL):
                    slot = pp % 2
                    dkey = (Bk, slot * 4096, slot * 4096 + 2048)
                    load_panel(w1n, pp, bigB, slot * 4096, dkey)
                    for half in range(2):
                        hj = pp * 2 + half
                        bank = hj % 2
                        for k in range(8):
                            top(lambda e, slot=slot, half=half, k=k, bank=bank: e.matmul(
                                    ps[bank][:, 0:512],
                                    bigB[:, slot * 4096 + k * 256 + half * 128:
                                         slot * 4096 + k * 256 + half * 128 + 128],
                                    bigC[:, k * 512:(k + 1) * 512],
                                    start=(k == 0), stop=(k == 7)),
                                [dkey, (C, k * 512, (k + 1) * 512)],
                                [(f"ps{bank}", 0, 512)] if k == 0 else [],
                                group_n=8 if k == 0 else 1)
                        gsl = (A, hj * 512, (hj + 1) * 512)
                        bcol = CI["b1"][ee] + hj
                        sop(lambda e, hj=hj, bank=bank, bcol=bcol: e.activation(
                                bigA[:, hj * 512:(hj + 1) * 512], ps[bank][:, 0:512],
                                AF.Gelu, bias=gb1c_raw[:, bcol:bcol + 1]),
                            [(f"ps{bank}", 0, 512), R("gb1c_raw", bcol, bcol + 1)], [gsl])
                        if ee < 8:
                            vop(lambda e, hj=hj: e.tensor_mul(
                                    bigA[:, hj * 512:(hj + 1) * 512],
                                    bigA[:, hj * 512:(hj + 1) * 512], wbt[:, 0:512]),
                                [gsl, R("wbt", 0, 512)], [gsl])
                w2pw = panelw(w2n)
                for pp in range(L // w2pw):
                    slot = pp % 2
                    pkey = (Bk, slot * 4096, slot * 4096 + nht * w2pw)
                    load_panel(w2n, pp, bigB, slot * 4096, pkey)
                    for half in range(w2pw // 128):
                        j = pp * (w2pw // 128) + half
                        bank = 2 + (j % 2)
                        for k in range(nht):
                            top(lambda e, slot=slot, half=half, k=k, bank=bank,
                                       w2pw=w2pw, nht=nht: e.matmul(
                                    ps[bank][:, 0:512],
                                    bigB[:, slot * 4096 + k * w2pw + half * 128:
                                         slot * 4096 + k * w2pw + half * 128 + 128],
                                    bigA[:, k * 512:(k + 1) * 512],
                                    start=(k == 0), stop=(k == nht - 1)),
                                [pkey, (A, k * 512, (k + 1) * 512)],
                                [(f"ps{bank}", 0, 512)] if k == 0 else [],
                                group_n=nht if k == 0 else 1)
                        if ee == 0:
                            vop(lambda e, j=j, bank=bank: e.tensor_copy(
                                    bigE[:, j * 512:(j + 1) * 512], ps[bank][:, 0:512]),
                                [(f"ps{bank}", 0, 512)], [(E, j * 512, (j + 1) * 512)])
                        else:
                            vop(lambda e, j=j, bank=bank: e.tensor_add(
                                    bigE[:, j * 512:(j + 1) * 512],
                                    bigE[:, j * 512:(j + 1) * 512], ps[bank][:, 0:512]),
                                [(f"ps{bank}", 0, 512), (E, j * 512, (j + 1) * 512)],
                                [(E, j * 512, (j + 1) * 512)])
            for j in range(8):
                top(lambda e, j=j: e.matmul(ps[2][:, 0:512], D8pm[0:8, j * 128:(j + 1) * 128],
                                            w8tb[0:8, 0:512], start=True, stop=False),
                    [R("D8pm", j * 128, (j + 1) * 128), R("w8tb", 0, 512)],
                    [("ps2", 0, 512)], group_n=2)
                top(lambda e, j=j: e.matmul(ps[2][:, 0:512],
                                            rowsb[0:1, 2048 + j * 128: 2048 + (j + 1) * 128],
                                            onesb[0:1, 0:512], start=False, stop=True),
                    [R("rowsb", 2048 + j * 128, 2048 + (j + 1) * 128), R("onesb", 0, 512)],
                    [])
                vop(lambda e, j=j: e.tensor_add(bigA[:, OM + j * 512: OM + (j + 1) * 512],
                                                bigE[:, j * 512:(j + 1) * 512],
                                                ps[2][:, 0:512]),
                    [(E, j * 512, (j + 1) * 512), ("ps2", 0, 512)],
                    [(A, OM + j * 512, OM + (j + 1) * 512)])
            for grp in range(16):
                j = grp
                slot = grp % 2
                bank = grp % 2
                ukey = (Bk, slot * 4096, slot * 4096 + 1024)
                ckey = (Bk, slot * 4096 + 1024, slot * 4096 + 3072)
                load_panel("L0_ffn_up_w", grp, bigB, slot * 4096, ukey)
                load_panel("L0_ffn_core_w", grp, bigB, slot * 4096 + 1024, ckey)
                for k in range(8):
                    top(lambda e, slot=slot, k=k, bank=bank: e.matmul(
                            ps[bank][:, 0:512],
                            bigB[:, slot * 4096 + k * 128: slot * 4096 + k * 128 + 128],
                            bigA[:, OM + k * 512: OM + (k + 1) * 512],
                            start=(k == 0), stop=False),
                        [ukey, (A, OM + k * 512, OM + (k + 1) * 512)],
                        [(f"ps{bank}", 0, 512)] if k == 0 else [],
                        group_n=25 if k == 0 else 1)
                for k in range(16):
                    top(lambda e, slot=slot, k=k, bank=bank: e.matmul(
                            ps[bank][:, 0:512],
                            bigB[:, slot * 4096 + 1024 + k * 128:
                                 slot * 4096 + 1024 + k * 128 + 128],
                            bigA[:, GCT + k * 512: GCT + (k + 1) * 512],
                            start=False, stop=False),
                        [ckey, (A, GCT + k * 512, GCT + (k + 1) * 512)], [])
                top(lambda e, j=j, bank=bank: e.matmul(
                        ps[bank][:, 0:512], rowsb[0:1, j * 128:(j + 1) * 128],
                        onesb[0:1, 0:512], start=False, stop=True),
                    [R("rowsb", j * 128, (j + 1) * 128), R("onesb", 0, 512)], [])
                xk = ("xfst", 0, 512)
                dma("y", xfst[:, 0:512],
                    src_hbm[j * 128:(j + 1) * 128, t0:t0 + 512],
                    [xiv(src_key, j, t0)], [xk])
                ykey = ("xfst", 512, 1024)
                vop(lambda e, bank=bank: e.scalar_tensor_tensor(
                        xfst[:, 512:1024], ps[bank][:, 0:512], 0.5,
                        xfst[:, 0:512], ALU.mult, ALU.add),
                    [(f"ps{bank}", 0, 512), xk], [ykey])
                dma("y", dst_hbm[j * 128:(j + 1) * 128, t0:t0 + 512],
                    xfst[:, 512:1024], [ykey], [xiv(dst_key, j, t0)])


    def attn(src_hbm, src_key, dst_hbm, dst_key):
        fslot_ctr = [0]

        def fslot():
            i = fslot_ctr[0] % 20
            fslot_ctr[0] += 1
            lo = 1536 + i * 128
            return bigE[:, lo:lo + 128], (E, lo, lo + 128)

        bslot_ctr = [0]

        def bslot():
            i = bslot_ctr[0] % 16
            bslot_ctr[0] += 1
            lo = 6144 + i * 128
            return bigD[:, lo:lo + 128], (Dk, lo, lo + 128)

        nslot_ctr = [0]

        def nslot():
            i = nslot_ctr[0] % 32
            nslot_ctr[0] += 1
            lo = 4096 + i * 128
            return bigB[:, lo:lo + 128], (Bk, lo, lo + 128)

        pslot_ctr = [0]

        def pslot():
            i = pslot_ctr[0] % 16
            pslot_ctr[0] += 1
            t = pbf[i // 8]
            lo = (i % 8) * 128
            return t[:, lo:lo + 128], (PB[i // 8], 0, 1024)

        def qtile(kh, c):
            lo = kh * 512 + c * 128
            return bigA[:, lo:lo + 128], (A, lo, lo + 128)

        def ktile(kh, c):
            lo = (16 + kh) * 512 + c * 128
            return bigA[:, lo:lo + 128], (A, lo, lo + 128)

        def vtile(h, c):
            lo = (32 + h) * 512 + c * 128
            return bigA[:, lo:lo + 128], (A, lo, lo + 128)

        for b in range(NBATCH):
            vop(lambda e: e.memset(S_sb[:, :], 0.0), [], [R("S_sb", 0, 4096)])
            vop(lambda e: e.memset(hist[:, :], 0.0), [], [R("hist", 0, 192)])
            for sc in range(SEQ // SC):
                t0 = b * SEQ + sc * SC
                layernorm(src_hbm, src_key, t0, bigC, 0, C)
                for pp in range(96):
                    slot = pp % 2
                    dkey = (Bk, slot * 4096, slot * 4096 + 2048)
                    load_panel("L0_attn_W_qkvz", pp, bigB, slot * 4096, dkey)
                    if True:
                        n = pp
                        bank = n % 2
                        for k in range(16):
                            top(lambda e, slot=slot, k=k, bank=bank: e.matmul(
                                    ps[bank][:, 0:512],
                                    bigB[:, slot * 4096 + k * 128:
                                         slot * 4096 + k * 128 + 128],
                                    bigC[:, k * 512:(k + 1) * 512],
                                    start=(k == 0), stop=(k == 15)),
                                [dkey, (C, k * 512, (k + 1) * 512)],
                                [(f"ps{bank}", 0, 512)] if k == 0 else [],
                                group_n=16 if k == 0 else 1)
                        if n < 64:
                            vop(lambda e, n=n: e.tensor_copy(bigE[:, 0:3],
                                                             hist[:, n * 3:(n + 1) * 3]),
                                [R("hist", n * 3, (n + 1) * 3)], [(E, 0, 3)])
                            vop(lambda e, bank=bank: e.tensor_copy(bigE[:, 3:515],
                                                                   ps[bank][:, 0:512]),
                                [(f"ps{bank}", 0, 512)], [(E, 3, 515)])
                            vop(lambda e, n=n: e.tensor_copy(hist[:, n * 3:(n + 1) * 3],
                                                             bigE[:, 512:515]),
                                [(E, 512, 515)], [R("hist", n * 3, (n + 1) * 3)])
                            c0 = CI["conv"][(n, 0)]
                            vop(lambda e, c0=c0: e.tensor_scalar_mul(
                                    bigE[:, 1536:2048], bigE[:, 0:512],
                                    cols[:, c0:c0 + 1]),
                                [(E, 0, 515), R("cols", c0, c0 + 1)], [(E, 1536, 2048)])
                            for j in range(1, 4):
                                cj = CI["conv"][(n, j)]
                                vop(lambda e, j=j, cj=cj: e.scalar_tensor_tensor(
                                        bigE[:, 1536:2048], bigE[:, j:j + 512],
                                        cols[:, cj:cj + 1], bigE[:, 1536:2048],
                                        ALU.mult, ALU.add),
                                    [(E, 0, 515), R("cols", cj, cj + 1), (E, 1536, 2048)],
                                    [(E, 1536, 2048)])
                            sop(lambda e, n=n: e.activation(
                                    bigA[:, n * 512:(n + 1) * 512], bigE[:, 1536:2048],
                                    AF.Silu),
                                [(E, 1536, 2048)], [(A, n * 512, (n + 1) * 512)])
                        else:
                            zt = n - 64
                            sop(lambda e, bank=bank: e.activation(
                                    wbt[:, 0:512], ps[bank][:, 0:512], AF.Silu),
                                [(f"ps{bank}", 0, 512)], [R("wbt", 0, 512)])
                            dma("y", zscr[zt * 65536:(zt + 1) * 65536]
                                .rearrange("(p j) -> p j", p=128), wbt[:, 0:512],
                                [R("wbt", 0, 512)], [("zscr", zt * 65536, (zt + 1) * 65536)])
                bakey = (Bk, 0, 1024)
                load_panel("L0_attn_W_ba", 0, bigB, 0, bakey)
                for k in range(16):
                    top(lambda e, k=k: e.matmul(
                            ps[2][0:64, 0:512], bigB[:, k * 64: k * 64 + 64],
                            bigC[:, k * 512:(k + 1) * 512],
                            start=(k == 0), stop=(k == 15)),
                        [bakey, (C, k * 512, (k + 1) * 512)],
                        [("ps2", 0, 512)] if k == 0 else [], group_n=16 if k == 0 else 1)
                vop(lambda e: e.tensor_copy(bigE[0:64, 0:512], ps[2][0:64, 0:512]),
                    [("ps2", 0, 512)], [(E, 0, 512)])
                for c in range(NCH):
                    tb = c * 256
                    top(lambda e, c=c: e.transpose(ps[3][:, 0:64],
                                                   bigE[0:64, c * 128:(c + 1) * 128],
                                                   identf[0:64, 0:64]),
                        [(E, 0, 512)], [("ps3", 0, 512)])
                    for par in range(2):
                        vop(lambda e, par=par: e.tensor_copy(
                                bass.AP(bigE, 1408 + par, [[4096, 128], [2, 16]]),
                                bass.AP(ps[3], par, [[512, 128], [4, 16]])),
                            [("ps3", 0, 512)], [(E, 1408, 1440)])
                        vop(lambda e, par=par: e.tensor_copy(
                                bass.AP(bigE, 1440 + par, [[4096, 128], [2, 16]]),
                                bass.AP(ps[3], 2 + par, [[512, 128], [4, 16]])),
                            [("ps3", 0, 512)], [(E, 1440, 1472)])
                    sop(lambda e, tb=tb: e.activation(tokc[:, tb:tb + 32],
                                                      bigE[:, 1408:1440], AF.Sigmoid),
                        [(E, 1408, 1440)], [R("tokc", tb, tb + 32)])
                    vop(lambda e, tb=tb: e.tensor_add(bigE[:, 1536:1568],
                                                      bigE[:, 1440:1472], cols[:, 320:352]),
                        [(E, 1440, 1472), R("cols", 320, 352)], [(E, 1536, 1568)])
                    sop(lambda e: e.activation(bigE[:, 1568:1600], bigE[:, 1536:1568],
                                               AF.Abs),
                        [(E, 1536, 1568)], [(E, 1568, 1600)])
                    sop(lambda e: e.activation(bigE[:, 1600:1632], bigE[:, 1568:1600],
                                               AF.Exp, scale=-1.0),
                        [(E, 1568, 1600)], [(E, 1600, 1632)])
                    sop(lambda e: e.activation(bigE[:, 1600:1632], bigE[:, 1600:1632],
                                               AF.Ln, bias=cols[:, CI["one"]:CI["one"] + 1]),
                        [(E, 1600, 1632), R("cols", CI["one"], CI["one"] + 1)],
                        [(E, 1600, 1632)])
                    vop(lambda e: e.tensor_scalar_max(bigE[:, 1536:1568],
                                                      bigE[:, 1536:1568], 0.0),
                        [(E, 1536, 1568)], [(E, 1536, 1568)])
                    vop(lambda e: e.tensor_add(bigE[:, 1536:1568], bigE[:, 1536:1568],
                                               bigE[:, 1600:1632]),
                        [(E, 1536, 1632)], [(E, 1536, 1568)])
                    vop(lambda e: e.tensor_mul(bigE[:, 1568:1600], bigE[:, 1536:1568],
                                               cols[:, 352:384]),
                        [(E, 1536, 1568), R("cols", 352, 384)], [(E, 1568, 1600)])
                    top(lambda e: e.transpose(ps[4][0:32, 0:128], bigE[:, 1568:1600],
                                              identf[:, :]),
                        [(E, 1568, 1600)], [("ps4", 0, 512)])
                    vop(lambda e: e.tensor_copy(bg2[0:32, 0:128], ps[4][0:32, 0:128]),
                        [("ps4", 0, 512)], [R("bg2", 0, 128)])
                    vop(lambda e: e.tensor_tensor_scan(bg2[0:32, 128:256],
                                                       bg2[0:32, 0:128],
                                                       bg2[0:32, 0:128], 0.0,
                                                       ALU.add, ALU.bypass),
                        [R("bg2", 0, 128)], [R("bg2", 128, 256)])
                    top(lambda e: e.transpose(ps[4][:, 128:160], bg2[0:32, 128:256],
                                              identf[0:32, 0:32]),
                        [R("bg2", 128, 256)], [("ps4", 0, 512)])
                    vop(lambda e, tb=tb: e.tensor_copy(tokc[:, tb + 32:tb + 64],
                                                       ps[4][:, 128:160]),
                        [("ps4", 0, 512)], [R("tokc", tb + 32, tb + 64)])
                    vop(lambda e, tb=tb: e.tensor_scalar_mul(tokc[:, tb + 64:tb + 96],
                                                             tokc[:, tb:tb + 32], -1.0),
                        [R("tokc", tb, tb + 32)], [R("tokc", tb + 64, tb + 96)])
                    sop(lambda e, tb=tb: e.activation(tokc[:, tb + 96:tb + 128],
                                                      tokc[:, tb + 32:tb + 64], AF.Exp),
                        [R("tokc", tb + 32, tb + 64)], [R("tokc", tb + 96, tb + 128)])
                    vop(lambda e: e.tensor_scalar_mul(bigE[0:32, 0:128],
                                                      ones2d[0:32, 0:128],
                                                      bg2[0:32, 255:256]),
                        [R("ones2d", 0, 128), R("bg2", 128, 256)], [(E, 0, 128)])
                    top(lambda e: e.transpose(ps[4][:, 160:192], bigE[0:32, 0:128],
                                              identf[0:32, 0:32]),
                        [(E, 0, 128)], [("ps4", 0, 512)])
                    sop(lambda e, tb=tb: e.activation(tokc[:, tb + 160:tb + 192],
                                                      ps[4][:, 160:192], AF.Exp),
                        [("ps4", 0, 512)], [R("tokc", tb + 160, tb + 192)])
                    vop(lambda e, tb=tb: e.tensor_sub(bigE[:, 1536:1568],
                                                      ps[4][:, 160:192],
                                                      tokc[:, tb + 32:tb + 64]),
                        [("ps4", 0, 512), R("tokc", tb + 32, tb + 64)], [(E, 1536, 1568)])
                    sop(lambda e, tb=tb: e.activation(tokc[:, tb + 128:tb + 160],
                                                      bigE[:, 1536:1568], AF.Exp),
                        [(E, 1536, 1568)], [R("tokc", tb + 128, tb + 160)])
                    sop(lambda e, tb=tb: e.activation(bigE[:, 1536:1568],
                                                      tokc[:, tb:tb + 32], AF.Ln),
                        [R("tokc", tb, tb + 32)], [(E, 1536, 1568)])
                    vop(lambda e, tb=tb: e.tensor_sub(tokc[:, tb + 192:tb + 224],
                                                      bigE[:, 1536:1568],
                                                      tokc[:, tb + 32:tb + 64]),
                        [(E, 1536, 1568), R("tokc", tb + 32, tb + 64)],
                        [R("tokc", tb + 192, tb + 224)])
                for kt in range(32):
                    fac = (1.0 / DK) if kt < 16 else (DK ** -0.5)
                    tkey = (A, kt * 512, (kt + 1) * 512)
                    vop(lambda e, kt=kt: e.tensor_mul(wbt[:, 0:512],
                                                      bigA[:, kt * 512:(kt + 1) * 512],
                                                      bigA[:, kt * 512:(kt + 1) * 512]),
                        [tkey], [R("wbt", 0, 512)])
                    top(lambda e: e.matmul(ps[4][0:1, 0:512], onescb[:, 0:1],
                                           wbt[:, 0:512], start=True, stop=True),
                        [R("wbt", 0, 512), R("onescb", 0, 1)], [("ps4", 0, 512)])
                    vop(lambda e: e.tensor_scalar_mul(rowB[0:1, 0:512],
                                                      ps[4][0:1, 0:512], 1.0 / DK),
                        [("ps4", 0, 512)], [R("rowB", 0, 512)])
                    sop(lambda e: e.activation(rowB[0:1, 0:512], rowB[0:1, 0:512],
                                               AF.Ln, bias=cols[0:1, CI["rmseps"]:CI["rmseps"] + 1]),
                        [R("rowB", 0, 512), R("cols", CI["rmseps"], CI["rmseps"] + 1)],
                        [R("rowB", 0, 512)])
                    sop(lambda e: e.activation(rowB[0:1, 0:512], rowB[0:1, 0:512],
                                               AF.Exp, scale=-0.5),
                        [R("rowB", 0, 512)], [R("rowB", 0, 512)])
                    vop(lambda e, fac=fac: e.tensor_scalar_mul(rowB[0:1, 0:512],
                                                               rowB[0:1, 0:512], fac),
                        [R("rowB", 0, 512)], [R("rowB", 0, 512)])
                    top(lambda e: e.matmul(ps[5][:, 0:512], onesf[0:1, 0:128],
                                           rowB[0:1, 0:512], start=True, stop=True),
                        [R("rowB", 0, 512), R("onesf", 0, 128)], [("ps5", 0, 512)])
                    vop(lambda e, kt=kt: e.tensor_mul(bigA[:, kt * 512:(kt + 1) * 512],
                                                      bigA[:, kt * 512:(kt + 1) * 512],
                                                      ps[5][:, 0:512]),
                        [tkey, ("ps5", 0, 512)], [tkey])
                for c in range(NCH):
                    tb = c * 256
                    ccol_ = lambda off, hh: tokc[:, tb + off + hh: tb + off + hh + 1]
                    ccolk = lambda off, hh: ("tokc", tb + off + hh, tb + off + hh + 1)
                    kksb = {}
                    for h in range(NVH):
                        kh = h // 2
                        par = h % 2
                        kpar = kh % 2
                        kap, kkey = ktile(kh, c)
                        qap, qkey = qtile(kh, c)
                        vap, vkey = vtile(h, c)
                        if h % 2 == 0:
                            top(lambda e, kap=kap, kpar=kpar: e.matmul(
                                    ps[4 + kpar][:, 0:128], kap, kap,
                                    start=True, stop=True),
                                [kkey], [(f"ps{4 + kpar}", 0, 512)])
                            top(lambda e, kap=kap, qap=qap, kpar=kpar: e.matmul(
                                    ps[4 + kpar][:, 128:256], kap, qap,
                                    start=True, stop=True),
                                [kkey, qkey], [(f"ps{4 + kpar}", 0, 512)])
                            kkt, kktk = nslot()
                            kqt, kqtk = nslot()
                            sop(lambda e, kkt=kkt, kpar=kpar: e.copy(
                                    kkt, ps[4 + kpar][:, 0:128]),
                                [(f"ps{4 + kpar}", 0, 512)], [kktk])
                            vop(lambda e, kqt=kqt, kpar=kpar: e.tensor_copy(
                                    kqt, ps[4 + kpar][:, 128:256]),
                                [(f"ps{4 + kpar}", 0, 512)], [kqtk])
                            kksb[kh] = (kkt, kktk, kqt, kqtk)
                            pb, pbk = pslot()
                            top(lambda e, kap=kap, pb=pb: e.transpose(pb, kap, identb[:, :]),
                                [kkey, R("identb", 0, 128)], [pbk])
                            vop(lambda e, pb=pb, kh=kh: e.tensor_copy(
                                    bigD[:, 4096 + kh * 128: 4096 + (kh + 1) * 128], pb),
                                [pbk], [(Dk, 4096 + kh * 128, 4096 + (kh + 1) * 128)])
                        kkt, kktk, kqt, kqtk = kksb[kh]
                        ktokk = (Dk, 4096 + kh * 128, 4096 + (kh + 1) * 128)
                        sV1, kV1 = fslot()
                        cap_cum = ccol_(32, h)
                        vop(lambda e, sV1=sV1, cap=cap_cum: e.tensor_scalar_mul(
                                sV1, ones2d[:, :], cap),
                            [R("ones2d", 0, 128), ccolk(32, h)], [kV1])
                        top(lambda e, sV1=sV1, par=par: e.transpose(
                                ps[par][:, 0:128], sV1, identf[:, :]),
                            [kV1, R("identf", 0, 128)], [(f"ps{par}", 0, 512)])
                        sM, kM = fslot()
                        vop(lambda e, sM=sM, cap=cap_cum, par=par: e.tensor_scalar(
                                sM, ps[par][:, 0:128], cap, 0.0,
                                ALU.subtract, ALU.min),
                            [(f"ps{par}", 0, 512), ccolk(32, h)], [kM])
                        sE, kE = fslot()
                        sop(lambda e, sM=sM, sE=sE: e.activation(sE, sM, AF.Exp),
                            [kM], [kE])
                        sV2, kV2 = fslot()
                        cap_cmb = ccol_(192, h)
                        vop(lambda e, sV2=sV2, cap=cap_cmb: e.tensor_scalar_mul(
                                sV2, ones2d[:, :], cap),
                            [R("ones2d", 0, 128), ccolk(192, h)], [kV2])
                        top(lambda e, sV2=sV2, par=par: e.transpose(
                                ps[par][:, 128:256], sV2, identf[:, :]),
                            [kV2, R("identf", 0, 128)], [(f"ps{par}", 0, 512)])
                        sM2, kM2 = fslot()
                        vop(lambda e, sM2=sM2, cap=cap_cum, par=par: e.tensor_scalar(
                                sM2, ps[par][:, 128:256], cap, 0.0,
                                ALU.add, ALU.min),
                            [(f"ps{par}", 0, 512), ccolk(32, h)], [kM2])
                        sE2, kE2 = fslot()
                        sop(lambda e, sM2=sM2, sE2=sE2: e.activation(sE2, sM2, AF.Exp),
                            [kM2], [kE2])
                        sA1, kA1 = fslot()
                        vop(lambda e, sA1=sA1, sE=sE, kkt=kkt: e.tensor_mul(sA1, sE, kkt),
                            [kE, kktk], [kA1])
                        nT, nTk = nslot()
                        cap_nb = ccol_(64, h)
                        vop(lambda e, sA1=sA1, nT=nT, cap=cap_nb: e.scalar_tensor_tensor(
                                nT, sA1, cap, mask_t[:, :], ALU.mult, ALU.mult),
                            [kA1, ccolk(64, h), R("mask_t", 0, 128)], [nTk])
                        sA2, kA2 = fslot()
                        vop(lambda e, sA2=sA2, sE2=sE2, kkt=kkt: e.tensor_mul(sA2, sE2, kkt),
                            [kE2, kktk], [kA2])
                        nS, nSk = nslot()
                        vop(lambda e, sA2=sA2, nS=nS: e.tensor_mul(nS, sA2,
                                                                   mask_sn[:, :]),
                            [kA2, R("mask_sn", 0, 128)], [nSk])
                        sB1, kB1 = fslot()
                        vop(lambda e, sB1=sB1, sE=sE, kqt=kqt: e.tensor_mul(sB1, sE, kqt),
                            [kE, kqtk], [kB1])
                        btk = (Dk, h * 128, (h + 1) * 128)
                        vop(lambda e, sB1=sB1, h=h: e.tensor_mul(
                                bigD[:, h * 128:(h + 1) * 128], sB1, mask_i[:, :]),
                            [kB1, R("mask_i", 0, 128)], [btk])
                        zzlo = h * 256
                        zzk = (C, zzlo, zzlo + 256)
                        pb2, pbk2 = pslot()
                        top(lambda e, vap=vap, pb2=pb2: e.transpose(pb2, vap, identb[:, :]),
                            [vkey, R("identb", 0, 128)], [pbk2])
                        vop(lambda e, pb2=pb2, zzlo=zzlo: e.tensor_copy(
                                bigC[:, zzlo:zzlo + 128], pb2),
                            [pbk2], [(C, zzlo, zzlo + 128)])
                        cap_gam = ccol_(96, h)
                        vop(lambda e, zzlo=zzlo, kh=kh, cap=cap_gam: e.tensor_scalar_mul(
                                bigC[:, zzlo + 128:zzlo + 256],
                                bigD[:, 4096 + kh * 128: 4096 + (kh + 1) * 128],
                                cap),
                            [ktokk, ccolk(96, h)], [(C, zzlo + 128, zzlo + 256)])
                        curT, curTk = nT, nTk
                        curS, curSk = nS, nSk
                        ab = 2 + par
                        sqb = par
                        for j in range(7):
                            top(lambda e, curT=curT, zzlo=zzlo, ab=ab: e.matmul(
                                    ps[ab][:, 0:256], curT,
                                    bigC[:, zzlo:zzlo + 256], start=True, stop=True),
                                [curTk, zzk], [(f"ps{ab}", 0, 512)])
                            vop(lambda e, zzlo=zzlo, ab=ab: e.tensor_add(
                                    bigC[:, zzlo:zzlo + 256],
                                    bigC[:, zzlo:zzlo + 256], ps[ab][:, 0:256]),
                                [(f"ps{ab}", 0, 512), zzk], [zzk])
                            if j < 6:
                                top(lambda e, curS=curS, curT=curT, sqb=sqb: e.matmul(
                                        ps[sqb][:, 256:384], curS, curT,
                                        start=True, stop=True),
                                    [curSk, curTk], [(f"ps{sqb}", 0, 512)])
                                top(lambda e, curS=curS, curT=curT, sqb=sqb: e.matmul(
                                        ps[sqb][:, 384:512], curT, curS,
                                        start=True, stop=True),
                                    [curTk, curSk], [(f"ps{sqb}", 0, 512)])
                                nT2, nT2k = nslot()
                                nS2, nS2k = nslot()
                                sop(lambda e, nT2=nT2, sqb=sqb: e.copy(
                                        nT2, ps[sqb][:, 256:384]),
                                    [(f"ps{sqb}", 0, 512)], [nT2k])
                                vop(lambda e, nS2=nS2, sqb=sqb: e.tensor_copy(
                                        nS2, ps[sqb][:, 384:512]),
                                    [(f"ps{sqb}", 0, 512)], [nS2k])
                                curT, curTk = nT2, nT2k
                                curS, curSk = nS2, nS2k
                    for h in range(NVH):
                        kh = h // 2
                        qap, qkey = qtile(kh, c)
                        zzlo = h * 256
                        zzk = (C, zzlo, zzlo + 256)
                        skey = ("S_sb", h * 128, (h + 1) * 128)
                        btk = (Dk, h * 128, (h + 1) * 128)
                        ktokk = (Dk, 4096 + kh * 128, 4096 + (kh + 1) * 128)
                        fb = 2 if h % 2 == 0 else 3
                        ob = 4 if h % 2 == 0 else 5
                        zr = bigB[:, 2048 + (h % 16) * 128: 2048 + (h % 16 + 1) * 128]
                        zrk = (Bk, 2048 + (h % 16) * 128, 2048 + (h % 16 + 1) * 128)
                        dma("y", zr, zscr[h * 65536:(h + 1) * 65536]
                            .rearrange("(p j) -> p j", p=128)[:, c * 128:(c + 1) * 128],
                            [("zscr", h * 65536, (h + 1) * 65536)], [zrk])
                        if c > 0:
                            sbf = bigB[:, (h % 16) * 128:(h % 16 + 1) * 128]
                            sbfk = (Bk, (h % 16) * 128, (h % 16 + 1) * 128)
                            vop(lambda e, sbf=sbf, h=h: e.tensor_copy(
                                    sbf, S_sb[:, h * 128:(h + 1) * 128]),
                                [skey], [sbfk])
                            ek, ekk = bslot()
                            pb3, pbk3 = pslot()
                            top(lambda e, pb3=pb3, zzlo=zzlo: e.transpose(
                                    pb3, bigC[:, zzlo + 128:zzlo + 256], identb[:, :]),
                                [zzk, R("identb", 0, 128)], [pbk3])
                            vop(lambda e, ek=ek, pb3=pb3: e.tensor_copy(ek, pb3),
                                [pbk3], [ekk])
                            top(lambda e, ek=ek, sbf=sbf, fb=fb: e.matmul(
                                    ps[fb][:, 0:128], ek, sbf, start=True, stop=True),
                                [ekk, sbfk], [(f"ps{fb}", 0, 512)])
                            top(lambda e, qap=qap, sbf=sbf, fb=fb: e.matmul(
                                    ps[fb][:, 128:256], qap, sbf, start=True, stop=True),
                                [qkey, sbfk], [(f"ps{fb}", 0, 512)])
                        sT, kT = fslot()
                        if c > 0:
                            vop(lambda e, sT=sT, zzlo=zzlo, fb=fb: e.tensor_sub(
                                    sT, bigC[:, zzlo:zzlo + 128], ps[fb][:, 0:128]),
                                [zzk, (f"ps{fb}", 0, 512)], [kT])
                        else:
                            vop(lambda e, sT=sT, zzlo=zzlo: e.tensor_copy(
                                    sT, bigC[:, zzlo:zzlo + 128]),
                                [zzk], [kT])
                        dl, dlk = bslot()
                        cap_beta = ccol_(0, h)
                        vop(lambda e, dl=dl, sT=sT, cap=cap_beta: e.tensor_scalar_mul(
                                dl, sT, cap),
                            [kT, ccolk(0, h)], [dlk])
                        top(lambda e, dl=dl, h=h, ob=ob: e.matmul(
                                ps[ob][:, 0:128], bigD[:, h * 128:(h + 1) * 128], dl,
                                start=True, stop=True),
                            [btk, dlk], [(f"ps{ob}", 0, 512)])
                        sO, kO = fslot()
                        if c > 0:
                            cap_gam2 = ccol_(96, h)
                            vop(lambda e, sO=sO, cap=cap_gam2, fb=fb: e.tensor_scalar_mul(
                                    sO, ps[fb][:, 128:256], cap),
                                [(f"ps{fb}", 0, 512), ccolk(96, h)], [kO])
                            vop(lambda e, sO=sO, ob=ob: e.tensor_add(
                                    sO, sO, ps[ob][:, 0:128]),
                                [(f"ps{ob}", 0, 512), kO], [kO])
                        else:
                            vop(lambda e, sO=sO, ob=ob: e.tensor_copy(
                                    sO, ps[ob][:, 0:128]),
                                [(f"ps{ob}", 0, 512)], [kO])
                        sSq, kSq = fslot()
                        sop(lambda e, sO=sO, sSq=sSq: e.activation(
                                sSq, sO, AF.Square, accum_out=prob[:, 32:33]),
                            [kO], [kSq, R("prob", 32, 33)])
                        sop(lambda e: e.activation(prob[:, 33:34], prob[:, 32:33],
                                                   AF.Ln, scale=1.0 / DV,
                                                   bias=cols[:, CI["rmseps"]:CI["rmseps"] + 1]),
                            [R("prob", 32, 33), R("cols", CI["rmseps"], CI["rmseps"] + 1)],
                            [R("prob", 33, 34)])
                        sop(lambda e: e.activation(prob[:, 34:35], prob[:, 33:34],
                                                   AF.Exp, scale=-0.5),
                            [R("prob", 33, 34)], [R("prob", 34, 35)])
                        onb, onbk = bslot()
                        vop(lambda e, onb=onb, sO=sO: e.tensor_scalar_mul(
                                onb, sO, prob[:, 34:35]),
                            [kO, R("prob", 34, 35)], [onbk])
                        pb4, pbk4 = pslot()
                        top(lambda e, onb=onb, pb4=pb4: e.transpose(pb4, onb, identb[:, :]),
                            [onbk, R("identb", 0, 128)], [pbk4])
                        gmap = (16 + h) if h < 16 else (h - 16)
                        glo = gmap * 512 + c * 128
                        vop(lambda e, pb4=pb4, glo=glo, zr=zr: e.tensor_mul(
                                bigA[:, glo:glo + 128], zr, pb4),
                            [pbk4, zrk], [(A, glo, glo + 128)])
                        kb, kbk = bslot()
                        cap_dec = ccol_(128, h)
                        vop(lambda e, kb=kb, kh=kh, cap=cap_dec: e.tensor_scalar_mul(
                                kb, bigD[:, 4096 + kh * 128: 4096 + (kh + 1) * 128],
                                cap),
                            [ktokk, ccolk(128, h)], [kbk])
                        top(lambda e, kb=kb, dl=dl, ob=ob: e.matmul(
                                ps[ob][:, 128:256], kb, dl, start=True, stop=True),
                            [kbk, dlk], [(f"ps{ob}", 0, 512)])
                        if b == 1 and sc == 3 and c == 3 and h == 0:
                            vop(lambda e: e.tensor_copy(bigE[:, 2560:2688],
                                                        S_sb[:, 0:128]),
                                [skey], [(E, 2560, 2688)])
                            dma("y", dbg[:, 0:128], bigE[:, 2560:2688],
                                [(E, 2560, 2688)], [("DBG", 0, 128)])
                            vop(lambda e, sT=sT: e.tensor_copy(bigE[:, 2560:2688], sT),
                                [kT], [(E, 2560, 2688)])
                            dma("y", dbg[:, 128:256], bigE[:, 2560:2688],
                                [(E, 2560, 2688)], [("DBG", 128, 256)])
                            vop(lambda e, dl=dl: e.tensor_copy(bigE[:, 2560:2688], dl),
                                [dlk], [(E, 2560, 2688)])
                            dma("y", dbg[:, 256:384], bigE[:, 2560:2688],
                                [(E, 2560, 2688)], [("DBG", 256, 384)])
                            vop(lambda e, fb=fb: e.tensor_copy(bigE[:, 2560:2688],
                                                               ps[fb][:, 128:256]),
                                [(f"ps{fb}", 0, 512)], [(E, 2560, 2688)])
                            dma("y", dbg[:, 384:512], bigE[:, 2560:2688],
                                [(E, 2560, 2688)], [("DBG", 384, 512)])
                            vop(lambda e, sO=sO: e.tensor_copy(bigE[:, 2560:2688], sO),
                                [kO], [(E, 2560, 2688)])
                            dma("y", dbg[:, 512:640], bigE[:, 2560:2688],
                                [(E, 2560, 2688)], [("DBG", 512, 640)])
                            vop(lambda e, onb=onb: e.tensor_copy(bigE[:, 2560:2688], onb),
                                [onbk], [(E, 2560, 2688)])
                            dma("y", dbg[:, 640:768], bigE[:, 2560:2688],
                                [(E, 2560, 2688)], [("DBG", 640, 768)])
                            vop(lambda e, zzlo=zzlo: e.tensor_copy(
                                    bigE[:, 2560:2816], bigC[:, zzlo:zzlo + 256]),
                                [zzk], [(E, 2560, 2816)])
                            dma("y", dbg[:, 768:1024], bigE[:, 2560:2816],
                                [(E, 2560, 2816)], [("DBG", 768, 1024)])
                        if c == 0:
                            vop(lambda e, h=h, ob=ob: e.tensor_copy(
                                    S_sb[:, h * 128:(h + 1) * 128], ps[ob][:, 128:256]),
                                [(f"ps{ob}", 0, 512)], [skey])
                        else:
                            cap_gc = ccol_(160, h)
                            vop(lambda e, h=h, ob=ob, cap=cap_gc: e.scalar_tensor_tensor(
                                    S_sb[:, h * 128:(h + 1) * 128],
                                    S_sb[:, h * 128:(h + 1) * 128], cap,
                                    ps[ob][:, 128:256], ALU.mult, ALU.add),
                                [skey, ccolk(160, h), (f"ps{ob}", 0, 512)], [skey])
                for pp2 in range(16):
                    slot = pp2 % 2
                    wkey = (Bk, slot * 4096, slot * 4096 + 4096)
                    load_panel("L0_attn_W_out", pp2, bigB, slot * 4096, wkey)
                    j = pp2
                    bank = j % 2
                    for k in range(32):
                        gk = (16 + k) if k < 16 else (k - 16)
                        top(lambda e, slot=slot, k=k, bank=bank, gk=gk: e.matmul(
                                ps[bank][:, 0:512],
                                bigB[:, slot * 4096 + k * 128:
                                     slot * 4096 + k * 128 + 128],
                                bigA[:, gk * 512:(gk + 1) * 512],
                                start=(k == 0), stop=(k == 31)),
                            [wkey, (A, gk * 512, (gk + 1) * 512)],
                            [(f"ps{bank}", 0, 512)] if k == 0 else [],
                            group_n=32 if k == 0 else 1)
                    xk = ("xfst", 0, 512)
                    dma("y", xfst[:, 0:512],
                        src_hbm[j * 128:(j + 1) * 128, t0:t0 + 512],
                        [xiv(src_key, j, t0)], [xk])
                    ykey = ("xfst", 512, 1024)
                    vop(lambda e, bank=bank: e.tensor_add(
                            xfst[:, 512:1024], ps[bank][:, 0:512], xfst[:, 0:512]),
                        [(f"ps{bank}", 0, 512), xk], [ykey])
                    dma("y", dst_hbm[j * 128:(j + 1) * 128, t0:t0 + 512],
                        xfst[:, 512:1024], [ykey], [xiv(dst_key, j, t0)])

    setup_constants()
    setup_vectors()
    setup_cast()
    B.barrier()
    seqs = []
    if "ffn1" in parts:
        seqs.append(("ffn1", x_in, "x_in"))
    chain = {"ffn1": (xA, "xA"), "attn": (xB, "xB")}
    srcs = {"ffn1": (x_in, "x_in"), "attn": (xA, "xA"), "ffn2": (xB, "xB")}
    plist = [p for p in ("ffn1", "attn", "ffn2") if p in parts]
    for i, p in enumerate(plist):
        src_hbm, src_key = srcs[p]
        if i + 1 < len(plist):
            dst_hbm, dst_key = srcs[plist[i + 1]]
        else:
            dst_hbm, dst_key = out, "out"
        if p in ("ffn1", "ffn2"):
            ffn(src_hbm, src_key, dst_hbm, dst_key)
        else:
            attn(src_hbm, src_key, dst_hbm, dst_key)
        B.barrier()
    for s_ in list(B.sems):
        if s_ == "y":
            continue
        v = B.cnt[s_]
        if v > 0 and B.waited["y"].get(s_, 0) < v:
            B.waited["y"][s_] = v
            B.stream["y"].append(([(s_, v)], None, None, 0))
    B.finalize()
    mybir.codegen_inst_isa_subclasses(nc)
    return nc


def attn(*a, **k):
    raise NotImplementedError("attn emitted later")


def kernel(**inputs):
    x = inputs["L0_attn_x"]
    nc = build_program()
    in_maps = []
    for c in range(NCORE):
        m = {}
        xc = x[2 * c: 2 * c + 2]
        m["L0_attn_x"] = np.ascontiguousarray(xc.reshape(NTOK, D).T)
        for n in WMATS:
            m[n] = np.ascontiguousarray(inputs[n])
        for n in VECS:
            m[n] = np.ascontiguousarray(inputs[n]).reshape(-1)
        in_maps.append(m)
    res = run_bass_kernel_spmd(nc, in_maps, core_ids=list(range(NCORE)))
    outs = []
    for c in range(NCORE):
        o = res.results[c]["OUT"]
        outs.append(o.T.reshape(2, SEQ, D))
    return np.concatenate(outs, axis=0)


# version 06
# speedup vs baseline: 1.1469x; 1.0466x over previous
import contextlib

import numpy as np

import concourse.bass as bass
import concourse.mybir as mybir
from concourse.bass_utils import run_bass_kernel_spmd

F32 = mybir.dt.float32
BF16 = mybir.dt.bfloat16
AF = mybir.ActivationFunctionType
ALU = mybir.AluOpType
AX = mybir.AxisListType.X

D = 2048
NCORE = 8
NBATCH = 2
SEQ = 2048
NTOK = NBATCH * SEQ
L = 1024
HID = [1024, 2048, 3072, 4096, 1024, 2048, 3072, 4096]
COST_LAMBDA = 1e-7
LNEPS = 1e-5
NKH, NVH, DK, DV = 16, 32, 128, 128
KCONV = 4
RMSEPS = 1e-6
KEY_DIM = NKH * DK
VALUE_DIM = NVH * DV
CONV_DIM = 2 * KEY_DIM + VALUE_DIM
SC, CH = 512, 128
NCH = SC // CH
FB = 512
NEG_BIG = -1e30
PANEL = 256


def panelw(name):
    if name in ("L0_attn_W_qkvz", "L0_ffn_up_w", "L0_ffn_core_w"):
        return 128
    K = WMATS[name][0]
    return 128 if K > 2048 else 256


def qkvz_perm(bw):
    kh, r = divmod(bw, 6)
    if r == 0:
        return kh
    if r == 1:
        return 16 + kh
    if r in (2, 3):
        return 32 + 2 * kh + (r - 2)
    return 64 + 2 * kh + (r - 4)

WMATS = {
    "L0_attn_W_qkvz": (D, 12288),
    "L0_attn_W_ba": (D, 64),
    "L0_attn_W_out": (VALUE_DIM, D),
    "L0_ffn_down_w": (D, L),
    "L0_ffn_router_w": (D, 8),
    "L0_ffn_up_w": (L, D),
    "L0_ffn_core_w": (D, D),
    "L0_ffn_shared_w1": (L, L),
    "L0_ffn_shared_w2": (L, L),
}
for _e, _h in enumerate(HID):
    WMATS[f"L0_ffn_e{_e}_w1"] = (L, _h)
    WMATS[f"L0_ffn_e{_e}_w2"] = (_h, L)

VECS = {
    "L0_attn_conv_w": CONV_DIM * KCONV,
    "L0_attn_dt_bias": NVH,
    "L0_attn_A_log": NVH,
    "L0_attn_norm_w": DV,
    "L0_ffn_down_b": L,
    "L0_ffn_up_b": D,
    "L0_ffn_router_b": 8,
    "L0_ffn_shared_b1": L,
    "L0_ffn_shared_b2": L,
    "L0_ffn_core_b": D,
}
for _e, _h in enumerate(HID):
    VECS[f"L0_ffn_e{_e}_b1"] = _h
    VECS[f"L0_ffn_e{_e}_b2"] = L


def ceil_div(a, b):
    return -(-a // b)


class Builder:

    ENGS = ("t", "v", "s", "g", "y")

    NDMASEM = 8

    def __init__(self, nc):
        self.nc = nc
        self.es = contextlib.ExitStack()
        self.stream = {e: [] for e in self.ENGS}
        self.sems = {}
        names = ["t", "v", "s", "g", "y"]
        for q in ("dy", "ds", "dg"):
            for i in range(self.NDMASEM):
                names.append(f"{q}{i}")
        for name in names:
            self.sems[name] = self.es.enter_context(nc.semaphore("sem_" + name))
        self.cnt = {name: 0 for name in self.sems}
        self.rr = {"dy": 0, "ds": 0, "dg": 0}
        self.waited = {e: {} for e in self.ENGS}
        self.track = {}
        self.same_engine_sync = True

    def _acc(self, key, lo, hi, pt, write):
        ivs = self.track.setdefault(key, [])
        deps = []
        if write:
            keep = []
            nlo, nhi = lo, hi
            for iv in ivs:
                if iv[1] <= lo or iv[0] >= hi:
                    keep.append(iv)
                else:
                    if iv[2] is not None:
                        deps.append(iv[2])
                    deps.extend(iv[3])
                    nlo = min(nlo, iv[0])
                    nhi = max(nhi, iv[1])
            keep.append([nlo, nhi, pt, []])
            self.track[key] = keep
        else:
            hit = False
            for iv in ivs:
                if iv[1] <= lo or iv[0] >= hi:
                    continue
                hit = True
                if iv[2] is not None:
                    deps.append(iv[2])
                iv[3].append(pt)
            if not hit:
                ivs.append([lo, hi, None, [pt]])
        return deps

    def emit(self, eng, fn, reads=(), writes=(), dma=False, group_n=1):
        if dma:
            q = "d" + eng
            sem = f"{q}{self.rr[q]}"
            self.rr[q] = (self.rr[q] + 1) % self.NDMASEM
            inc = 16
        else:
            sem = eng
            inc = 1
        pt = (sem, self.cnt[sem] + inc)
        wpt = (sem, self.cnt[sem] + inc * group_n)
        deps = []
        if dma and self.cnt[sem] > 0:
            deps.append((sem, self.cnt[sem]))
        for r in reads:
            deps += self._acc(r[0], r[1], r[2], pt, False)
        for w in writes:
            deps += self._acc(w[0], w[1], w[2], wpt, True)
        cnt_before = pt[1] - inc
        self.cnt[sem] = pt[1]
        wd = self.waited[eng]
        bys = {}
        for (s, v) in deps:
            if s == sem and v > cnt_before:
                continue
            if s == eng and not self.same_engine_sync:
                continue
            if wd.get(s, 0) >= v:
                continue
            bys[s] = max(bys.get(s, 0), v)
        for s, v in bys.items():
            wd[s] = v
        self.stream[eng].append((list(bys.items()), fn, sem, inc))
        return pt

    def barrier(self, engs=None):
        engs = engs or self.ENGS
        tgt = dict(self.cnt)
        for e in engs:
            waits = []
            for s, v in tgt.items():
                if v > 0 and self.waited[e].get(s, 0) < v:
                    self.waited[e][s] = v
                    waits.append((s, v))
            if waits:
                self.stream[e].append((waits, None, None, 0))

    def finalize(self):
        sems = self.sems
        with self.nc.Block() as block:
            def mk(eng_key):
                def body(e):
                    for waits, fn, sem, inc in self.stream[eng_key]:
                        for s, v in waits:
                            e.wait_ge(sems[s], v)
                        if fn is not None:
                            fn(e).then_inc(sems[sem], inc)
                return body
            block.tensor(mk("t"))
            block.vector(mk("v"))
            block.scalar(mk("s"))
            block.gpsimd(mk("g"))
            block.sync(mk("y"))


def build_program(parts=("ffn1", "attn", "ffn2")):
    nc = bass.Bass("TRN2", target_bir_lowering=False)
    B = Builder(nc)
    es = B.es

    x_in = nc.dram_tensor("L0_attn_x", [D, NTOK], F32, kind="ExternalInput").ap()
    out = nc.dram_tensor("OUT", [D, NTOK], F32, kind="ExternalOutput").ap()
    wmat = {n: nc.dram_tensor(n, list(s), F32, kind="ExternalInput").ap()
            for n, s in WMATS.items()}
    wvec = {n: nc.dram_tensor(n, [s], F32, kind="ExternalInput").ap()
            for n, s in VECS.items()}
    wscr = {n: nc.dram_tensor("scr_" + n, [K * N], BF16, kind="Internal").ap()
            for n, (K, N) in WMATS.items()}
    xA = nc.dram_tensor("xA", [D, NTOK], F32, kind="Internal").ap()
    xB = nc.dram_tensor("xB", [D, NTOK], F32, kind="Internal").ap()
    rowscr = nc.dram_tensor("rowscr", [16384], F32, kind="Internal").ap()
    zscr = nc.dram_tensor("zscr", [NVH * 128 * SC], BF16, kind="Internal").ap()
    dbg = nc.dram_tensor("DBG", [128, 2048], F32, kind="Internal").ap()

    def sb(name, p, w, dt):
        return es.enter_context(nc.sbuf_tensor(name, [p, w], dt))

    bigA = sb("bigA", 128, 38912, BF16)
    bigB = sb("bigB", 128, 8192, BF16)
    bigC = sb("bigC", 128, 8192, BF16)
    bigD = sb("bigD", 128, 8192, BF16)
    bigE = sb("bigE", 128, 4096, F32)
    S_sb = sb("S_sb", 128, 4096, F32)
    xfst = sb("xfst", 128, 1024, F32)
    hist = sb("hist", 128, 192, F32)
    cols = sb("cols", 128, 384, F32)
    ones2d = sb("ones2d", 128, 128, F32)
    gb1c = sb("gb1c", 128, 176, F32)
    gb1cb = sb("gb1cb", 128, 176, BF16)
    gb1c_raw = sb("gb1c_raw", 128, 176, F32)
    identf = sb("identf", 128, 128, F32)
    identb = sb("identb", 128, 128, BF16)
    mask_t = sb("mask_t", 128, 128, BF16)
    mask_i = sb("mask_i", 128, 128, BF16)
    mask_sn = sb("mask_sn", 128, 128, BF16)
    onesf = sb("onesf", 1, 128, F32)
    onesb = sb("onesb", 1, 512, BF16)
    onescb = sb("onescb", 128, 1, BF16)
    selt = sb("selt", 8, 1024, BF16)
    rowA = sb("rowA", 1, 2048, F32)
    rowB = sb("rowB", 1, 1024, F32)
    rowsb = sb("rowsb", 1, 3328, BF16)
    D8pm = sb("D8pm", 8, 1024, BF16)
    w8f = sb("w8f", 128, 64, F32)
    w8tb = sb("w8tb", 8, 512, BF16)
    wbt = sb("wbt", 128, 512, BF16)
    stat = rowA
    bg = sb("bg", 128, 512, F32)
    bg2 = sb("bg2", 128, 256, F32)
    tokc = sb("tokc", 128, 1024, F32)
    prob = sb("prob", 128, 64, F32)
    iot = sb("iot", 128, 128, F32)

    ps = [es.enter_context(nc.psum_tensor(f"ps{i}", [128, 512], F32)) for i in range(6)]
    pbf = [es.enter_context(nc.psum_tensor(f"pbf{i}", [128, 1024], BF16)) for i in range(2)]

    def R(t, lo, hi):
        return (t, lo, hi)

    A, Bk, C, Dk, E = "bigA", "bigB", "bigC", "bigD", "bigE"
    PS = [f"ps{i}" for i in range(6)]
    PB = ["pbf0", "pbf1"]

    def dma(eng, out_ap, in_ap, reads, writes):
        return B.emit(eng, lambda e: e.dma_start(out_ap, in_ap), reads=reads,
                      writes=writes, dma=True)

    def vop(fn, reads, writes):
        return B.emit("v", fn, reads=reads, writes=writes)

    def sop(fn, reads, writes):
        return B.emit("s", fn, reads=reads, writes=writes)

    def gop(fn, reads, writes):
        return B.emit("g", fn, reads=reads, writes=writes)

    def top(fn, reads, writes, group_n=1):
        return B.emit("t", fn, reads=reads, writes=writes, group_n=group_n)

    def xiv(xkey, j, t0, w=512):
        return (xkey, j * NTOK + t0, j * NTOK + t0 + w)

    def setup_constants():
        gop(lambda e: e.iota(iot[:, :], [[1, 128]], channel_multiplier=-1,
                             allow_small_or_imprecise_dtypes=True),
            [], [R("iot", 0, 128)])
        gop(lambda e: e.memset(onesf[:, :], 1.0), [], [R("onesf", 0, 128)])
        gop(lambda e: e.memset(onesb[:, :], 1.0), [], [R("onesb", 0, 512)])
        gop(lambda e: e.memset(onescb[:, :], 1.0), [], [R("onescb", 0, 1)])
        gop(lambda e: e.memset(ones2d[:, :], 1.0), [], [R("ones2d", 0, 128)])
        vop(lambda e: e.tensor_single_scalar(identf[:, :], iot[:, :], 0.0, ALU.is_equal),
            [R("iot", 0, 128)], [R("identf", 0, 128)])
        vop(lambda e: e.tensor_copy(identb[:, :], identf[:, :]),
            [R("identf", 0, 128)], [R("identb", 0, 128)])
        vop(lambda e: e.tensor_single_scalar(mask_t[:, :], iot[:, :], 0.0, ALU.is_gt),
            [R("iot", 0, 128)], [R("mask_t", 0, 128)])
        vop(lambda e: e.tensor_single_scalar(mask_i[:, :], iot[:, :], 0.0, ALU.is_ge),
            [R("iot", 0, 128)], [R("mask_i", 0, 128)])
        vop(lambda e: e.tensor_scalar(mask_sn[:, :], iot[:, :], 0.0, -1.0,
                                      ALU.is_lt, ALU.mult),
            [R("iot", 0, 128)], [R("mask_sn", 0, 128)])
        gop(lambda e: e.iota(bigE[0:8, 0:128], [[0, 128]], channel_multiplier=1,
                             allow_small_or_imprecise_dtypes=True),
            [], [R(E, 0, 128)])
        for ee in range(8):
            vop(lambda e, ee=ee: e.tensor_single_scalar(
                    selt[0:8, ee * 128:(ee + 1) * 128], bigE[0:8, 0:128],
                    float(ee), ALU.is_equal),
                [R(E, 0, 128)], [R("selt", ee * 128, (ee + 1) * 128)])

    _colptr = [0]
    CI = {}

    def colify(row_t, row_key, off, n, dst_t, dst_name, dst_off, stride=1):
        pbk = ps[5]
        if stride == 1:
            src_ap = row_t[0:1, off:off + n]
        else:
            src_ap = bass.AP(row_t, off, [[row_t.shape[1], 1], [stride, n]])
        top(lambda e: e.matmul(pbk[:, 0:n], onesf[0:1, 0:128], src_ap,
                               start=True, stop=True),
            [R(row_key, 0, row_t.shape[1]), R("onesf", 0, 128)], [R("ps5", 0, 512)])
        vop(lambda e: e.tensor_mul(prob[:, 0:n] if n <= 64 else bigE[:, 0:n],
                                   pbk[:, 0:n], identf[:, 0:n]),
            [R("ps5", 0, 512), R("identf", 0, 128)],
            [R("prob", 0, 64) if n <= 64 else R(E, 0, 128)])
        srct = prob if n <= 64 else bigE
        srck = R("prob", 0, 64) if n <= 64 else R(E, 0, 128)
        vop(lambda e: e.reduce_sum(dst_t[:, dst_off:dst_off + 1], srct[:, 0:n], axis=AX),
            [srck], [(dst_name, dst_off, dst_off + 1)])

    def ccol(row_t, row_key, off, n, stride=1):
        idx = _colptr[0]
        _colptr[0] += 1
        colify(row_t, row_key, off, n, cols, "cols", idx, stride)
        return idx

    def load_row(dst_off, name, lo, hi, row_tensor, row_key):
        dma("y", row_tensor[0:1, dst_off:dst_off + (hi - lo)], wvec[name][lo:hi],
            [], [(row_key, dst_off, dst_off + (hi - lo))])

    def setup_vectors():
        load_row(0, "L0_ffn_down_b", 0, L, rowB, "rowB")
        CI["lneps"] = _colptr[0]; _colptr[0] += 1
        CI["rmseps"] = _colptr[0]; _colptr[0] += 1
        CI["one"] = _colptr[0]; _colptr[0] += 1
        vop(lambda e: e.memset(cols[:, CI["lneps"]:CI["lneps"] + 1], LNEPS),
            [], [R("cols", CI["lneps"], CI["lneps"] + 1)])
        vop(lambda e: e.memset(cols[:, CI["rmseps"]:CI["rmseps"] + 1], RMSEPS),
            [], [R("cols", CI["rmseps"], CI["rmseps"] + 1)])
        vop(lambda e: e.memset(cols[:, CI["one"]:CI["one"] + 1], 1.0),
            [], [R("cols", CI["one"], CI["one"] + 1)])
        CI["down_b"] = _colptr[0]
        for j in range(8):
            ccol(rowB, "rowB", j * 128, 128)
        for half in range(2):
            load_row(0, "L0_ffn_up_b", half * 1024, (half + 1) * 1024, rowB, "rowB")
            load_row(0, "L0_ffn_core_b", half * 1024, (half + 1) * 1024, rowA, "rowA")
            vop(lambda e, half=half: e.tensor_add(
                    rowsb[0:1, half * 1024:(half + 1) * 1024],
                    rowB[0:1, 0:1024], rowA[0:1, 0:1024]),
                [R("rowB", 0, 1024), R("rowA", 0, 1024)],
                [R("rowsb", half * 1024, (half + 1) * 1024)])
        load_row(0, "L0_attn_dt_bias", 0, NVH, rowB, "rowB")
        load_row(32, "L0_attn_A_log", 0, NVH, rowB, "rowB")
        load_row(64, "L0_attn_norm_w", 0, DV, rowB, "rowB")
        CI["normw"] = ccol(rowB, "rowB", 64, 128)
        CI["dtbB"], CI["neB"] = 320, 352
        top(lambda e: e.matmul(ps[5][:, 0:32], onesf[0:1, 0:128], rowB[0:1, 0:32],
                               start=True, stop=True),
            [R("rowB", 0, 32), R("onesf", 0, 128)], [R("ps5", 0, 512)])
        vop(lambda e: e.tensor_copy(cols[:, 320:352], ps[5][:, 0:32]),
            [R("ps5", 0, 512)], [R("cols", 320, 352)])
        sop(lambda e: e.activation(rowB[0:1, 64:96], rowB[0:1, 32:64], AF.Exp),
            [R("rowB", 32, 64)], [R("rowB", 64, 96)])
        vop(lambda e: e.tensor_scalar_mul(rowB[0:1, 96:128], rowB[0:1, 64:96], -1.0),
            [R("rowB", 64, 96)], [R("rowB", 96, 128)])
        top(lambda e: e.matmul(ps[5][:, 0:32], onesf[0:1, 0:128], rowB[0:1, 96:128],
                               start=True, stop=True),
            [R("rowB", 96, 128), R("onesf", 0, 128)], [R("ps5", 0, 512)])
        vop(lambda e: e.tensor_copy(cols[:, 352:384], ps[5][:, 0:32]),
            [R("ps5", 0, 512)], [R("cols", 352, 384)])
        CI["b1"] = {}
        ptr = 0
        for ee in range(9):
            h = HID[ee] if ee < 8 else L
            nm = f"L0_ffn_e{ee}_b1" if ee < 8 else "L0_ffn_shared_b1"
            CI["b1"][ee] = ptr
            for half in range(ceil_div(h, 1024)):
                lo = half * 1024
                w = min(1024, h - lo)
                load_row(0, nm, lo, lo + w, rowB, "rowB")
                for j in range(w // 128):
                    colify(rowB, "rowB", j * 128, 128, gb1c_raw, "gb1c_raw", ptr)
                    ptr += 1
        CI["sb1"] = CI["b1"][8]
        sop(lambda e, ptr=ptr: e.activation(gb1c[:, 0:ptr], gb1c_raw[:, 0:ptr], AF.Gelu),
            [R("gb1c_raw", 0, ptr)], [R("gb1c", 0, ptr)])
        vop(lambda e, ptr=ptr: e.tensor_copy(gb1cb[:, 0:ptr], gb1c[:, 0:ptr]),
            [R("gb1c", 0, ptr)], [R("gb1cb", 0, ptr)])
        CI["conv"] = {}
        for piece in range(32):
            load_row(0, "L0_attn_conv_w", piece * 1024, (piece + 1) * 1024, rowB, "rowB")
            for tt in range(2):
                for j in range(4):
                    tile = piece * 2 + tt
                    CI["conv"][(tile, j)] = _colptr[0]
                    ccol(rowB, "rowB", tt * 512 + j, 128, stride=4)
        load_row(0, "L0_ffn_router_b", 0, 8, rowB, "rowB")
        for ee in range(8):
            cst = COST_LAMBDA * (L * HID[ee] + HID[ee] * L)
            vop(lambda e, ee=ee, cst=cst: e.memset(rowB[0:1, 8 + ee:9 + ee], float(cst)),
                [], [R("rowB", 8, 16)])
        vop(lambda e: e.tensor_sub(rowB[0:1, 16:24], rowB[0:1, 0:8], rowB[0:1, 8:16]),
            [R("rowB", 0, 16)], [R("rowB", 16, 24)])
        vop(lambda e: e.tensor_copy(rowsb[0:1, 3072:3080], rowB[0:1, 16:24]),
            [R("rowB", 16, 24)], [R("rowsb", 3072, 3080)])

    CASTOFF = 24576

    def cast_weight(name, post=None, scale_col=None, scale_imm=None):
        K, N = WMATS[name]
        PW = panelw(name)
        src, dst = wmat[name], wscr[name]
        sname = "scr_" + name
        pieces = ceil_div(N, 1024)
        for ks in range(K // 128):
            for pi in range(pieces):
                p0 = pi * 1024
                w = min(1024, N - p0)
                slot = (ks * pieces + pi) % 2
                cb = bigA[:, CASTOFF + slot * 1024: CASTOFF + slot * 1024 + w]
                ckey = (A, CASTOFF + slot * 1024, CASTOFF + slot * 1024 + w)
                dma("g", cb, src[ks * 128:(ks + 1) * 128, p0:p0 + w], [], [ckey])
                if scale_col is not None:
                    vop(lambda e, cb=cb, sc=scale_col: e.tensor_scalar_mul(
                            cb, cb, cols[:, sc:sc + 1]),
                        [ckey, R("cols", scale_col, scale_col + 1)], [ckey])
                if scale_imm is not None:
                    vop(lambda e, cb=cb, si=scale_imm: e.tensor_scalar_mul(cb, cb, si),
                        [ckey], [ckey])
                if post is not None:
                    post(cb, ckey, ks, p0, w)
                nk = K // 128
                for pp in range(p0 // PW, ceil_div(p0 + w, PW)):
                    pw = min(PW, N - pp * PW)
                    dpp = qkvz_perm(pp) if name == "L0_attn_W_qkvz" else pp
                    base = dpp * PW * K + ks * pw
                    dma("y", bass.AP(dst.tensor, base, [[nk * pw, 128], [1, pw]]),
                        cb[:, pp * PW - p0: pp * PW - p0 + pw],
                        [ckey], [(sname, dpp * PW * K, (dpp + 1) * PW * K)])

    def load_panel(name, pp, dest_t, dest_off, dkey, eng="s"):
        K, N = WMATS[name]
        PW = panelw(name)
        pw = min(PW, N - pp * PW)
        base = pp * PW * K
        nk = K // 128
        dma(eng, dest_t[:, dest_off: dest_off + nk * pw],
            wscr[name][base: base + K * pw].rearrange("(p kj) -> p kj", p=128),
            [("scr_" + name, base, base + K * pw)], [dkey])
        return pw

    def make_w2_post(ee):
        h = HID[ee]
        base = CI["b1"][ee]
        nk = h // 128

        def post(cb, ckey, ks, p0, w):
            for half in range(2):
                bank = 3 + half
                top(lambda e, cb=cb, half=half, bank=bank, ks=ks: e.matmul(
                        ps[bank][0:1, 0:512], gb1cb[:, base + ks: base + ks + 1],
                        cb[:, half * 512: half * 512 + 512],
                        start=(ks == 0), stop=(ks == nk - 1)),
                    [ckey, R("gb1cb", base + ks, base + ks + 1)],
                    [(f"ps{bank}", 0, 512)] if ks == 0 else [],
                    group_n=(2 * nk - 1) if ks == 0 else 1)
        return post

    def setup_cast():
        cast_weight("L0_ffn_down_w")
        cast_weight("L0_ffn_router_w")
        for ee in range(8):
            cast_weight(f"L0_ffn_e{ee}_w1")
            load_row(0, f"L0_ffn_e{ee}_b2", 0, L, rowB, "rowB")
            cast_weight(f"L0_ffn_e{ee}_w2", post=make_w2_post(ee))
            for half in range(2):
                bank = 3 + half
                lo = half * 512
                co = 1024 + half * 512
                if ee == 0:
                    vop(lambda e, bank=bank, co=co, lo=lo: e.tensor_add(
                            rowA[0:1, co:co + 512], rowB[0:1, lo:lo + 512],
                            ps[bank][0:1, 0:512]),
                        [(f"ps{bank}", 0, 512), R("rowB", lo, lo + 512)],
                        [R("rowA", co, co + 512)])
                else:
                    vop(lambda e, bank=bank, co=co: e.tensor_add(
                            rowA[0:1, co:co + 512], rowA[0:1, co:co + 512],
                            ps[bank][0:1, 0:512]),
                        [(f"ps{bank}", 0, 512), R("rowA", co, co + 512)],
                        [R("rowA", co, co + 512)])
                    vop(lambda e, co=co, lo=lo: e.tensor_add(
                            rowA[0:1, co:co + 512], rowA[0:1, co:co + 512],
                            rowB[0:1, lo:lo + 512]),
                        [R("rowB", lo, lo + 512), R("rowA", co, co + 512)],
                        [R("rowA", co, co + 512)])
                vop(lambda e, bank=bank, lo=lo: e.tensor_scalar_mul(
                        rowB[0:1, lo:lo + 512], ps[bank][0:1, 0:512], -1.0),
                    [(f"ps{bank}", 0, 512)],
                    [R("rowB", lo, lo + 512)])
            dma("y", rowscr[ee * 1024:(ee + 1) * 1024], rowB[0:1, 0:1024],
                [R("rowB", 0, 1024)], [("rowscr", ee * 1024, (ee + 1) * 1024)])
        dma("g", D8pm[:, :], rowscr[0:8192].rearrange("(p f) -> p f", p=8),
            [("rowscr", 0, 8192)], [R("D8pm", 0, 1024)])
        load_row(0, "L0_ffn_shared_b2", 0, L, rowB, "rowB")
        vop(lambda e: e.scalar_tensor_tensor(rowA[0:1, 1024:2048], rowB[0:1, 0:1024],
                                             0.1, rowA[0:1, 1024:2048], ALU.mult, ALU.add),
            [R("rowB", 0, 1024), R("rowA", 1024, 2048)], [R("rowA", 1024, 2048)])
        vop(lambda e: e.tensor_copy(rowsb[0:1, 2048:3072], rowA[0:1, 1024:2048]),
            [R("rowA", 1024, 2048)], [R("rowsb", 2048, 3072)])
        cast_weight("L0_ffn_shared_w1")
        cast_weight("L0_ffn_shared_w2", scale_imm=0.1)
        cast_weight("L0_ffn_up_w")
        cast_weight("L0_ffn_core_w")
        cast_weight("L0_attn_W_qkvz")
        cast_weight("L0_attn_W_ba")
        cast_weight("L0_attn_W_out", scale_col=CI["normw"])

    def layernorm(src_hbm, src_key, t0, dst_t, dst_off, dst_key):
        for k in range(16):
            dma("g", bigD[:, k * 512:(k + 1) * 512],
                src_hbm[k * 128:(k + 1) * 128, t0:t0 + 512],
                [xiv(src_key, k, t0)],
                [(Dk, k * 512, (k + 1) * 512)])
        for k in range(16):
            top(lambda e, k=k: e.matmul(ps[4][0:1, 0:512], onescb[:, 0:1],
                                        bigD[:, k * 512:(k + 1) * 512],
                                        start=(k == 0), stop=(k == 15)),
                [(Dk, k * 512, (k + 1) * 512), R("onescb", 0, 1)],
                [("ps4", 0, 512)] if k == 0 else [], group_n=16 if k == 0 else 1)
        for k in range(16):
            vop(lambda e, k=k: e.tensor_mul(wbt[:, 0:512], bigD[:, k * 512:(k + 1) * 512],
                                            bigD[:, k * 512:(k + 1) * 512]),
                [(Dk, k * 512, (k + 1) * 512)], [R("wbt", 0, 512)])
            top(lambda e, k=k: e.matmul(ps[5][0:1, 0:512], onescb[:, 0:1], wbt[:, 0:512],
                                        start=(k == 0), stop=(k == 15)),
                [R("wbt", 0, 512), R("onescb", 0, 1)],
                [("ps5", 0, 512)] if k == 0 else [], group_n=16 if k == 0 else 1)
        vop(lambda e: e.tensor_scalar_mul(stat[0:1, 0:512], ps[4][0:1, 0:512], 1.0 / D),
            [("ps4", 0, 512)], [R("rowA", 0, 512)])
        vop(lambda e: e.tensor_scalar_mul(stat[0:1, 512:1024], ps[5][0:1, 0:512], 1.0 / D),
            [("ps5", 0, 512)], [R("rowA", 512, 1024)])
        vop(lambda e: e.tensor_mul(stat[0:1, 1024:1536], stat[0:1, 0:512], stat[0:1, 0:512]),
            [R("rowA", 0, 512)], [R("rowA", 1024, 1536)])
        vop(lambda e: e.tensor_sub(stat[0:1, 512:1024], stat[0:1, 512:1024],
                                   stat[0:1, 1024:1536]),
            [R("rowA", 512, 1024), R("rowA", 1024, 1536)], [R("rowA", 512, 1024)])
        sop(lambda e: e.activation(stat[0:1, 512:1024], stat[0:1, 512:1024], AF.Ln,
                                   bias=cols[0:1, CI["lneps"]:CI["lneps"] + 1]),
            [R("rowA", 512, 1024), R("cols", CI["lneps"], CI["lneps"] + 1)],
            [R("rowA", 512, 1024)])
        sop(lambda e: e.activation(stat[0:1, 512:1024], stat[0:1, 512:1024], AF.Exp,
                                   scale=-0.5),
            [R("rowA", 512, 1024)], [R("rowA", 512, 1024)])
        vop(lambda e: e.tensor_mul(stat[0:1, 0:512], stat[0:1, 0:512], stat[0:1, 512:1024]),
            [R("rowA", 0, 512), R("rowA", 512, 1024)], [R("rowA", 0, 512)])
        top(lambda e: e.matmul(ps[4][:, 0:512], onesf[0:1, 0:128], stat[0:1, 512:1024],
                               start=True, stop=True),
            [R("rowA", 512, 1024), R("onesf", 0, 128)], [("ps4", 0, 512)])
        top(lambda e: e.matmul(ps[5][:, 0:512], onesf[0:1, 0:128], stat[0:1, 0:512],
                               start=True, stop=True),
            [R("rowA", 0, 512), R("onesf", 0, 128)], [("ps5", 0, 512)])
        for k in range(16):
            vop(lambda e, k=k: e.tensor_mul(bigE[:, 0:512], bigD[:, k * 512:(k + 1) * 512],
                                            ps[4][:, 0:512]),
                [(Dk, k * 512, (k + 1) * 512), ("ps4", 0, 512)], [(E, 0, 512)])
            vop(lambda e, k=k: e.tensor_sub(dst_t[:, dst_off + k * 512: dst_off + (k + 1) * 512],
                                            bigE[:, 0:512], ps[5][:, 0:512]),
                [(E, 0, 512), ("ps5", 0, 512)],
                [(dst_key, dst_off + k * 512, dst_off + (k + 1) * 512)])

    def ffn(src_hbm, src_key, dst_hbm, dst_key):
        TOFF, GCT, OM = 16384, 26624, 34816
        for blk in range(NTOK // FB):
            t0 = blk * FB
            layernorm(src_hbm, src_key, t0, bigA, TOFF, A)
            for k in range(16):
                sop(lambda e, k=k: e.activation(bigA[:, GCT + k * 512: GCT + (k + 1) * 512],
                                                bigA[:, TOFF + k * 512: TOFF + (k + 1) * 512],
                                                AF.Gelu),
                    [(A, TOFF + k * 512, TOFF + (k + 1) * 512)],
                    [(A, GCT + k * 512, GCT + (k + 1) * 512)])
            for pp in range(4):
                slot = pp % 2
                dkey = (Bk, slot * 4096, slot * 4096 + 4096)
                load_panel("L0_ffn_down_w", pp, bigB, slot * 4096, dkey)
                for half in range(2):
                    j = pp * 2 + half
                    bank = j % 2
                    for k in range(16):
                        top(lambda e, slot=slot, half=half, k=k, bank=bank: e.matmul(
                                ps[bank][:, 0:512],
                                bigB[:, slot * 4096 + k * 256 + half * 128:
                                     slot * 4096 + k * 256 + half * 128 + 128],
                                bigA[:, TOFF + k * 512: TOFF + (k + 1) * 512],
                                start=(k == 0), stop=(k == 15)),
                            [dkey, (A, TOFF + k * 512, TOFF + (k + 1) * 512)],
                            [(f"ps{bank}", 0, 512)] if k == 0 else [],
                            group_n=16 if k == 0 else 1)
                    bcol = CI["down_b"] + j
                    sop(lambda e, j=j, bank=bank, bcol=bcol: e.activation(
                            bigC[:, j * 512:(j + 1) * 512], ps[bank][:, 0:512], AF.Gelu,
                            bias=cols[:, bcol:bcol + 1]),
                        [(f"ps{bank}", 0, 512), R("cols", bcol, bcol + 1)],
                        [(C, j * 512, (j + 1) * 512)])
            rkey = (Bk, 0, 128)
            load_panel("L0_ffn_router_w", 0, bigB, 0, rkey)
            for sub in range(4):
                for k in range(16):
                    top(lambda e, sub=sub, k=k: e.matmul(
                            ps[2][:, 0:8],
                            bigA[:, TOFF + k * 512 + sub * 128: TOFF + k * 512 + sub * 128 + 128],
                            bigB[:, k * 8: k * 8 + 8],
                            start=(k == 0), stop=False),
                        [rkey, (A, TOFF + k * 512 + sub * 128,
                                TOFF + k * 512 + sub * 128 + 128)],
                        [("ps2", 0, 512)] if k == 0 else [],
                        group_n=17 if k == 0 else 1)
                top(lambda e: e.matmul(ps[2][:, 0:8], onesb[0:1, 0:128],
                                       rowsb[0:1, 3072:3080], start=False, stop=True),
                    [R("rowsb", 3072, 3080), R("onesb", 0, 128)], [])
                vop(lambda e: e.reduce_max(prob[:, 8:9], ps[2][:, 0:8], axis=AX, negate=True),
                    [("ps2", 0, 512)], [R("prob", 8, 9)])
                sop(lambda e: e.activation(prob[:, 0:8], ps[2][:, 0:8], AF.Exp,
                                           bias=prob[:, 8:9], accum_out=prob[:, 9:10]),
                    [("ps2", 0, 512), R("prob", 8, 9)],
                    [R("prob", 0, 8), R("prob", 9, 10)])
                vop(lambda e: e.reciprocal(prob[:, 10:11], prob[:, 9:10]),
                    [R("prob", 9, 10)], [R("prob", 10, 11)])
                vop(lambda e: e.tensor_scalar_mul(prob[:, 0:8], prob[:, 0:8], prob[:, 10:11]),
                    [R("prob", 0, 8), R("prob", 10, 11)], [R("prob", 0, 8)])
                vop(lambda e: e.max(prob[:, 16:24], prob[:, 0:8]),
                    [R("prob", 0, 8)], [R("prob", 16, 24)])
                vop(lambda e: e.memset(prob[:, 24:32], NEG_BIG), [], [R("prob", 24, 32)])
                vop(lambda e: e.tensor_copy(prob[:, 24:25], prob[:, 16:17]),
                    [R("prob", 16, 24)], [R("prob", 24, 32)])
                vop(lambda e: e.match_replace(prob[:, 32:40], prob[:, 24:32], prob[:, 0:8],
                                              NEG_BIG),
                    [R("prob", 24, 32), R("prob", 0, 8)], [R("prob", 32, 40)])
                vop(lambda e: e.tensor_tensor(prob[:, 40:48], prob[:, 0:8], prob[:, 32:40],
                                              ALU.not_equal),
                    [R("prob", 0, 8), R("prob", 32, 40)], [R("prob", 40, 48)])
                vop(lambda e: e.tensor_copy(prob[:, 24:25], prob[:, 17:18]),
                    [R("prob", 16, 24)], [R("prob", 24, 32)])
                vop(lambda e: e.match_replace(prob[:, 48:56], prob[:, 24:32], prob[:, 32:40],
                                              NEG_BIG),
                    [R("prob", 24, 32), R("prob", 32, 40)], [R("prob", 48, 56)])
                vop(lambda e: e.tensor_tensor(prob[:, 56:64], prob[:, 32:40], prob[:, 48:56],
                                              ALU.not_equal),
                    [R("prob", 32, 40), R("prob", 48, 56)], [R("prob", 56, 64)])
                vop(lambda e: e.tensor_sub(prob[:, 11:12], prob[:, 17:18], prob[:, 16:17]),
                    [R("prob", 16, 24)], [R("prob", 11, 12)])
                sop(lambda e: e.activation(prob[:, 12:13], prob[:, 11:12], AF.Exp),
                    [R("prob", 11, 12)], [R("prob", 12, 13)])
                vop(lambda e: e.tensor_scalar_add(prob[:, 13:14], prob[:, 12:13], 1.0),
                    [R("prob", 12, 13)], [R("prob", 13, 14)])
                vop(lambda e: e.reciprocal(prob[:, 13:14], prob[:, 13:14]),
                    [R("prob", 13, 14)], [R("prob", 13, 14)])
                vop(lambda e: e.tensor_mul(prob[:, 14:15], prob[:, 12:13], prob[:, 13:14]),
                    [R("prob", 12, 13), R("prob", 13, 14)], [R("prob", 14, 15)])
                vop(lambda e: e.tensor_scalar_mul(prob[:, 40:48], prob[:, 40:48],
                                                  prob[:, 13:14]),
                    [R("prob", 40, 48), R("prob", 13, 14)], [R("prob", 40, 48)])
                vop(lambda e, sub=sub: e.scalar_tensor_tensor(
                        w8f[:, sub * 8:(sub + 1) * 8], prob[:, 56:64], prob[:, 14:15],
                        prob[:, 40:48], ALU.mult, ALU.add),
                    [R("prob", 56, 64), R("prob", 40, 48), R("prob", 14, 15)],
                    [R("w8f", sub * 8, (sub + 1) * 8)])
                top(lambda e, sub=sub: e.transpose(ps[3][0:8, sub * 128:(sub + 1) * 128],
                                                   w8f[:, sub * 8:(sub + 1) * 8],
                                                   identf[:, :]),
                    [R("w8f", sub * 8, (sub + 1) * 8), R("identf", 0, 128)],
                    [("ps3", 0, 512)])
            vop(lambda e: e.tensor_copy(w8tb[0:8, 0:512], ps[3][0:8, 0:512]),
                [("ps3", 0, 512)], [R("w8tb", 0, 512)])
            for ee in range(9):
                h = HID[ee] if ee < 8 else L
                nht = h // 128
                w1n = f"L0_ffn_e{ee}_w1" if ee < 8 else "L0_ffn_shared_w1"
                w2n = f"L0_ffn_e{ee}_w2" if ee < 8 else "L0_ffn_shared_w2"
                if ee < 8:
                    top(lambda e, ee=ee: e.matmul(ps[2][:, 0:512],
                                                  selt[0:8, ee * 128:(ee + 1) * 128],
                                                  w8tb[0:8, 0:512], start=True, stop=True),
                        [R("selt", ee * 128, (ee + 1) * 128), R("w8tb", 0, 512)],
                        [("ps2", 0, 512)])
                    vop(lambda e: e.tensor_copy(wbt[:, 0:512], ps[2][:, 0:512]),
                        [("ps2", 0, 512)], [R("wbt", 0, 512)])
                for pp in range(h // PANEL):
                    slot = pp % 2
                    dkey = (Bk, slot * 4096, slot * 4096 + 2048)
                    load_panel(w1n, pp, bigB, slot * 4096, dkey)
                    for half in range(2):
                        hj = pp * 2 + half
                        bank = hj % 2
                        for k in range(8):
                            top(lambda e, slot=slot, half=half, k=k, bank=bank: e.matmul(
                                    ps[bank][:, 0:512],
                                    bigB[:, slot * 4096 + k * 256 + half * 128:
                                         slot * 4096 + k * 256 + half * 128 + 128],
                                    bigC[:, k * 512:(k + 1) * 512],
                                    start=(k == 0), stop=(k == 7)),
                                [dkey, (C, k * 512, (k + 1) * 512)],
                                [(f"ps{bank}", 0, 512)] if k == 0 else [],
                                group_n=8 if k == 0 else 1)
                        gsl = (A, hj * 512, (hj + 1) * 512)
                        bcol = CI["b1"][ee] + hj
                        sop(lambda e, hj=hj, bank=bank, bcol=bcol: e.activation(
                                bigA[:, hj * 512:(hj + 1) * 512], ps[bank][:, 0:512],
                                AF.Gelu, bias=gb1c_raw[:, bcol:bcol + 1]),
                            [(f"ps{bank}", 0, 512), R("gb1c_raw", bcol, bcol + 1)], [gsl])
                        if ee < 8:
                            vop(lambda e, hj=hj: e.tensor_mul(
                                    bigA[:, hj * 512:(hj + 1) * 512],
                                    bigA[:, hj * 512:(hj + 1) * 512], wbt[:, 0:512]),
                                [gsl, R("wbt", 0, 512)], [gsl])
                w2pw = panelw(w2n)
                for pp in range(L // w2pw):
                    slot = pp % 2
                    pkey = (Bk, slot * 4096, slot * 4096 + nht * w2pw)
                    load_panel(w2n, pp, bigB, slot * 4096, pkey)
                    for half in range(w2pw // 128):
                        j = pp * (w2pw // 128) + half
                        bank = 2 + (j % 2)
                        for k in range(nht):
                            top(lambda e, slot=slot, half=half, k=k, bank=bank,
                                       w2pw=w2pw, nht=nht: e.matmul(
                                    ps[bank][:, 0:512],
                                    bigB[:, slot * 4096 + k * w2pw + half * 128:
                                         slot * 4096 + k * w2pw + half * 128 + 128],
                                    bigA[:, k * 512:(k + 1) * 512],
                                    start=(k == 0), stop=(k == nht - 1)),
                                [pkey, (A, k * 512, (k + 1) * 512)],
                                [(f"ps{bank}", 0, 512)] if k == 0 else [],
                                group_n=nht if k == 0 else 1)
                        if ee == 0:
                            vop(lambda e, j=j, bank=bank: e.tensor_copy(
                                    bigE[:, j * 512:(j + 1) * 512], ps[bank][:, 0:512]),
                                [(f"ps{bank}", 0, 512)], [(E, j * 512, (j + 1) * 512)])
                        else:
                            vop(lambda e, j=j, bank=bank: e.tensor_add(
                                    bigE[:, j * 512:(j + 1) * 512],
                                    bigE[:, j * 512:(j + 1) * 512], ps[bank][:, 0:512]),
                                [(f"ps{bank}", 0, 512), (E, j * 512, (j + 1) * 512)],
                                [(E, j * 512, (j + 1) * 512)])
            for j in range(8):
                top(lambda e, j=j: e.matmul(ps[2][:, 0:512], D8pm[0:8, j * 128:(j + 1) * 128],
                                            w8tb[0:8, 0:512], start=True, stop=False),
                    [R("D8pm", j * 128, (j + 1) * 128), R("w8tb", 0, 512)],
                    [("ps2", 0, 512)], group_n=2)
                top(lambda e, j=j: e.matmul(ps[2][:, 0:512],
                                            rowsb[0:1, 2048 + j * 128: 2048 + (j + 1) * 128],
                                            onesb[0:1, 0:512], start=False, stop=True),
                    [R("rowsb", 2048 + j * 128, 2048 + (j + 1) * 128), R("onesb", 0, 512)],
                    [])
                vop(lambda e, j=j: e.tensor_add(bigA[:, OM + j * 512: OM + (j + 1) * 512],
                                                bigE[:, j * 512:(j + 1) * 512],
                                                ps[2][:, 0:512]),
                    [(E, j * 512, (j + 1) * 512), ("ps2", 0, 512)],
                    [(A, OM + j * 512, OM + (j + 1) * 512)])
            for grp in range(16):
                j = grp
                slot = grp % 2
                bank = grp % 2
                ukey = (Bk, slot * 4096, slot * 4096 + 1024)
                ckey = (Bk, slot * 4096 + 1024, slot * 4096 + 3072)
                load_panel("L0_ffn_up_w", grp, bigB, slot * 4096, ukey)
                load_panel("L0_ffn_core_w", grp, bigB, slot * 4096 + 1024, ckey)
                for k in range(8):
                    top(lambda e, slot=slot, k=k, bank=bank: e.matmul(
                            ps[bank][:, 0:512],
                            bigB[:, slot * 4096 + k * 128: slot * 4096 + k * 128 + 128],
                            bigA[:, OM + k * 512: OM + (k + 1) * 512],
                            start=(k == 0), stop=False),
                        [ukey, (A, OM + k * 512, OM + (k + 1) * 512)],
                        [(f"ps{bank}", 0, 512)] if k == 0 else [],
                        group_n=25 if k == 0 else 1)
                for k in range(16):
                    top(lambda e, slot=slot, k=k, bank=bank: e.matmul(
                            ps[bank][:, 0:512],
                            bigB[:, slot * 4096 + 1024 + k * 128:
                                 slot * 4096 + 1024 + k * 128 + 128],
                            bigA[:, GCT + k * 512: GCT + (k + 1) * 512],
                            start=False, stop=False),
                        [ckey, (A, GCT + k * 512, GCT + (k + 1) * 512)], [])
                top(lambda e, j=j, bank=bank: e.matmul(
                        ps[bank][:, 0:512], rowsb[0:1, j * 128:(j + 1) * 128],
                        onesb[0:1, 0:512], start=False, stop=True),
                    [R("rowsb", j * 128, (j + 1) * 128), R("onesb", 0, 512)], [])
                xk = ("xfst", 0, 512)
                dma("y", xfst[:, 0:512],
                    src_hbm[j * 128:(j + 1) * 128, t0:t0 + 512],
                    [xiv(src_key, j, t0)], [xk])
                ykey = ("xfst", 512, 1024)
                vop(lambda e, bank=bank: e.scalar_tensor_tensor(
                        xfst[:, 512:1024], ps[bank][:, 0:512], 0.5,
                        xfst[:, 0:512], ALU.mult, ALU.add),
                    [(f"ps{bank}", 0, 512), xk], [ykey])
                dma("y", dst_hbm[j * 128:(j + 1) * 128, t0:t0 + 512],
                    xfst[:, 512:1024], [ykey], [xiv(dst_key, j, t0)])


    def attn(src_hbm, src_key, dst_hbm, dst_key):
        fslot_ctr = [0]

        def fslot():
            i = fslot_ctr[0] % 20
            fslot_ctr[0] += 1
            lo = 1536 + i * 128
            return bigE[:, lo:lo + 128], (E, lo, lo + 128)

        bslot_ctr = [0]

        def bslot():
            i = bslot_ctr[0] % 16
            bslot_ctr[0] += 1
            lo = 6144 + i * 128
            return bigD[:, lo:lo + 128], (Dk, lo, lo + 128)

        nslot_ctr = [0]

        def nslot():
            i = nslot_ctr[0] % 32
            nslot_ctr[0] += 1
            lo = 4096 + i * 128
            return bigB[:, lo:lo + 128], (Bk, lo, lo + 128)

        def nslot2():
            if nslot_ctr[0] % 2 == 1:
                nslot_ctr[0] += 1
            i = nslot_ctr[0] % 32
            nslot_ctr[0] += 2
            lo = 4096 + i * 128
            return bigB[:, lo:lo + 256], (Bk, lo, lo + 256)

        pslot_ctr = [0]

        def pslot():
            i = pslot_ctr[0] % 16
            pslot_ctr[0] += 1
            t = pbf[i // 8]
            lo = (i % 8) * 128
            return t[:, lo:lo + 128], (PB[i // 8], 0, 1024)

        def qtile(kh, c):
            lo = kh * 512 + c * 128
            return bigA[:, lo:lo + 128], (A, lo, lo + 128)

        def ktile(kh, c):
            lo = (16 + kh) * 512 + c * 128
            return bigA[:, lo:lo + 128], (A, lo, lo + 128)

        def vtile(h, c):
            lo = (32 + h) * 512 + c * 128
            return bigA[:, lo:lo + 128], (A, lo, lo + 128)

        for b in range(NBATCH):
            vop(lambda e: e.memset(S_sb[:, :], 0.0), [], [R("S_sb", 0, 4096)])
            vop(lambda e: e.memset(hist[:, :], 0.0), [], [R("hist", 0, 192)])
            for sc in range(SEQ // SC):
                t0 = b * SEQ + sc * SC
                layernorm(src_hbm, src_key, t0, bigC, 0, C)
                for pp in range(96):
                    slot = pp % 2
                    dkey = (Bk, slot * 4096, slot * 4096 + 2048)
                    load_panel("L0_attn_W_qkvz", pp, bigB, slot * 4096, dkey)
                    if True:
                        n = pp
                        bank = n % 2
                        for k in range(16):
                            top(lambda e, slot=slot, k=k, bank=bank: e.matmul(
                                    ps[bank][:, 0:512],
                                    bigB[:, slot * 4096 + k * 128:
                                         slot * 4096 + k * 128 + 128],
                                    bigC[:, k * 512:(k + 1) * 512],
                                    start=(k == 0), stop=(k == 15)),
                                [dkey, (C, k * 512, (k + 1) * 512)],
                                [(f"ps{bank}", 0, 512)] if k == 0 else [],
                                group_n=16 if k == 0 else 1)
                        if n < 64:
                            vop(lambda e, n=n: e.tensor_copy(bigE[:, 0:3],
                                                             hist[:, n * 3:(n + 1) * 3]),
                                [R("hist", n * 3, (n + 1) * 3)], [(E, 0, 3)])
                            vop(lambda e, bank=bank: e.tensor_copy(bigE[:, 3:515],
                                                                   ps[bank][:, 0:512]),
                                [(f"ps{bank}", 0, 512)], [(E, 3, 515)])
                            vop(lambda e, n=n: e.tensor_copy(hist[:, n * 3:(n + 1) * 3],
                                                             bigE[:, 512:515]),
                                [(E, 512, 515)], [R("hist", n * 3, (n + 1) * 3)])
                            c0 = CI["conv"][(n, 0)]
                            vop(lambda e, c0=c0: e.tensor_scalar_mul(
                                    bigE[:, 1536:2048], bigE[:, 0:512],
                                    cols[:, c0:c0 + 1]),
                                [(E, 0, 515), R("cols", c0, c0 + 1)], [(E, 1536, 2048)])
                            for j in range(1, 4):
                                cj = CI["conv"][(n, j)]
                                vop(lambda e, j=j, cj=cj: e.scalar_tensor_tensor(
                                        bigE[:, 1536:2048], bigE[:, j:j + 512],
                                        cols[:, cj:cj + 1], bigE[:, 1536:2048],
                                        ALU.mult, ALU.add),
                                    [(E, 0, 515), R("cols", cj, cj + 1), (E, 1536, 2048)],
                                    [(E, 1536, 2048)])
                            sop(lambda e, n=n: e.activation(
                                    bigA[:, n * 512:(n + 1) * 512], bigE[:, 1536:2048],
                                    AF.Silu),
                                [(E, 1536, 2048)], [(A, n * 512, (n + 1) * 512)])
                        else:
                            zt = n - 64
                            sop(lambda e, bank=bank: e.activation(
                                    wbt[:, 0:512], ps[bank][:, 0:512], AF.Silu),
                                [(f"ps{bank}", 0, 512)], [R("wbt", 0, 512)])
                            dma("y", zscr[zt * 65536:(zt + 1) * 65536]
                                .rearrange("(p j) -> p j", p=128), wbt[:, 0:512],
                                [R("wbt", 0, 512)], [("zscr", zt * 65536, (zt + 1) * 65536)])
                bakey = (Bk, 0, 1024)
                load_panel("L0_attn_W_ba", 0, bigB, 0, bakey)
                for k in range(16):
                    top(lambda e, k=k: e.matmul(
                            ps[2][0:64, 0:512], bigB[:, k * 64: k * 64 + 64],
                            bigC[:, k * 512:(k + 1) * 512],
                            start=(k == 0), stop=(k == 15)),
                        [bakey, (C, k * 512, (k + 1) * 512)],
                        [("ps2", 0, 512)] if k == 0 else [], group_n=16 if k == 0 else 1)
                vop(lambda e: e.tensor_copy(bigE[0:64, 0:512], ps[2][0:64, 0:512]),
                    [("ps2", 0, 512)], [(E, 0, 512)])
                for c in range(NCH):
                    tb = c * 256
                    top(lambda e, c=c: e.transpose(ps[3][:, 0:64],
                                                   bigE[0:64, c * 128:(c + 1) * 128],
                                                   identf[0:64, 0:64]),
                        [(E, 0, 512)], [("ps3", 0, 512)])
                    for par in range(2):
                        vop(lambda e, par=par: e.tensor_copy(
                                bass.AP(bigE, 1408 + par, [[4096, 128], [2, 16]]),
                                bass.AP(ps[3], par, [[512, 128], [4, 16]])),
                            [("ps3", 0, 512)], [(E, 1408, 1440)])
                        vop(lambda e, par=par: e.tensor_copy(
                                bass.AP(bigE, 1440 + par, [[4096, 128], [2, 16]]),
                                bass.AP(ps[3], 2 + par, [[512, 128], [4, 16]])),
                            [("ps3", 0, 512)], [(E, 1440, 1472)])
                    sop(lambda e, tb=tb: e.activation(tokc[:, tb:tb + 32],
                                                      bigE[:, 1408:1440], AF.Sigmoid),
                        [(E, 1408, 1440)], [R("tokc", tb, tb + 32)])
                    vop(lambda e, tb=tb: e.tensor_add(bigE[:, 1536:1568],
                                                      bigE[:, 1440:1472], cols[:, 320:352]),
                        [(E, 1440, 1472), R("cols", 320, 352)], [(E, 1536, 1568)])
                    sop(lambda e: e.activation(bigE[:, 1568:1600], bigE[:, 1536:1568],
                                               AF.Abs),
                        [(E, 1536, 1568)], [(E, 1568, 1600)])
                    sop(lambda e: e.activation(bigE[:, 1600:1632], bigE[:, 1568:1600],
                                               AF.Exp, scale=-1.0),
                        [(E, 1568, 1600)], [(E, 1600, 1632)])
                    sop(lambda e: e.activation(bigE[:, 1600:1632], bigE[:, 1600:1632],
                                               AF.Ln, bias=cols[:, CI["one"]:CI["one"] + 1]),
                        [(E, 1600, 1632), R("cols", CI["one"], CI["one"] + 1)],
                        [(E, 1600, 1632)])
                    vop(lambda e: e.tensor_scalar_max(bigE[:, 1536:1568],
                                                      bigE[:, 1536:1568], 0.0),
                        [(E, 1536, 1568)], [(E, 1536, 1568)])
                    vop(lambda e: e.tensor_add(bigE[:, 1536:1568], bigE[:, 1536:1568],
                                               bigE[:, 1600:1632]),
                        [(E, 1536, 1632)], [(E, 1536, 1568)])
                    vop(lambda e: e.tensor_mul(bigE[:, 1568:1600], bigE[:, 1536:1568],
                                               cols[:, 352:384]),
                        [(E, 1536, 1568), R("cols", 352, 384)], [(E, 1568, 1600)])
                    top(lambda e: e.transpose(ps[4][0:32, 0:128], bigE[:, 1568:1600],
                                              identf[:, :]),
                        [(E, 1568, 1600)], [("ps4", 0, 512)])
                    vop(lambda e: e.tensor_copy(bg2[0:32, 0:128], ps[4][0:32, 0:128]),
                        [("ps4", 0, 512)], [R("bg2", 0, 128)])
                    vop(lambda e: e.tensor_tensor_scan(bg2[0:32, 128:256],
                                                       bg2[0:32, 0:128],
                                                       bg2[0:32, 0:128], 0.0,
                                                       ALU.add, ALU.bypass),
                        [R("bg2", 0, 128)], [R("bg2", 128, 256)])
                    top(lambda e: e.transpose(ps[4][:, 128:160], bg2[0:32, 128:256],
                                              identf[0:32, 0:32]),
                        [R("bg2", 128, 256)], [("ps4", 0, 512)])
                    vop(lambda e, tb=tb: e.tensor_copy(tokc[:, tb + 32:tb + 64],
                                                       ps[4][:, 128:160]),
                        [("ps4", 0, 512)], [R("tokc", tb + 32, tb + 64)])
                    vop(lambda e, tb=tb: e.tensor_scalar_mul(tokc[:, tb + 64:tb + 96],
                                                             tokc[:, tb:tb + 32], -1.0),
                        [R("tokc", tb, tb + 32)], [R("tokc", tb + 64, tb + 96)])
                    sop(lambda e, tb=tb: e.activation(tokc[:, tb + 96:tb + 128],
                                                      tokc[:, tb + 32:tb + 64], AF.Exp),
                        [R("tokc", tb + 32, tb + 64)], [R("tokc", tb + 96, tb + 128)])
                    vop(lambda e: e.tensor_scalar_mul(bigE[0:32, 0:128],
                                                      ones2d[0:32, 0:128],
                                                      bg2[0:32, 255:256]),
                        [R("ones2d", 0, 128), R("bg2", 128, 256)], [(E, 0, 128)])
                    top(lambda e: e.transpose(ps[4][:, 160:192], bigE[0:32, 0:128],
                                              identf[0:32, 0:32]),
                        [(E, 0, 128)], [("ps4", 0, 512)])
                    sop(lambda e, tb=tb: e.activation(tokc[:, tb + 160:tb + 192],
                                                      ps[4][:, 160:192], AF.Exp),
                        [("ps4", 0, 512)], [R("tokc", tb + 160, tb + 192)])
                    vop(lambda e, tb=tb: e.tensor_sub(bigE[:, 1536:1568],
                                                      ps[4][:, 160:192],
                                                      tokc[:, tb + 32:tb + 64]),
                        [("ps4", 0, 512), R("tokc", tb + 32, tb + 64)], [(E, 1536, 1568)])
                    sop(lambda e, tb=tb: e.activation(tokc[:, tb + 128:tb + 160],
                                                      bigE[:, 1536:1568], AF.Exp),
                        [(E, 1536, 1568)], [R("tokc", tb + 128, tb + 160)])
                    sop(lambda e, tb=tb: e.activation(bigE[:, 1536:1568],
                                                      tokc[:, tb:tb + 32], AF.Ln),
                        [R("tokc", tb, tb + 32)], [(E, 1536, 1568)])
                    vop(lambda e, tb=tb: e.tensor_sub(tokc[:, tb + 192:tb + 224],
                                                      bigE[:, 1536:1568],
                                                      tokc[:, tb + 32:tb + 64]),
                        [(E, 1536, 1568), R("tokc", tb + 32, tb + 64)],
                        [R("tokc", tb + 192, tb + 224)])
                for kt in range(32):
                    fac = (1.0 / DK) if kt < 16 else (DK ** -0.5)
                    tkey = (A, kt * 512, (kt + 1) * 512)
                    vop(lambda e, kt=kt: e.tensor_mul(wbt[:, 0:512],
                                                      bigA[:, kt * 512:(kt + 1) * 512],
                                                      bigA[:, kt * 512:(kt + 1) * 512]),
                        [tkey], [R("wbt", 0, 512)])
                    top(lambda e: e.matmul(ps[4][0:1, 0:512], onescb[:, 0:1],
                                           wbt[:, 0:512], start=True, stop=True),
                        [R("wbt", 0, 512), R("onescb", 0, 1)], [("ps4", 0, 512)])
                    vop(lambda e: e.tensor_scalar_mul(rowB[0:1, 0:512],
                                                      ps[4][0:1, 0:512], 1.0 / DK),
                        [("ps4", 0, 512)], [R("rowB", 0, 512)])
                    sop(lambda e: e.activation(rowB[0:1, 0:512], rowB[0:1, 0:512],
                                               AF.Ln, bias=cols[0:1, CI["rmseps"]:CI["rmseps"] + 1]),
                        [R("rowB", 0, 512), R("cols", CI["rmseps"], CI["rmseps"] + 1)],
                        [R("rowB", 0, 512)])
                    sop(lambda e: e.activation(rowB[0:1, 0:512], rowB[0:1, 0:512],
                                               AF.Exp, scale=-0.5),
                        [R("rowB", 0, 512)], [R("rowB", 0, 512)])
                    vop(lambda e, fac=fac: e.tensor_scalar_mul(rowB[0:1, 0:512],
                                                               rowB[0:1, 0:512], fac),
                        [R("rowB", 0, 512)], [R("rowB", 0, 512)])
                    top(lambda e: e.matmul(ps[5][:, 0:512], onesf[0:1, 0:128],
                                           rowB[0:1, 0:512], start=True, stop=True),
                        [R("rowB", 0, 512), R("onesf", 0, 128)], [("ps5", 0, 512)])
                    vop(lambda e, kt=kt: e.tensor_mul(bigA[:, kt * 512:(kt + 1) * 512],
                                                      bigA[:, kt * 512:(kt + 1) * 512],
                                                      ps[5][:, 0:512]),
                        [tkey, ("ps5", 0, 512)], [tkey])
                for c in range(NCH):
                    tb = c * 256
                    ccol_ = lambda off, hh: tokc[:, tb + off + hh: tb + off + hh + 1]
                    ccolk = lambda off, hh: ("tokc", tb + off + hh, tb + off + hh + 1)
                    kksb = {}

                    def r1_pair(hp):
                        kh = hp
                        kpar = kh % 2
                        kap, kkey = ktile(kh, c)
                        qap, qkey = qtile(kh, c)
                        top(lambda e, kap=kap, kpar=kpar: e.matmul(
                                ps[4 + kpar][:, 0:128], kap, kap,
                                start=True, stop=True),
                            [kkey], [(f"ps{4 + kpar}", 0, 512)])
                        yield
                        top(lambda e, kap=kap, qap=qap, kpar=kpar: e.matmul(
                                ps[4 + kpar][:, 128:256], kap, qap,
                                start=True, stop=True),
                            [kkey, qkey], [(f"ps{4 + kpar}", 0, 512)])
                        yield
                        kkt, kktk = nslot()
                        kqt, kqtk = nslot()
                        sop(lambda e, kkt=kkt, kpar=kpar: e.copy(
                                kkt, ps[4 + kpar][:, 0:128]),
                            [(f"ps{4 + kpar}", 0, 512)], [kktk])
                        yield
                        vop(lambda e, kqt=kqt, kpar=kpar: e.tensor_copy(
                                kqt, ps[4 + kpar][:, 128:256]),
                            [(f"ps{4 + kpar}", 0, 512)], [kqtk])
                        yield
                        pb, pbk = pslot()
                        top(lambda e, kap=kap, pb=pb: e.transpose(pb, kap, identb[:, :]),
                            [kkey, R("identb", 0, 128)], [pbk])
                        yield
                        vop(lambda e, pb=pb, kh=kh: e.tensor_copy(
                                bigD[:, 4096 + kh * 128: 4096 + (kh + 1) * 128], pb),
                            [pbk], [(Dk, 4096 + kh * 128, 4096 + (kh + 1) * 128)])
                        yield
                        ktokk = (Dk, 4096 + kh * 128, 4096 + (kh + 1) * 128)
                        for h in (2 * hp, 2 * hp + 1):
                            par = h % 2
                            vap, vkey = vtile(h, c)
                            sV1, kV1 = fslot()
                            cap_cum = ccol_(32, h)
                            vop(lambda e, sV1=sV1, cap=cap_cum: e.tensor_scalar_mul(
                                    sV1, ones2d[:, :], cap),
                                [R("ones2d", 0, 128), ccolk(32, h)], [kV1])
                            yield
                            top(lambda e, sV1=sV1, par=par: e.transpose(
                                    ps[par][:, 0:128], sV1, identf[:, :]),
                                [kV1, R("identf", 0, 128)], [(f"ps{par}", 0, 512)])
                            yield
                            sM, kM = fslot()
                            vop(lambda e, sM=sM, cap=cap_cum, par=par: e.tensor_scalar(
                                    sM, ps[par][:, 0:128], cap, 0.0,
                                    ALU.subtract, ALU.min),
                                [(f"ps{par}", 0, 512), ccolk(32, h)], [kM])
                            yield
                            sE, kE = fslot()
                            sop(lambda e, sM=sM, sE=sE: e.activation(sE, sM, AF.Exp),
                                [kM], [kE])
                            yield
                            sV2, kV2 = fslot()
                            cap_cmb = ccol_(192, h)
                            vop(lambda e, sV2=sV2, cap=cap_cmb: e.tensor_scalar_mul(
                                    sV2, ones2d[:, :], cap),
                                [R("ones2d", 0, 128), ccolk(192, h)], [kV2])
                            yield
                            top(lambda e, sV2=sV2, par=par: e.transpose(
                                    ps[par][:, 128:256], sV2, identf[:, :]),
                                [kV2, R("identf", 0, 128)], [(f"ps{par}", 0, 512)])
                            yield
                            sM2, kM2 = fslot()
                            vop(lambda e, sM2=sM2, cap=cap_cum, par=par: e.tensor_scalar(
                                    sM2, ps[par][:, 128:256], cap, 0.0,
                                    ALU.add, ALU.min),
                                [(f"ps{par}", 0, 512), ccolk(32, h)], [kM2])
                            yield
                            sE2, kE2 = fslot()
                            sop(lambda e, sM2=sM2, sE2=sE2: e.activation(sE2, sM2, AF.Exp),
                                [kM2], [kE2])
                            yield
                            sA1, kA1 = fslot()
                            vop(lambda e, sA1=sA1, sE=sE, kkt=kkt: e.tensor_mul(sA1, sE, kkt),
                                [kE, kktk], [kA1])
                            yield
                            nT, nTk = nslot()
                            cap_nb = ccol_(64, h)
                            vop(lambda e, sA1=sA1, nT=nT, cap=cap_nb: e.scalar_tensor_tensor(
                                    nT, sA1, cap, mask_t[:, :], ALU.mult, ALU.mult),
                                [kA1, ccolk(64, h), R("mask_t", 0, 128)], [nTk])
                            yield
                            sA2, kA2 = fslot()
                            vop(lambda e, sA2=sA2, sE2=sE2, kkt=kkt: e.tensor_mul(sA2, sE2, kkt),
                                [kE2, kktk], [kA2])
                            yield
                            nS, nSk = nslot()
                            vop(lambda e, sA2=sA2, nS=nS: e.tensor_mul(nS, sA2,
                                                                       mask_sn[:, :]),
                                [kA2, R("mask_sn", 0, 128)], [nSk])
                            yield
                            sB1, kB1 = fslot()
                            vop(lambda e, sB1=sB1, sE=sE, kqt=kqt: e.tensor_mul(sB1, sE, kqt),
                                [kE, kqtk], [kB1])
                            yield
                            btk = (Dk, h * 128, (h + 1) * 128)
                            vop(lambda e, sB1=sB1, h=h: e.tensor_mul(
                                    bigD[:, h * 128:(h + 1) * 128], sB1, mask_i[:, :]),
                                [kB1, R("mask_i", 0, 128)], [btk])
                            yield
                            zzlo = h * 256
                            zzk = (C, zzlo, zzlo + 256)
                            pb2, pbk2 = pslot()
                            top(lambda e, vap=vap, pb2=pb2: e.transpose(pb2, vap, identb[:, :]),
                                [vkey, R("identb", 0, 128)], [pbk2])
                            yield
                            vop(lambda e, pb2=pb2, zzlo=zzlo: e.tensor_copy(
                                    bigC[:, zzlo:zzlo + 128], pb2),
                                [pbk2], [(C, zzlo, zzlo + 128)])
                            yield
                            cap_gam = ccol_(96, h)
                            vop(lambda e, zzlo=zzlo, kh=kh, cap=cap_gam: e.tensor_scalar_mul(
                                    bigC[:, zzlo + 128:zzlo + 256],
                                    bigD[:, 4096 + kh * 128: 4096 + (kh + 1) * 128],
                                    cap),
                                [ktokk, ccolk(96, h)], [(C, zzlo + 128, zzlo + 256)])
                            yield
                            if h % 2 == 0:
                                pairN = [(nT, nTk, nS, nSk)]
                                kksb[kh] = pairN
                            else:
                                kksb[kh].append((nT, nTk, nS, nSk))
                        (t0n, t0k, s0n, s0k), (t1n, t1k, s1n, s1k) = kksb[kh]
                        zzlo = 2 * hp * 256
                        zzk2 = (C, zzlo, zzlo + 512)
                        ab = 2 + (hp % 2)
                        q0 = hp % 2
                        q1 = 4 + (hp % 2)
                        for j in range(7):
                            top(lambda e, t0n=t0n, zzlo=zzlo, ab=ab: e.matmul(
                                    ps[ab][:, 0:256], t0n,
                                    bigC[:, zzlo:zzlo + 256], start=True, stop=True),
                                [t0k, zzk2], [(f"ps{ab}", 0, 512)])
                            yield
                            top(lambda e, t1n=t1n, zzlo=zzlo, ab=ab: e.matmul(
                                    ps[ab][:, 256:512], t1n,
                                    bigC[:, zzlo + 256:zzlo + 512], start=True, stop=True),
                                [t1k, zzk2], [(f"ps{ab}", 0, 512)])
                            yield
                            vop(lambda e, zzlo=zzlo, ab=ab: e.tensor_add(
                                    bigC[:, zzlo:zzlo + 512],
                                    bigC[:, zzlo:zzlo + 512], ps[ab][:, 0:512]),
                                [(f"ps{ab}", 0, 512), zzk2], [zzk2])
                            yield
                            if j < 6:
                                top(lambda e, s0n=s0n, t0n=t0n, q0=q0: e.matmul(
                                        ps[q0][:, 256:384], s0n, t0n,
                                        start=True, stop=True),
                                    [s0k, t0k], [(f"ps{q0}", 0, 512)])
                                yield
                                top(lambda e, s1n=s1n, t1n=t1n, q0=q0: e.matmul(
                                        ps[q0][:, 384:512], s1n, t1n,
                                        start=True, stop=True),
                                    [s1k, t1k], [(f"ps{q0}", 0, 512)])
                                yield
                                top(lambda e, s0n=s0n, t0n=t0n, q1=q1: e.matmul(
                                        ps[q1][:, 256:384], t0n, s0n,
                                        start=True, stop=True),
                                    [t0k, s0k], [(f"ps{q1}", 0, 512)])
                                yield
                                top(lambda e, s1n=s1n, t1n=t1n, q1=q1: e.matmul(
                                        ps[q1][:, 384:512], t1n, s1n,
                                        start=True, stop=True),
                                    [t1k, s1k], [(f"ps{q1}", 0, 512)])
                                yield
                                tp_, tpk = nslot2()
                                sp_, spk = nslot2()
                                sop(lambda e, tp_=tp_, q0=q0: e.copy(
                                        tp_, ps[q0][:, 256:512]),
                                    [(f"ps{q0}", 0, 512)], [tpk])
                                yield
                                vop(lambda e, sp_=sp_, q1=q1: e.tensor_copy(
                                        sp_, ps[q1][:, 256:512]),
                                    [(f"ps{q1}", 0, 512)], [spk])
                                yield
                                t0n, t1n = tp_[:, 0:128], tp_[:, 128:256]
                                s0n, s1n = sp_[:, 0:128], sp_[:, 128:256]
                                t0k = t1k = tpk
                                s0k = s1k = spk

                    gens = [r1_pair(hp) for hp in range(NKH)]
                    active = []
                    gi = 0
                    while active or gi < NKH:
                        while len(active) < 1 and gi < NKH:
                            active.append(gens[gi])
                            gi += 1
                        for gg in list(active):
                            try:
                                next(gg)
                            except StopIteration:
                                active.remove(gg)
                    def r2_head(h):
                        kh = h // 2
                        qap, qkey = qtile(kh, c)
                        zzlo = h * 256
                        zzk = (C, zzlo, zzlo + 256)
                        skey = ("S_sb", h * 128, (h + 1) * 128)
                        btk = (Dk, h * 128, (h + 1) * 128)
                        ktokk = (Dk, 4096 + kh * 128, 4096 + (kh + 1) * 128)
                        fb = 2 if h % 2 == 0 else 3
                        ob = 4 if h % 2 == 0 else 5
                        pcol = 32 + (h % 4) * 4
                        zr = bigB[:, 2048 + (h % 16) * 128: 2048 + (h % 16 + 1) * 128]
                        zrk = (Bk, 2048 + (h % 16) * 128, 2048 + (h % 16 + 1) * 128)
                        dma("y", zr, zscr[h * 65536:(h + 1) * 65536]
                            .rearrange("(p j) -> p j", p=128)[:, c * 128:(c + 1) * 128],
                            [("zscr", h * 65536, (h + 1) * 65536)], [zrk])
                        yield
                        if c > 0:
                            sbf = bigB[:, (h % 16) * 128:(h % 16 + 1) * 128]
                            sbfk = (Bk, (h % 16) * 128, (h % 16 + 1) * 128)
                            vop(lambda e, sbf=sbf, h=h: e.tensor_copy(
                                    sbf, S_sb[:, h * 128:(h + 1) * 128]),
                                [skey], [sbfk])
                            yield
                            ek, ekk = bslot()
                            pb3, pbk3 = pslot()
                            top(lambda e, pb3=pb3, zzlo=zzlo: e.transpose(
                                    pb3, bigC[:, zzlo + 128:zzlo + 256], identb[:, :]),
                                [zzk, R("identb", 0, 128)], [pbk3])
                            yield
                            vop(lambda e, ek=ek, pb3=pb3: e.tensor_copy(ek, pb3),
                                [pbk3], [ekk])
                            yield
                            top(lambda e, ek=ek, sbf=sbf, fb=fb: e.matmul(
                                    ps[fb][:, 0:128], ek, sbf, start=True, stop=True),
                                [ekk, sbfk], [(f"ps{fb}", 0, 512)])
                            yield
                            top(lambda e, qap=qap, sbf=sbf, fb=fb: e.matmul(
                                    ps[fb][:, 128:256], qap, sbf, start=True, stop=True),
                                [qkey, sbfk], [(f"ps{fb}", 0, 512)])
                            yield
                        sT, kT = fslot()
                        if c > 0:
                            vop(lambda e, sT=sT, zzlo=zzlo, fb=fb: e.tensor_sub(
                                    sT, bigC[:, zzlo:zzlo + 128], ps[fb][:, 0:128]),
                                [zzk, (f"ps{fb}", 0, 512)], [kT])
                        else:
                            vop(lambda e, sT=sT, zzlo=zzlo: e.tensor_copy(
                                    sT, bigC[:, zzlo:zzlo + 128]),
                                [zzk], [kT])
                        yield
                        dl, dlk = bslot()
                        cap_beta = ccol_(0, h)
                        vop(lambda e, dl=dl, sT=sT, cap=cap_beta: e.tensor_scalar_mul(
                                dl, sT, cap),
                            [kT, ccolk(0, h)], [dlk])
                        yield
                        top(lambda e, dl=dl, h=h, ob=ob: e.matmul(
                                ps[ob][:, 0:128], bigD[:, h * 128:(h + 1) * 128], dl,
                                start=True, stop=True),
                            [btk, dlk], [(f"ps{ob}", 0, 512)])
                        yield
                        sO, kO = fslot()
                        if c > 0:
                            cap_gam2 = ccol_(96, h)
                            vop(lambda e, sO=sO, cap=cap_gam2, fb=fb: e.tensor_scalar_mul(
                                    sO, ps[fb][:, 128:256], cap),
                                [(f"ps{fb}", 0, 512), ccolk(96, h)], [kO])
                            yield
                            vop(lambda e, sO=sO, ob=ob: e.tensor_add(
                                    sO, sO, ps[ob][:, 0:128]),
                                [(f"ps{ob}", 0, 512), kO], [kO])
                        else:
                            vop(lambda e, sO=sO, ob=ob: e.tensor_copy(
                                    sO, ps[ob][:, 0:128]),
                                [(f"ps{ob}", 0, 512)], [kO])
                        yield
                        sSq, kSq = fslot()
                        sop(lambda e, sO=sO, sSq=sSq, pcol=pcol: e.activation(
                                sSq, sO, AF.Square, accum_out=prob[:, pcol:pcol + 1]),
                            [kO], [kSq, R("prob", pcol, pcol + 1)])
                        yield
                        sop(lambda e, pcol=pcol: e.activation(
                                prob[:, pcol + 1:pcol + 2], prob[:, pcol:pcol + 1],
                                AF.Ln, scale=1.0 / DV,
                                bias=cols[:, CI["rmseps"]:CI["rmseps"] + 1]),
                            [R("prob", pcol, pcol + 1),
                             R("cols", CI["rmseps"], CI["rmseps"] + 1)],
                            [R("prob", pcol + 1, pcol + 2)])
                        yield
                        sop(lambda e, pcol=pcol: e.activation(
                                prob[:, pcol + 2:pcol + 3], prob[:, pcol + 1:pcol + 2],
                                AF.Exp, scale=-0.5),
                            [R("prob", pcol + 1, pcol + 2)],
                            [R("prob", pcol + 2, pcol + 3)])
                        yield
                        onb, onbk = bslot()
                        vop(lambda e, onb=onb, sO=sO, pcol=pcol: e.tensor_scalar_mul(
                                onb, sO, prob[:, pcol + 2:pcol + 3]),
                            [kO, R("prob", pcol + 2, pcol + 3)], [onbk])
                        yield
                        pb4, pbk4 = pslot()
                        top(lambda e, onb=onb, pb4=pb4: e.transpose(pb4, onb, identb[:, :]),
                            [onbk, R("identb", 0, 128)], [pbk4])
                        yield
                        gmap = (16 + h) if h < 16 else (h - 16)
                        glo = gmap * 512 + c * 128
                        vop(lambda e, pb4=pb4, glo=glo, zr=zr: e.tensor_mul(
                                bigA[:, glo:glo + 128], zr, pb4),
                            [pbk4, zrk], [(A, glo, glo + 128)])
                        yield
                        kb, kbk = bslot()
                        cap_dec = ccol_(128, h)
                        vop(lambda e, kb=kb, kh=kh, cap=cap_dec: e.tensor_scalar_mul(
                                kb, bigD[:, 4096 + kh * 128: 4096 + (kh + 1) * 128],
                                cap),
                            [ktokk, ccolk(128, h)], [kbk])
                        yield
                        top(lambda e, kb=kb, dl=dl, ob=ob: e.matmul(
                                ps[ob][:, 128:256], kb, dl, start=True, stop=True),
                            [kbk, dlk], [(f"ps{ob}", 0, 512)])
                        yield
                        if c == 0:
                            vop(lambda e, h=h, ob=ob: e.tensor_copy(
                                    S_sb[:, h * 128:(h + 1) * 128], ps[ob][:, 128:256]),
                                [(f"ps{ob}", 0, 512)], [skey])
                        else:
                            cap_gc = ccol_(160, h)
                            vop(lambda e, h=h, ob=ob, cap=cap_gc: e.scalar_tensor_tensor(
                                    S_sb[:, h * 128:(h + 1) * 128],
                                    S_sb[:, h * 128:(h + 1) * 128], cap,
                                    ps[ob][:, 128:256], ALU.mult, ALU.add),
                                [skey, ccolk(160, h), (f"ps{ob}", 0, 512)], [skey])
                        yield

                    gens2 = [r2_head(h) for h in range(NVH)]
                    active2 = []
                    g2 = 0
                    while active2 or g2 < NVH:
                        while len(active2) < 2 and g2 < NVH:
                            active2.append(gens2[g2])
                            g2 += 1
                        for gg in list(active2):
                            try:
                                next(gg)
                            except StopIteration:
                                active2.remove(gg)
                for pp2 in range(16):
                    slot = pp2 % 2
                    wkey = (Bk, slot * 4096, slot * 4096 + 4096)
                    load_panel("L0_attn_W_out", pp2, bigB, slot * 4096, wkey)
                    j = pp2
                    bank = j % 2
                    for k in range(32):
                        gk = (16 + k) if k < 16 else (k - 16)
                        top(lambda e, slot=slot, k=k, bank=bank, gk=gk: e.matmul(
                                ps[bank][:, 0:512],
                                bigB[:, slot * 4096 + k * 128:
                                     slot * 4096 + k * 128 + 128],
                                bigA[:, gk * 512:(gk + 1) * 512],
                                start=(k == 0), stop=(k == 31)),
                            [wkey, (A, gk * 512, (gk + 1) * 512)],
                            [(f"ps{bank}", 0, 512)] if k == 0 else [],
                            group_n=32 if k == 0 else 1)
                    xk = ("xfst", 0, 512)
                    dma("y", xfst[:, 0:512],
                        src_hbm[j * 128:(j + 1) * 128, t0:t0 + 512],
                        [xiv(src_key, j, t0)], [xk])
                    ykey = ("xfst", 512, 1024)
                    vop(lambda e, bank=bank: e.tensor_add(
                            xfst[:, 512:1024], ps[bank][:, 0:512], xfst[:, 0:512]),
                        [(f"ps{bank}", 0, 512), xk], [ykey])
                    dma("y", dst_hbm[j * 128:(j + 1) * 128, t0:t0 + 512],
                        xfst[:, 512:1024], [ykey], [xiv(dst_key, j, t0)])

    setup_constants()
    setup_vectors()
    setup_cast()
    B.barrier()
    seqs = []
    if "ffn1" in parts:
        seqs.append(("ffn1", x_in, "x_in"))
    chain = {"ffn1": (xA, "xA"), "attn": (xB, "xB")}
    srcs = {"ffn1": (x_in, "x_in"), "attn": (xA, "xA"), "ffn2": (xB, "xB")}
    plist = [p for p in ("ffn1", "attn", "ffn2") if p in parts]
    for i, p in enumerate(plist):
        src_hbm, src_key = srcs[p]
        if i + 1 < len(plist):
            dst_hbm, dst_key = srcs[plist[i + 1]]
        else:
            dst_hbm, dst_key = out, "out"
        if p in ("ffn1", "ffn2"):
            ffn(src_hbm, src_key, dst_hbm, dst_key)
        else:
            attn(src_hbm, src_key, dst_hbm, dst_key)
        B.barrier()
    for s_ in list(B.sems):
        if s_ == "y":
            continue
        v = B.cnt[s_]
        if v > 0 and B.waited["y"].get(s_, 0) < v:
            B.waited["y"][s_] = v
            B.stream["y"].append(([(s_, v)], None, None, 0))
    B.finalize()
    mybir.codegen_inst_isa_subclasses(nc)
    return nc


def attn(*a, **k):
    raise NotImplementedError("attn emitted later")


def kernel(**inputs):
    x = inputs["L0_attn_x"]
    nc = build_program()
    in_maps = []
    for c in range(NCORE):
        m = {}
        xc = x[2 * c: 2 * c + 2]
        m["L0_attn_x"] = np.ascontiguousarray(xc.reshape(NTOK, D).T)
        for n in WMATS:
            m[n] = np.ascontiguousarray(inputs[n])
        for n in VECS:
            m[n] = np.ascontiguousarray(inputs[n]).reshape(-1)
        in_maps.append(m)
    res = run_bass_kernel_spmd(nc, in_maps, core_ids=list(range(NCORE)))
    outs = []
    for c in range(NCORE):
        o = res.results[c]["OUT"]
        outs.append(o.T.reshape(2, SEQ, D))
    return np.concatenate(outs, axis=0)


# version 07 (final)
# speedup vs baseline: 1.2517x; 1.0914x over previous
import contextlib

import numpy as np

import concourse.bass as bass
import concourse.mybir as mybir
from concourse.bass_utils import run_bass_kernel_spmd

F32 = mybir.dt.float32
BF16 = mybir.dt.bfloat16
AF = mybir.ActivationFunctionType
ALU = mybir.AluOpType
AX = mybir.AxisListType.X

D = 2048
NCORE = 8
NBATCH = 2
SEQ = 2048
NTOK = NBATCH * SEQ
L = 1024
HID = [1024, 2048, 3072, 4096, 1024, 2048, 3072, 4096]
COST_LAMBDA = 1e-7
LNEPS = 1e-5
NKH, NVH, DK, DV = 16, 32, 128, 128
KCONV = 4
RMSEPS = 1e-6
KEY_DIM = NKH * DK
VALUE_DIM = NVH * DV
CONV_DIM = 2 * KEY_DIM + VALUE_DIM
SC, CH = 512, 128
NCH = SC // CH
FB = 512
NEG_BIG = -1e30
PANEL = 256


def panelw(name):
    if name in ("L0_attn_W_qkvz", "L0_ffn_up_w", "L0_ffn_core_w"):
        return 128
    K = WMATS[name][0]
    return 128 if K > 2048 else 256


def qkvz_perm(bw):
    kh, r = divmod(bw, 6)
    if r == 0:
        return kh
    if r == 1:
        return 16 + kh
    if r in (2, 3):
        return 32 + 2 * kh + (r - 2)
    return 64 + 2 * kh + (r - 4)

WMATS = {
    "L0_attn_W_qkvz": (D, 12288),
    "L0_attn_W_ba": (D, 64),
    "L0_attn_W_out": (VALUE_DIM, D),
    "L0_ffn_down_w": (D, L),
    "L0_ffn_router_w": (D, 8),
    "L0_ffn_up_w": (L, D),
    "L0_ffn_core_w": (D, D),
    "L0_ffn_shared_w1": (L, L),
    "L0_ffn_shared_w2": (L, L),
}
for _e, _h in enumerate(HID):
    WMATS[f"L0_ffn_e{_e}_w1"] = (L, _h)
    WMATS[f"L0_ffn_e{_e}_w2"] = (_h, L)

VECS = {
    "L0_attn_conv_w": CONV_DIM * KCONV,
    "L0_attn_dt_bias": NVH,
    "L0_attn_A_log": NVH,
    "L0_attn_norm_w": DV,
    "L0_ffn_down_b": L,
    "L0_ffn_up_b": D,
    "L0_ffn_router_b": 8,
    "L0_ffn_shared_b1": L,
    "L0_ffn_shared_b2": L,
    "L0_ffn_core_b": D,
}
for _e, _h in enumerate(HID):
    VECS[f"L0_ffn_e{_e}_b1"] = _h
    VECS[f"L0_ffn_e{_e}_b2"] = L


def ceil_div(a, b):
    return -(-a // b)


class Builder:

    ENGS = ("t", "v", "s", "g", "y")

    NDMASEM = 8

    def __init__(self, nc):
        self.nc = nc
        self.es = contextlib.ExitStack()
        self.stream = {e: [] for e in self.ENGS}
        self.sems = {}
        names = ["t", "v", "s", "g", "y"]
        for q in ("dy", "ds", "dg"):
            for i in range(self.NDMASEM):
                names.append(f"{q}{i}")
        for name in names:
            self.sems[name] = self.es.enter_context(nc.semaphore("sem_" + name))
        self.cnt = {name: 0 for name in self.sems}
        self.rr = {"dy": 0, "ds": 0, "dg": 0}
        self.waited = {e: {} for e in self.ENGS}
        self.track = {}
        self.same_engine_sync = True

    def _acc(self, key, lo, hi, pt, write):
        ivs = self.track.setdefault(key, [])
        deps = []
        if write:
            keep = []
            nlo, nhi = lo, hi
            for iv in ivs:
                if iv[1] <= lo or iv[0] >= hi:
                    keep.append(iv)
                else:
                    if iv[2] is not None:
                        deps.append(iv[2])
                    deps.extend(iv[3])
                    nlo = min(nlo, iv[0])
                    nhi = max(nhi, iv[1])
            keep.append([nlo, nhi, pt, []])
            self.track[key] = keep
        else:
            hit = False
            for iv in ivs:
                if iv[1] <= lo or iv[0] >= hi:
                    continue
                hit = True
                if iv[2] is not None:
                    deps.append(iv[2])
                iv[3].append(pt)
            if not hit:
                ivs.append([lo, hi, None, [pt]])
        return deps

    def emit(self, eng, fn, reads=(), writes=(), dma=False, group_n=1):
        if dma:
            q = "d" + eng
            sem = f"{q}{self.rr[q]}"
            self.rr[q] = (self.rr[q] + 1) % self.NDMASEM
            inc = 16
        else:
            sem = eng
            inc = 1
        pt = (sem, self.cnt[sem] + inc)
        wpt = (sem, self.cnt[sem] + inc * group_n)
        deps = []
        if dma and self.cnt[sem] > 0:
            deps.append((sem, self.cnt[sem]))
        for r in reads:
            deps += self._acc(r[0], r[1], r[2], pt, False)
        for w in writes:
            deps += self._acc(w[0], w[1], w[2], wpt, True)
        cnt_before = pt[1] - inc
        self.cnt[sem] = pt[1]
        wd = self.waited[eng]
        bys = {}
        for (s, v) in deps:
            if s == sem and v > cnt_before:
                continue
            if s == eng and not self.same_engine_sync:
                continue
            if wd.get(s, 0) >= v:
                continue
            bys[s] = max(bys.get(s, 0), v)
        for s, v in bys.items():
            wd[s] = v
        self.stream[eng].append((list(bys.items()), fn, sem, inc))
        return pt

    def barrier(self, engs=None):
        engs = engs or self.ENGS
        tgt = dict(self.cnt)
        for e in engs:
            waits = []
            for s, v in tgt.items():
                if v > 0 and self.waited[e].get(s, 0) < v:
                    self.waited[e][s] = v
                    waits.append((s, v))
            if waits:
                self.stream[e].append((waits, None, None, 0))

    def finalize(self):
        sems = self.sems
        with self.nc.Block() as block:
            def mk(eng_key):
                def body(e):
                    for waits, fn, sem, inc in self.stream[eng_key]:
                        for s, v in waits:
                            e.wait_ge(sems[s], v)
                        if fn is not None:
                            fn(e).then_inc(sems[sem], inc)
                return body
            block.tensor(mk("t"))
            block.vector(mk("v"))
            block.scalar(mk("s"))
            block.gpsimd(mk("g"))
            block.sync(mk("y"))


def build_program(parts=("ffn1", "attn", "ffn2")):
    nc = bass.Bass("TRN2", target_bir_lowering=False)
    B = Builder(nc)
    es = B.es

    x_in = nc.dram_tensor("L0_attn_x", [D, NTOK], F32, kind="ExternalInput").ap()
    out = nc.dram_tensor("OUT", [D, NTOK], F32, kind="ExternalOutput").ap()
    wmat = {n: nc.dram_tensor(n, list(s), F32, kind="ExternalInput").ap()
            for n, s in WMATS.items()}
    wvec = {n: nc.dram_tensor(n, [s], F32, kind="ExternalInput").ap()
            for n, s in VECS.items()}
    wscr = {n: nc.dram_tensor("scr_" + n, [K * N], BF16, kind="Internal").ap()
            for n, (K, N) in WMATS.items()}
    xA = nc.dram_tensor("xA", [D, NTOK], F32, kind="Internal").ap()
    xB = nc.dram_tensor("xB", [D, NTOK], F32, kind="Internal").ap()
    rowscr = nc.dram_tensor("rowscr", [16384], F32, kind="Internal").ap()
    zscr = nc.dram_tensor("zscr", [NVH * 128 * SC], BF16, kind="Internal").ap()
    dbg = nc.dram_tensor("DBG", [128, 2048], F32, kind="Internal").ap()

    def sb(name, p, w, dt):
        return es.enter_context(nc.sbuf_tensor(name, [p, w], dt))

    bigA = sb("bigA", 128, 38912, BF16)
    bigB = sb("bigB", 128, 8192, BF16)
    bigC = sb("bigC", 128, 8192, BF16)
    bigD = sb("bigD", 128, 8192, BF16)
    bigE = sb("bigE", 128, 4096, F32)
    S_sb = sb("S_sb", 128, 4096, F32)
    xfst = sb("xfst", 128, 1024, F32)
    hist = sb("hist", 128, 192, F32)
    cols = sb("cols", 128, 384, F32)
    ones2d = sb("ones2d", 128, 128, F32)
    gb1c = sb("gb1c", 128, 176, F32)
    gb1cb = sb("gb1cb", 128, 176, BF16)
    gb1c_raw = sb("gb1c_raw", 128, 176, F32)
    identf = sb("identf", 128, 128, F32)
    identb = sb("identb", 128, 128, BF16)
    mask_t = sb("mask_t", 128, 128, BF16)
    mask_i = sb("mask_i", 128, 128, BF16)
    mask_sn = sb("mask_sn", 128, 128, BF16)
    onesf = sb("onesf", 1, 128, F32)
    onesb = sb("onesb", 1, 512, BF16)
    onescb = sb("onescb", 128, 1, BF16)
    selt = sb("selt", 8, 1024, BF16)
    rowA = sb("rowA", 1, 2048, F32)
    rowB = sb("rowB", 1, 1024, F32)
    rowsb = sb("rowsb", 1, 3328, BF16)
    D8pm = sb("D8pm", 8, 1024, BF16)
    w8f = sb("w8f", 128, 64, F32)
    w8tb = sb("w8tb", 8, 512, BF16)
    wbt = sb("wbt", 128, 512, BF16)
    stat = rowA
    bg = sb("bg", 128, 512, F32)
    bg2 = sb("bg2", 128, 256, F32)
    tokc = sb("tokc", 128, 1024, F32)
    prob = sb("prob", 128, 64, F32)
    iot = sb("iot", 128, 128, F32)

    ps = [es.enter_context(nc.psum_tensor(f"ps{i}", [128, 512], F32)) for i in range(6)]
    pbf = [es.enter_context(nc.psum_tensor(f"pbf{i}", [128, 1024], BF16)) for i in range(2)]

    def R(t, lo, hi):
        return (t, lo, hi)

    A, Bk, C, Dk, E = "bigA", "bigB", "bigC", "bigD", "bigE"
    PS = [f"ps{i}" for i in range(6)]
    PB = ["pbf0", "pbf1"]

    def dma(eng, out_ap, in_ap, reads, writes):
        return B.emit(eng, lambda e: e.dma_start(out_ap, in_ap), reads=reads,
                      writes=writes, dma=True)

    def vop(fn, reads, writes):
        return B.emit("v", fn, reads=reads, writes=writes)

    def sop(fn, reads, writes):
        return B.emit("s", fn, reads=reads, writes=writes)

    def gop(fn, reads, writes):
        return B.emit("g", fn, reads=reads, writes=writes)

    def top(fn, reads, writes, group_n=1):
        return B.emit("t", fn, reads=reads, writes=writes, group_n=group_n)

    def xiv(xkey, j, t0, w=512):
        return (xkey, j * NTOK + t0, j * NTOK + t0 + w)

    def setup_constants():
        gop(lambda e: e.iota(iot[:, :], [[1, 128]], channel_multiplier=-1,
                             allow_small_or_imprecise_dtypes=True),
            [], [R("iot", 0, 128)])
        gop(lambda e: e.memset(onesf[:, :], 1.0), [], [R("onesf", 0, 128)])
        gop(lambda e: e.memset(onesb[:, :], 1.0), [], [R("onesb", 0, 512)])
        gop(lambda e: e.memset(onescb[:, :], 1.0), [], [R("onescb", 0, 1)])
        gop(lambda e: e.memset(ones2d[:, :], 1.0), [], [R("ones2d", 0, 128)])
        vop(lambda e: e.tensor_single_scalar(identf[:, :], iot[:, :], 0.0, ALU.is_equal),
            [R("iot", 0, 128)], [R("identf", 0, 128)])
        vop(lambda e: e.tensor_copy(identb[:, :], identf[:, :]),
            [R("identf", 0, 128)], [R("identb", 0, 128)])
        vop(lambda e: e.tensor_single_scalar(mask_t[:, :], iot[:, :], 0.0, ALU.is_gt),
            [R("iot", 0, 128)], [R("mask_t", 0, 128)])
        vop(lambda e: e.tensor_single_scalar(mask_i[:, :], iot[:, :], 0.0, ALU.is_ge),
            [R("iot", 0, 128)], [R("mask_i", 0, 128)])
        vop(lambda e: e.tensor_scalar(mask_sn[:, :], iot[:, :], 0.0, -1.0,
                                      ALU.is_lt, ALU.mult),
            [R("iot", 0, 128)], [R("mask_sn", 0, 128)])
        gop(lambda e: e.iota(bigE[0:8, 0:128], [[0, 128]], channel_multiplier=1,
                             allow_small_or_imprecise_dtypes=True),
            [], [R(E, 0, 128)])
        for ee in range(8):
            vop(lambda e, ee=ee: e.tensor_single_scalar(
                    selt[0:8, ee * 128:(ee + 1) * 128], bigE[0:8, 0:128],
                    float(ee), ALU.is_equal),
                [R(E, 0, 128)], [R("selt", ee * 128, (ee + 1) * 128)])

    _colptr = [0]
    CI = {}

    def colify(row_t, row_key, off, n, dst_t, dst_name, dst_off, stride=1):
        pbk = ps[5]
        if stride == 1:
            src_ap = row_t[0:1, off:off + n]
        else:
            src_ap = bass.AP(row_t, off, [[row_t.shape[1], 1], [stride, n]])
        top(lambda e: e.matmul(pbk[:, 0:n], onesf[0:1, 0:128], src_ap,
                               start=True, stop=True),
            [R(row_key, 0, row_t.shape[1]), R("onesf", 0, 128)], [R("ps5", 0, 512)])
        vop(lambda e: e.tensor_mul(prob[:, 0:n] if n <= 64 else bigE[:, 0:n],
                                   pbk[:, 0:n], identf[:, 0:n]),
            [R("ps5", 0, 512), R("identf", 0, 128)],
            [R("prob", 0, 64) if n <= 64 else R(E, 0, 128)])
        srct = prob if n <= 64 else bigE
        srck = R("prob", 0, 64) if n <= 64 else R(E, 0, 128)
        vop(lambda e: e.reduce_sum(dst_t[:, dst_off:dst_off + 1], srct[:, 0:n], axis=AX),
            [srck], [(dst_name, dst_off, dst_off + 1)])

    def ccol(row_t, row_key, off, n, stride=1):
        idx = _colptr[0]
        _colptr[0] += 1
        colify(row_t, row_key, off, n, cols, "cols", idx, stride)
        return idx

    def load_row(dst_off, name, lo, hi, row_tensor, row_key):
        dma("y", row_tensor[0:1, dst_off:dst_off + (hi - lo)], wvec[name][lo:hi],
            [], [(row_key, dst_off, dst_off + (hi - lo))])

    def setup_vectors():
        load_row(0, "L0_ffn_down_b", 0, L, rowB, "rowB")
        CI["lneps"] = _colptr[0]; _colptr[0] += 1
        CI["rmseps"] = _colptr[0]; _colptr[0] += 1
        CI["one"] = _colptr[0]; _colptr[0] += 1
        vop(lambda e: e.memset(cols[:, CI["lneps"]:CI["lneps"] + 1], LNEPS),
            [], [R("cols", CI["lneps"], CI["lneps"] + 1)])
        vop(lambda e: e.memset(cols[:, CI["rmseps"]:CI["rmseps"] + 1], RMSEPS),
            [], [R("cols", CI["rmseps"], CI["rmseps"] + 1)])
        vop(lambda e: e.memset(cols[:, CI["one"]:CI["one"] + 1], 1.0),
            [], [R("cols", CI["one"], CI["one"] + 1)])
        CI["down_b"] = _colptr[0]
        for j in range(8):
            ccol(rowB, "rowB", j * 128, 128)
        for half in range(2):
            load_row(0, "L0_ffn_up_b", half * 1024, (half + 1) * 1024, rowB, "rowB")
            load_row(0, "L0_ffn_core_b", half * 1024, (half + 1) * 1024, rowA, "rowA")
            vop(lambda e, half=half: e.tensor_add(
                    rowsb[0:1, half * 1024:(half + 1) * 1024],
                    rowB[0:1, 0:1024], rowA[0:1, 0:1024]),
                [R("rowB", 0, 1024), R("rowA", 0, 1024)],
                [R("rowsb", half * 1024, (half + 1) * 1024)])
        load_row(0, "L0_attn_dt_bias", 0, NVH, rowB, "rowB")
        load_row(32, "L0_attn_A_log", 0, NVH, rowB, "rowB")
        load_row(64, "L0_attn_norm_w", 0, DV, rowB, "rowB")
        CI["normw"] = ccol(rowB, "rowB", 64, 128)
        CI["dtbB"], CI["neB"] = 320, 352
        top(lambda e: e.matmul(ps[5][:, 0:32], onesf[0:1, 0:128], rowB[0:1, 0:32],
                               start=True, stop=True),
            [R("rowB", 0, 32), R("onesf", 0, 128)], [R("ps5", 0, 512)])
        vop(lambda e: e.tensor_copy(cols[:, 320:352], ps[5][:, 0:32]),
            [R("ps5", 0, 512)], [R("cols", 320, 352)])
        sop(lambda e: e.activation(rowB[0:1, 64:96], rowB[0:1, 32:64], AF.Exp),
            [R("rowB", 32, 64)], [R("rowB", 64, 96)])
        vop(lambda e: e.tensor_scalar_mul(rowB[0:1, 96:128], rowB[0:1, 64:96], -1.0),
            [R("rowB", 64, 96)], [R("rowB", 96, 128)])
        top(lambda e: e.matmul(ps[5][:, 0:32], onesf[0:1, 0:128], rowB[0:1, 96:128],
                               start=True, stop=True),
            [R("rowB", 96, 128), R("onesf", 0, 128)], [R("ps5", 0, 512)])
        vop(lambda e: e.tensor_copy(cols[:, 352:384], ps[5][:, 0:32]),
            [R("ps5", 0, 512)], [R("cols", 352, 384)])
        CI["b1"] = {}
        ptr = 0
        for ee in range(9):
            h = HID[ee] if ee < 8 else L
            nm = f"L0_ffn_e{ee}_b1" if ee < 8 else "L0_ffn_shared_b1"
            CI["b1"][ee] = ptr
            for half in range(ceil_div(h, 1024)):
                lo = half * 1024
                w = min(1024, h - lo)
                load_row(0, nm, lo, lo + w, rowB, "rowB")
                for j in range(w // 128):
                    colify(rowB, "rowB", j * 128, 128, gb1c_raw, "gb1c_raw", ptr)
                    ptr += 1
        CI["sb1"] = CI["b1"][8]
        sop(lambda e, ptr=ptr: e.activation(gb1c[:, 0:ptr], gb1c_raw[:, 0:ptr], AF.Gelu),
            [R("gb1c_raw", 0, ptr)], [R("gb1c", 0, ptr)])
        vop(lambda e, ptr=ptr: e.tensor_copy(gb1cb[:, 0:ptr], gb1c[:, 0:ptr]),
            [R("gb1c", 0, ptr)], [R("gb1cb", 0, ptr)])
        CI["conv"] = {}
        for piece in range(32):
            load_row(0, "L0_attn_conv_w", piece * 1024, (piece + 1) * 1024, rowB, "rowB")
            for tt in range(2):
                for j in range(4):
                    tile = piece * 2 + tt
                    CI["conv"][(tile, j)] = _colptr[0]
                    ccol(rowB, "rowB", tt * 512 + j, 128, stride=4)
        load_row(0, "L0_ffn_router_b", 0, 8, rowB, "rowB")
        for ee in range(8):
            cst = COST_LAMBDA * (L * HID[ee] + HID[ee] * L)
            vop(lambda e, ee=ee, cst=cst: e.memset(rowB[0:1, 8 + ee:9 + ee], float(cst)),
                [], [R("rowB", 8, 16)])
        vop(lambda e: e.tensor_sub(rowB[0:1, 16:24], rowB[0:1, 0:8], rowB[0:1, 8:16]),
            [R("rowB", 0, 16)], [R("rowB", 16, 24)])
        vop(lambda e: e.tensor_copy(rowsb[0:1, 3072:3080], rowB[0:1, 16:24]),
            [R("rowB", 16, 24)], [R("rowsb", 3072, 3080)])

    CASTOFF = 24576

    def cast_weight(name, post=None, scale_col=None, scale_imm=None):
        K, N = WMATS[name]
        PW = panelw(name)
        src, dst = wmat[name], wscr[name]
        sname = "scr_" + name
        pieces = ceil_div(N, 1024)
        for ks in range(K // 128):
            for pi in range(pieces):
                p0 = pi * 1024
                w = min(1024, N - p0)
                slot = (ks * pieces + pi) % 2
                cb = bigA[:, CASTOFF + slot * 1024: CASTOFF + slot * 1024 + w]
                ckey = (A, CASTOFF + slot * 1024, CASTOFF + slot * 1024 + w)
                dma("g", cb, src[ks * 128:(ks + 1) * 128, p0:p0 + w], [], [ckey])
                if scale_col is not None:
                    vop(lambda e, cb=cb, sc=scale_col: e.tensor_scalar_mul(
                            cb, cb, cols[:, sc:sc + 1]),
                        [ckey, R("cols", scale_col, scale_col + 1)], [ckey])
                if scale_imm is not None:
                    vop(lambda e, cb=cb, si=scale_imm: e.tensor_scalar_mul(cb, cb, si),
                        [ckey], [ckey])
                if post is not None:
                    post(cb, ckey, ks, p0, w)
                nk = K // 128
                for pp in range(p0 // PW, ceil_div(p0 + w, PW)):
                    pw = min(PW, N - pp * PW)
                    dpp = qkvz_perm(pp) if name == "L0_attn_W_qkvz" else pp
                    base = dpp * PW * K + ks * pw
                    dma("y", bass.AP(dst.tensor, base, [[nk * pw, 128], [1, pw]]),
                        cb[:, pp * PW - p0: pp * PW - p0 + pw],
                        [ckey], [(sname, dpp * PW * K, (dpp + 1) * PW * K)])

    def load_panel(name, pp, dest_t, dest_off, dkey, eng="s"):
        K, N = WMATS[name]
        PW = panelw(name)
        pw = min(PW, N - pp * PW)
        base = pp * PW * K
        nk = K // 128
        dma(eng, dest_t[:, dest_off: dest_off + nk * pw],
            wscr[name][base: base + K * pw].rearrange("(p kj) -> p kj", p=128),
            [("scr_" + name, base, base + K * pw)], [dkey])
        return pw

    def make_w2_post(ee):
        h = HID[ee]
        base = CI["b1"][ee]
        nk = h // 128

        def post(cb, ckey, ks, p0, w):
            for half in range(2):
                bank = 3 + half
                top(lambda e, cb=cb, half=half, bank=bank, ks=ks: e.matmul(
                        ps[bank][0:1, 0:512], gb1cb[:, base + ks: base + ks + 1],
                        cb[:, half * 512: half * 512 + 512],
                        start=(ks == 0), stop=(ks == nk - 1)),
                    [ckey, R("gb1cb", base + ks, base + ks + 1)],
                    [(f"ps{bank}", 0, 512)] if ks == 0 else [],
                    group_n=(2 * nk - 1) if ks == 0 else 1)
        return post

    def setup_cast():
        cast_weight("L0_ffn_down_w")
        cast_weight("L0_ffn_router_w")
        for ee in range(8):
            cast_weight(f"L0_ffn_e{ee}_w1")
            load_row(0, f"L0_ffn_e{ee}_b2", 0, L, rowB, "rowB")
            cast_weight(f"L0_ffn_e{ee}_w2", post=make_w2_post(ee))
            for half in range(2):
                bank = 3 + half
                lo = half * 512
                co = 1024 + half * 512
                if ee == 0:
                    vop(lambda e, bank=bank, co=co, lo=lo: e.tensor_add(
                            rowA[0:1, co:co + 512], rowB[0:1, lo:lo + 512],
                            ps[bank][0:1, 0:512]),
                        [(f"ps{bank}", 0, 512), R("rowB", lo, lo + 512)],
                        [R("rowA", co, co + 512)])
                else:
                    vop(lambda e, bank=bank, co=co: e.tensor_add(
                            rowA[0:1, co:co + 512], rowA[0:1, co:co + 512],
                            ps[bank][0:1, 0:512]),
                        [(f"ps{bank}", 0, 512), R("rowA", co, co + 512)],
                        [R("rowA", co, co + 512)])
                    vop(lambda e, co=co, lo=lo: e.tensor_add(
                            rowA[0:1, co:co + 512], rowA[0:1, co:co + 512],
                            rowB[0:1, lo:lo + 512]),
                        [R("rowB", lo, lo + 512), R("rowA", co, co + 512)],
                        [R("rowA", co, co + 512)])
                vop(lambda e, bank=bank, lo=lo: e.tensor_scalar_mul(
                        rowB[0:1, lo:lo + 512], ps[bank][0:1, 0:512], -1.0),
                    [(f"ps{bank}", 0, 512)],
                    [R("rowB", lo, lo + 512)])
            dma("y", rowscr[ee * 1024:(ee + 1) * 1024], rowB[0:1, 0:1024],
                [R("rowB", 0, 1024)], [("rowscr", ee * 1024, (ee + 1) * 1024)])
        dma("g", D8pm[:, :], rowscr[0:8192].rearrange("(p f) -> p f", p=8),
            [("rowscr", 0, 8192)], [R("D8pm", 0, 1024)])
        load_row(0, "L0_ffn_shared_b2", 0, L, rowB, "rowB")
        vop(lambda e: e.scalar_tensor_tensor(rowA[0:1, 1024:2048], rowB[0:1, 0:1024],
                                             0.1, rowA[0:1, 1024:2048], ALU.mult, ALU.add),
            [R("rowB", 0, 1024), R("rowA", 1024, 2048)], [R("rowA", 1024, 2048)])
        vop(lambda e: e.tensor_copy(rowsb[0:1, 2048:3072], rowA[0:1, 1024:2048]),
            [R("rowA", 1024, 2048)], [R("rowsb", 2048, 3072)])
        cast_weight("L0_ffn_shared_w1")
        cast_weight("L0_ffn_shared_w2", scale_imm=0.1)
        cast_weight("L0_ffn_up_w")
        cast_weight("L0_ffn_core_w")
        cast_weight("L0_attn_W_qkvz")
        cast_weight("L0_attn_W_ba")
        cast_weight("L0_attn_W_out", scale_col=CI["normw"])

    def layernorm(src_hbm, src_key, t0, dst_t, dst_off, dst_key):
        for k in range(16):
            dma("g", bigD[:, k * 512:(k + 1) * 512],
                src_hbm[k * 128:(k + 1) * 128, t0:t0 + 512],
                [xiv(src_key, k, t0)],
                [(Dk, k * 512, (k + 1) * 512)])
        for k in range(16):
            top(lambda e, k=k: e.matmul(ps[4][0:1, 0:512], onescb[:, 0:1],
                                        bigD[:, k * 512:(k + 1) * 512],
                                        start=(k == 0), stop=(k == 15)),
                [(Dk, k * 512, (k + 1) * 512), R("onescb", 0, 1)],
                [("ps4", 0, 512)] if k == 0 else [], group_n=16 if k == 0 else 1)
        for k in range(16):
            vop(lambda e, k=k: e.tensor_mul(wbt[:, 0:512], bigD[:, k * 512:(k + 1) * 512],
                                            bigD[:, k * 512:(k + 1) * 512]),
                [(Dk, k * 512, (k + 1) * 512)], [R("wbt", 0, 512)])
            top(lambda e, k=k: e.matmul(ps[5][0:1, 0:512], onescb[:, 0:1], wbt[:, 0:512],
                                        start=(k == 0), stop=(k == 15)),
                [R("wbt", 0, 512), R("onescb", 0, 1)],
                [("ps5", 0, 512)] if k == 0 else [], group_n=16 if k == 0 else 1)
        vop(lambda e: e.tensor_scalar_mul(stat[0:1, 0:512], ps[4][0:1, 0:512], 1.0 / D),
            [("ps4", 0, 512)], [R("rowA", 0, 512)])
        vop(lambda e: e.tensor_scalar_mul(stat[0:1, 512:1024], ps[5][0:1, 0:512], 1.0 / D),
            [("ps5", 0, 512)], [R("rowA", 512, 1024)])
        vop(lambda e: e.tensor_mul(stat[0:1, 1024:1536], stat[0:1, 0:512], stat[0:1, 0:512]),
            [R("rowA", 0, 512)], [R("rowA", 1024, 1536)])
        vop(lambda e: e.tensor_sub(stat[0:1, 512:1024], stat[0:1, 512:1024],
                                   stat[0:1, 1024:1536]),
            [R("rowA", 512, 1024), R("rowA", 1024, 1536)], [R("rowA", 512, 1024)])
        sop(lambda e: e.activation(stat[0:1, 512:1024], stat[0:1, 512:1024], AF.Ln,
                                   bias=cols[0:1, CI["lneps"]:CI["lneps"] + 1]),
            [R("rowA", 512, 1024), R("cols", CI["lneps"], CI["lneps"] + 1)],
            [R("rowA", 512, 1024)])
        sop(lambda e: e.activation(stat[0:1, 512:1024], stat[0:1, 512:1024], AF.Exp,
                                   scale=-0.5),
            [R("rowA", 512, 1024)], [R("rowA", 512, 1024)])
        vop(lambda e: e.tensor_mul(stat[0:1, 0:512], stat[0:1, 0:512], stat[0:1, 512:1024]),
            [R("rowA", 0, 512), R("rowA", 512, 1024)], [R("rowA", 0, 512)])
        top(lambda e: e.matmul(ps[4][:, 0:512], onesf[0:1, 0:128], stat[0:1, 512:1024],
                               start=True, stop=True),
            [R("rowA", 512, 1024), R("onesf", 0, 128)], [("ps4", 0, 512)])
        top(lambda e: e.matmul(ps[5][:, 0:512], onesf[0:1, 0:128], stat[0:1, 0:512],
                               start=True, stop=True),
            [R("rowA", 0, 512), R("onesf", 0, 128)], [("ps5", 0, 512)])
        for k in range(16):
            vop(lambda e, k=k: e.tensor_mul(bigE[:, 0:512], bigD[:, k * 512:(k + 1) * 512],
                                            ps[4][:, 0:512]),
                [(Dk, k * 512, (k + 1) * 512), ("ps4", 0, 512)], [(E, 0, 512)])
            vop(lambda e, k=k: e.tensor_sub(dst_t[:, dst_off + k * 512: dst_off + (k + 1) * 512],
                                            bigE[:, 0:512], ps[5][:, 0:512]),
                [(E, 0, 512), ("ps5", 0, 512)],
                [(dst_key, dst_off + k * 512, dst_off + (k + 1) * 512)])

    def ffn(src_hbm, src_key, dst_hbm, dst_key):
        TOFF, GCT, OM = 16384, 26624, 34816
        for blk in range(NTOK // FB):
            t0 = blk * FB
            layernorm(src_hbm, src_key, t0, bigA, TOFF, A)
            for k in range(16):
                sop(lambda e, k=k: e.activation(bigA[:, GCT + k * 512: GCT + (k + 1) * 512],
                                                bigA[:, TOFF + k * 512: TOFF + (k + 1) * 512],
                                                AF.Gelu),
                    [(A, TOFF + k * 512, TOFF + (k + 1) * 512)],
                    [(A, GCT + k * 512, GCT + (k + 1) * 512)])
            for pp in range(4):
                slot = pp % 2
                dkey = (Bk, slot * 4096, slot * 4096 + 4096)
                load_panel("L0_ffn_down_w", pp, bigB, slot * 4096, dkey)
                for half in range(2):
                    j = pp * 2 + half
                    bank = j % 2
                    for k in range(16):
                        top(lambda e, slot=slot, half=half, k=k, bank=bank: e.matmul(
                                ps[bank][:, 0:512],
                                bigB[:, slot * 4096 + k * 256 + half * 128:
                                     slot * 4096 + k * 256 + half * 128 + 128],
                                bigA[:, TOFF + k * 512: TOFF + (k + 1) * 512],
                                start=(k == 0), stop=(k == 15)),
                            [dkey, (A, TOFF + k * 512, TOFF + (k + 1) * 512)],
                            [(f"ps{bank}", 0, 512)] if k == 0 else [],
                            group_n=16 if k == 0 else 1)
                    bcol = CI["down_b"] + j
                    sop(lambda e, j=j, bank=bank, bcol=bcol: e.activation(
                            bigC[:, j * 512:(j + 1) * 512], ps[bank][:, 0:512], AF.Gelu,
                            bias=cols[:, bcol:bcol + 1]),
                        [(f"ps{bank}", 0, 512), R("cols", bcol, bcol + 1)],
                        [(C, j * 512, (j + 1) * 512)])
            rkey = (Bk, 0, 128)
            load_panel("L0_ffn_router_w", 0, bigB, 0, rkey)
            for sub in range(4):
                for k in range(16):
                    top(lambda e, sub=sub, k=k: e.matmul(
                            ps[2][:, 0:8],
                            bigA[:, TOFF + k * 512 + sub * 128: TOFF + k * 512 + sub * 128 + 128],
                            bigB[:, k * 8: k * 8 + 8],
                            start=(k == 0), stop=False),
                        [rkey, (A, TOFF + k * 512 + sub * 128,
                                TOFF + k * 512 + sub * 128 + 128)],
                        [("ps2", 0, 512)] if k == 0 else [],
                        group_n=17 if k == 0 else 1)
                top(lambda e: e.matmul(ps[2][:, 0:8], onesb[0:1, 0:128],
                                       rowsb[0:1, 3072:3080], start=False, stop=True),
                    [R("rowsb", 3072, 3080), R("onesb", 0, 128)], [])
                vop(lambda e: e.reduce_max(prob[:, 8:9], ps[2][:, 0:8], axis=AX, negate=True),
                    [("ps2", 0, 512)], [R("prob", 8, 9)])
                sop(lambda e: e.activation(prob[:, 0:8], ps[2][:, 0:8], AF.Exp,
                                           bias=prob[:, 8:9], accum_out=prob[:, 9:10]),
                    [("ps2", 0, 512), R("prob", 8, 9)],
                    [R("prob", 0, 8), R("prob", 9, 10)])
                vop(lambda e: e.reciprocal(prob[:, 10:11], prob[:, 9:10]),
                    [R("prob", 9, 10)], [R("prob", 10, 11)])
                vop(lambda e: e.tensor_scalar_mul(prob[:, 0:8], prob[:, 0:8], prob[:, 10:11]),
                    [R("prob", 0, 8), R("prob", 10, 11)], [R("prob", 0, 8)])
                vop(lambda e: e.max(prob[:, 16:24], prob[:, 0:8]),
                    [R("prob", 0, 8)], [R("prob", 16, 24)])
                vop(lambda e: e.memset(prob[:, 24:32], NEG_BIG), [], [R("prob", 24, 32)])
                vop(lambda e: e.tensor_copy(prob[:, 24:25], prob[:, 16:17]),
                    [R("prob", 16, 24)], [R("prob", 24, 32)])
                vop(lambda e: e.match_replace(prob[:, 32:40], prob[:, 24:32], prob[:, 0:8],
                                              NEG_BIG),
                    [R("prob", 24, 32), R("prob", 0, 8)], [R("prob", 32, 40)])
                vop(lambda e: e.tensor_tensor(prob[:, 40:48], prob[:, 0:8], prob[:, 32:40],
                                              ALU.not_equal),
                    [R("prob", 0, 8), R("prob", 32, 40)], [R("prob", 40, 48)])
                vop(lambda e: e.tensor_copy(prob[:, 24:25], prob[:, 17:18]),
                    [R("prob", 16, 24)], [R("prob", 24, 32)])
                vop(lambda e: e.match_replace(prob[:, 48:56], prob[:, 24:32], prob[:, 32:40],
                                              NEG_BIG),
                    [R("prob", 24, 32), R("prob", 32, 40)], [R("prob", 48, 56)])
                vop(lambda e: e.tensor_tensor(prob[:, 56:64], prob[:, 32:40], prob[:, 48:56],
                                              ALU.not_equal),
                    [R("prob", 32, 40), R("prob", 48, 56)], [R("prob", 56, 64)])
                vop(lambda e: e.tensor_sub(prob[:, 11:12], prob[:, 17:18], prob[:, 16:17]),
                    [R("prob", 16, 24)], [R("prob", 11, 12)])
                sop(lambda e: e.activation(prob[:, 12:13], prob[:, 11:12], AF.Exp),
                    [R("prob", 11, 12)], [R("prob", 12, 13)])
                vop(lambda e: e.tensor_scalar_add(prob[:, 13:14], prob[:, 12:13], 1.0),
                    [R("prob", 12, 13)], [R("prob", 13, 14)])
                vop(lambda e: e.reciprocal(prob[:, 13:14], prob[:, 13:14]),
                    [R("prob", 13, 14)], [R("prob", 13, 14)])
                vop(lambda e: e.tensor_mul(prob[:, 14:15], prob[:, 12:13], prob[:, 13:14]),
                    [R("prob", 12, 13), R("prob", 13, 14)], [R("prob", 14, 15)])
                vop(lambda e: e.tensor_scalar_mul(prob[:, 40:48], prob[:, 40:48],
                                                  prob[:, 13:14]),
                    [R("prob", 40, 48), R("prob", 13, 14)], [R("prob", 40, 48)])
                vop(lambda e, sub=sub: e.scalar_tensor_tensor(
                        w8f[:, sub * 8:(sub + 1) * 8], prob[:, 56:64], prob[:, 14:15],
                        prob[:, 40:48], ALU.mult, ALU.add),
                    [R("prob", 56, 64), R("prob", 40, 48), R("prob", 14, 15)],
                    [R("w8f", sub * 8, (sub + 1) * 8)])
                top(lambda e, sub=sub: e.transpose(ps[3][0:8, sub * 128:(sub + 1) * 128],
                                                   w8f[:, sub * 8:(sub + 1) * 8],
                                                   identf[:, :]),
                    [R("w8f", sub * 8, (sub + 1) * 8), R("identf", 0, 128)],
                    [("ps3", 0, 512)])
            vop(lambda e: e.tensor_copy(w8tb[0:8, 0:512], ps[3][0:8, 0:512]),
                [("ps3", 0, 512)], [R("w8tb", 0, 512)])
            for ee in range(9):
                h = HID[ee] if ee < 8 else L
                nht = h // 128
                w1n = f"L0_ffn_e{ee}_w1" if ee < 8 else "L0_ffn_shared_w1"
                w2n = f"L0_ffn_e{ee}_w2" if ee < 8 else "L0_ffn_shared_w2"
                if ee < 8:
                    top(lambda e, ee=ee: e.matmul(ps[2][:, 0:512],
                                                  selt[0:8, ee * 128:(ee + 1) * 128],
                                                  w8tb[0:8, 0:512], start=True, stop=True),
                        [R("selt", ee * 128, (ee + 1) * 128), R("w8tb", 0, 512)],
                        [("ps2", 0, 512)])
                    vop(lambda e: e.tensor_copy(wbt[:, 0:512], ps[2][:, 0:512]),
                        [("ps2", 0, 512)], [R("wbt", 0, 512)])
                for pp in range(h // PANEL):
                    slot = pp % 2
                    dkey = (Bk, slot * 4096, slot * 4096 + 2048)
                    load_panel(w1n, pp, bigB, slot * 4096, dkey)
                    for half in range(2):
                        hj = pp * 2 + half
                        bank = hj % 2
                        for k in range(8):
                            top(lambda e, slot=slot, half=half, k=k, bank=bank: e.matmul(
                                    ps[bank][:, 0:512],
                                    bigB[:, slot * 4096 + k * 256 + half * 128:
                                         slot * 4096 + k * 256 + half * 128 + 128],
                                    bigC[:, k * 512:(k + 1) * 512],
                                    start=(k == 0), stop=(k == 7)),
                                [dkey, (C, k * 512, (k + 1) * 512)],
                                [(f"ps{bank}", 0, 512)] if k == 0 else [],
                                group_n=8 if k == 0 else 1)
                        gsl = (A, hj * 512, (hj + 1) * 512)
                        bcol = CI["b1"][ee] + hj
                        sop(lambda e, hj=hj, bank=bank, bcol=bcol: e.activation(
                                bigA[:, hj * 512:(hj + 1) * 512], ps[bank][:, 0:512],
                                AF.Gelu, bias=gb1c_raw[:, bcol:bcol + 1]),
                            [(f"ps{bank}", 0, 512), R("gb1c_raw", bcol, bcol + 1)], [gsl])
                        if ee < 8:
                            vop(lambda e, hj=hj: e.tensor_mul(
                                    bigA[:, hj * 512:(hj + 1) * 512],
                                    bigA[:, hj * 512:(hj + 1) * 512], wbt[:, 0:512]),
                                [gsl, R("wbt", 0, 512)], [gsl])
                w2pw = panelw(w2n)
                for pp in range(L // w2pw):
                    slot = pp % 2
                    pkey = (Bk, slot * 4096, slot * 4096 + nht * w2pw)
                    load_panel(w2n, pp, bigB, slot * 4096, pkey)
                    for half in range(w2pw // 128):
                        j = pp * (w2pw // 128) + half
                        bank = 2 + (j % 2)
                        for k in range(nht):
                            top(lambda e, slot=slot, half=half, k=k, bank=bank,
                                       w2pw=w2pw, nht=nht: e.matmul(
                                    ps[bank][:, 0:512],
                                    bigB[:, slot * 4096 + k * w2pw + half * 128:
                                         slot * 4096 + k * w2pw + half * 128 + 128],
                                    bigA[:, k * 512:(k + 1) * 512],
                                    start=(k == 0), stop=(k == nht - 1)),
                                [pkey, (A, k * 512, (k + 1) * 512)],
                                [(f"ps{bank}", 0, 512)] if k == 0 else [],
                                group_n=nht if k == 0 else 1)
                        if ee == 0:
                            vop(lambda e, j=j, bank=bank: e.tensor_copy(
                                    bigE[:, j * 512:(j + 1) * 512], ps[bank][:, 0:512]),
                                [(f"ps{bank}", 0, 512)], [(E, j * 512, (j + 1) * 512)])
                        else:
                            vop(lambda e, j=j, bank=bank: e.tensor_add(
                                    bigE[:, j * 512:(j + 1) * 512],
                                    bigE[:, j * 512:(j + 1) * 512], ps[bank][:, 0:512]),
                                [(f"ps{bank}", 0, 512), (E, j * 512, (j + 1) * 512)],
                                [(E, j * 512, (j + 1) * 512)])
            for j in range(8):
                top(lambda e, j=j: e.matmul(ps[2][:, 0:512], D8pm[0:8, j * 128:(j + 1) * 128],
                                            w8tb[0:8, 0:512], start=True, stop=False),
                    [R("D8pm", j * 128, (j + 1) * 128), R("w8tb", 0, 512)],
                    [("ps2", 0, 512)], group_n=2)
                top(lambda e, j=j: e.matmul(ps[2][:, 0:512],
                                            rowsb[0:1, 2048 + j * 128: 2048 + (j + 1) * 128],
                                            onesb[0:1, 0:512], start=False, stop=True),
                    [R("rowsb", 2048 + j * 128, 2048 + (j + 1) * 128), R("onesb", 0, 512)],
                    [])
                vop(lambda e, j=j: e.tensor_add(bigA[:, OM + j * 512: OM + (j + 1) * 512],
                                                bigE[:, j * 512:(j + 1) * 512],
                                                ps[2][:, 0:512]),
                    [(E, j * 512, (j + 1) * 512), ("ps2", 0, 512)],
                    [(A, OM + j * 512, OM + (j + 1) * 512)])
            for grp in range(16):
                j = grp
                slot = grp % 2
                bank = grp % 2
                ukey = (Bk, slot * 4096, slot * 4096 + 1024)
                ckey = (Bk, slot * 4096 + 1024, slot * 4096 + 3072)
                load_panel("L0_ffn_up_w", grp, bigB, slot * 4096, ukey)
                load_panel("L0_ffn_core_w", grp, bigB, slot * 4096 + 1024, ckey)
                for k in range(8):
                    top(lambda e, slot=slot, k=k, bank=bank: e.matmul(
                            ps[bank][:, 0:512],
                            bigB[:, slot * 4096 + k * 128: slot * 4096 + k * 128 + 128],
                            bigA[:, OM + k * 512: OM + (k + 1) * 512],
                            start=(k == 0), stop=False),
                        [ukey, (A, OM + k * 512, OM + (k + 1) * 512)],
                        [(f"ps{bank}", 0, 512)] if k == 0 else [],
                        group_n=25 if k == 0 else 1)
                for k in range(16):
                    top(lambda e, slot=slot, k=k, bank=bank: e.matmul(
                            ps[bank][:, 0:512],
                            bigB[:, slot * 4096 + 1024 + k * 128:
                                 slot * 4096 + 1024 + k * 128 + 128],
                            bigA[:, GCT + k * 512: GCT + (k + 1) * 512],
                            start=False, stop=False),
                        [ckey, (A, GCT + k * 512, GCT + (k + 1) * 512)], [])
                top(lambda e, j=j, bank=bank: e.matmul(
                        ps[bank][:, 0:512], rowsb[0:1, j * 128:(j + 1) * 128],
                        onesb[0:1, 0:512], start=False, stop=True),
                    [R("rowsb", j * 128, (j + 1) * 128), R("onesb", 0, 512)], [])
                xk = ("xfst", 0, 512)
                dma("y", xfst[:, 0:512],
                    src_hbm[j * 128:(j + 1) * 128, t0:t0 + 512],
                    [xiv(src_key, j, t0)], [xk])
                ykey = ("xfst", 512, 1024)
                vop(lambda e, bank=bank: e.scalar_tensor_tensor(
                        xfst[:, 512:1024], ps[bank][:, 0:512], 0.5,
                        xfst[:, 0:512], ALU.mult, ALU.add),
                    [(f"ps{bank}", 0, 512), xk], [ykey])
                dma("y", dst_hbm[j * 128:(j + 1) * 128, t0:t0 + 512],
                    xfst[:, 512:1024], [ykey], [xiv(dst_key, j, t0)])


    def attn(src_hbm, src_key, dst_hbm, dst_key):
        fslot_ctr = [0]

        def fslot():
            i = fslot_ctr[0] % 20
            fslot_ctr[0] += 1
            lo = 1536 + i * 128
            return bigE[:, lo:lo + 128], (E, lo, lo + 128)

        bslot_ctr = [0]

        def bslot():
            i = bslot_ctr[0] % 16
            bslot_ctr[0] += 1
            lo = 1024 + i * 128
            return bigB[:, lo:lo + 128], (Bk, lo, lo + 128)

        nslot_ctr = [0]

        def nslot():
            i = nslot_ctr[0] % 40
            nslot_ctr[0] += 1
            lo = 3072 + i * 128
            return bigB[:, lo:lo + 128], (Bk, lo, lo + 128)

        def nslot2():
            if nslot_ctr[0] % 2 == 1:
                nslot_ctr[0] += 1
            i = nslot_ctr[0] % 40
            nslot_ctr[0] += 2
            lo = 3072 + i * 128
            return bigB[:, lo:lo + 256], (Bk, lo, lo + 256)

        pslot_ctr = [0]

        def pslot():
            i = pslot_ctr[0] % 16
            pslot_ctr[0] += 1
            t = pbf[i // 8]
            lo = (i % 8) * 128
            return t[:, lo:lo + 128], (PB[i // 8], 0, 1024)

        def qtile(kh, c):
            lo = kh * 512 + c * 128
            return bigA[:, lo:lo + 128], (A, lo, lo + 128)

        def ktile(kh, c):
            lo = (16 + kh) * 512 + c * 128
            return bigA[:, lo:lo + 128], (A, lo, lo + 128)

        def vtile(h, c):
            lo = (32 + h) * 512 + c * 128
            return bigA[:, lo:lo + 128], (A, lo, lo + 128)

        for b in range(NBATCH):
            vop(lambda e: e.memset(S_sb[:, :], 0.0), [], [R("S_sb", 0, 4096)])
            vop(lambda e: e.memset(hist[:, :], 0.0), [], [R("hist", 0, 192)])
            for sc in range(SEQ // SC):
                t0 = b * SEQ + sc * SC
                layernorm(src_hbm, src_key, t0, bigC, 0, C)
                for pp in range(96):
                    slot = pp % 2
                    dkey = (Bk, slot * 4096, slot * 4096 + 2048)
                    load_panel("L0_attn_W_qkvz", pp, bigB, slot * 4096, dkey)
                    if True:
                        n = pp
                        bank = n % 2
                        for k in range(16):
                            top(lambda e, slot=slot, k=k, bank=bank: e.matmul(
                                    ps[bank][:, 0:512],
                                    bigB[:, slot * 4096 + k * 128:
                                         slot * 4096 + k * 128 + 128],
                                    bigC[:, k * 512:(k + 1) * 512],
                                    start=(k == 0), stop=(k == 15)),
                                [dkey, (C, k * 512, (k + 1) * 512)],
                                [(f"ps{bank}", 0, 512)] if k == 0 else [],
                                group_n=16 if k == 0 else 1)
                        if n < 64:
                            vop(lambda e, n=n: e.tensor_copy(bigE[:, 0:3],
                                                             hist[:, n * 3:(n + 1) * 3]),
                                [R("hist", n * 3, (n + 1) * 3)], [(E, 0, 3)])
                            vop(lambda e, bank=bank: e.tensor_copy(bigE[:, 3:515],
                                                                   ps[bank][:, 0:512]),
                                [(f"ps{bank}", 0, 512)], [(E, 3, 515)])
                            vop(lambda e, n=n: e.tensor_copy(hist[:, n * 3:(n + 1) * 3],
                                                             bigE[:, 512:515]),
                                [(E, 512, 515)], [R("hist", n * 3, (n + 1) * 3)])
                            c0 = CI["conv"][(n, 0)]
                            vop(lambda e, c0=c0: e.tensor_scalar_mul(
                                    bigE[:, 1536:2048], bigE[:, 0:512],
                                    cols[:, c0:c0 + 1]),
                                [(E, 0, 515), R("cols", c0, c0 + 1)], [(E, 1536, 2048)])
                            for j in range(1, 4):
                                cj = CI["conv"][(n, j)]
                                vop(lambda e, j=j, cj=cj: e.scalar_tensor_tensor(
                                        bigE[:, 1536:2048], bigE[:, j:j + 512],
                                        cols[:, cj:cj + 1], bigE[:, 1536:2048],
                                        ALU.mult, ALU.add),
                                    [(E, 0, 515), R("cols", cj, cj + 1), (E, 1536, 2048)],
                                    [(E, 1536, 2048)])
                            sop(lambda e, n=n: e.activation(
                                    bigA[:, n * 512:(n + 1) * 512], bigE[:, 1536:2048],
                                    AF.Silu),
                                [(E, 1536, 2048)], [(A, n * 512, (n + 1) * 512)])
                        else:
                            zt = n - 64
                            sop(lambda e, bank=bank: e.activation(
                                    wbt[:, 0:512], ps[bank][:, 0:512], AF.Silu),
                                [(f"ps{bank}", 0, 512)], [R("wbt", 0, 512)])
                            dma("y", zscr[zt * 65536:(zt + 1) * 65536]
                                .rearrange("(p j) -> p j", p=128), wbt[:, 0:512],
                                [R("wbt", 0, 512)], [("zscr", zt * 65536, (zt + 1) * 65536)])
                bakey = (Bk, 0, 1024)
                load_panel("L0_attn_W_ba", 0, bigB, 0, bakey)
                for k in range(16):
                    top(lambda e, k=k: e.matmul(
                            ps[2][0:64, 0:512], bigB[:, k * 64: k * 64 + 64],
                            bigC[:, k * 512:(k + 1) * 512],
                            start=(k == 0), stop=(k == 15)),
                        [bakey, (C, k * 512, (k + 1) * 512)],
                        [("ps2", 0, 512)] if k == 0 else [], group_n=16 if k == 0 else 1)
                vop(lambda e: e.tensor_copy(bigE[0:64, 0:512], ps[2][0:64, 0:512]),
                    [("ps2", 0, 512)], [(E, 0, 512)])
                for c in range(NCH):
                    tb = c * 256
                    top(lambda e, c=c: e.transpose(ps[3][:, 0:64],
                                                   bigE[0:64, c * 128:(c + 1) * 128],
                                                   identf[0:64, 0:64]),
                        [(E, 0, 512)], [("ps3", 0, 512)])
                    for par in range(2):
                        vop(lambda e, par=par: e.tensor_copy(
                                bass.AP(bigE, 1408 + par, [[4096, 128], [2, 16]]),
                                bass.AP(ps[3], par, [[512, 128], [4, 16]])),
                            [("ps3", 0, 512)], [(E, 1408, 1440)])
                        vop(lambda e, par=par: e.tensor_copy(
                                bass.AP(bigE, 1440 + par, [[4096, 128], [2, 16]]),
                                bass.AP(ps[3], 2 + par, [[512, 128], [4, 16]])),
                            [("ps3", 0, 512)], [(E, 1440, 1472)])
                    sop(lambda e, tb=tb: e.activation(tokc[:, tb:tb + 32],
                                                      bigE[:, 1408:1440], AF.Sigmoid),
                        [(E, 1408, 1440)], [R("tokc", tb, tb + 32)])
                    vop(lambda e, tb=tb: e.tensor_add(bigE[:, 1536:1568],
                                                      bigE[:, 1440:1472], cols[:, 320:352]),
                        [(E, 1440, 1472), R("cols", 320, 352)], [(E, 1536, 1568)])
                    sop(lambda e: e.activation(bigE[:, 1568:1600], bigE[:, 1536:1568],
                                               AF.Abs),
                        [(E, 1536, 1568)], [(E, 1568, 1600)])
                    sop(lambda e: e.activation(bigE[:, 1600:1632], bigE[:, 1568:1600],
                                               AF.Exp, scale=-1.0),
                        [(E, 1568, 1600)], [(E, 1600, 1632)])
                    sop(lambda e: e.activation(bigE[:, 1600:1632], bigE[:, 1600:1632],
                                               AF.Ln, bias=cols[:, CI["one"]:CI["one"] + 1]),
                        [(E, 1600, 1632), R("cols", CI["one"], CI["one"] + 1)],
                        [(E, 1600, 1632)])
                    vop(lambda e: e.tensor_scalar_max(bigE[:, 1536:1568],
                                                      bigE[:, 1536:1568], 0.0),
                        [(E, 1536, 1568)], [(E, 1536, 1568)])
                    vop(lambda e: e.tensor_add(bigE[:, 1536:1568], bigE[:, 1536:1568],
                                               bigE[:, 1600:1632]),
                        [(E, 1536, 1632)], [(E, 1536, 1568)])
                    vop(lambda e: e.tensor_mul(bigE[:, 1568:1600], bigE[:, 1536:1568],
                                               cols[:, 352:384]),
                        [(E, 1536, 1568), R("cols", 352, 384)], [(E, 1568, 1600)])
                    top(lambda e: e.transpose(ps[4][0:32, 0:128], bigE[:, 1568:1600],
                                              identf[:, :]),
                        [(E, 1568, 1600)], [("ps4", 0, 512)])
                    vop(lambda e: e.tensor_copy(bg2[0:32, 0:128], ps[4][0:32, 0:128]),
                        [("ps4", 0, 512)], [R("bg2", 0, 128)])
                    vop(lambda e: e.tensor_tensor_scan(bg2[0:32, 128:256],
                                                       bg2[0:32, 0:128],
                                                       bg2[0:32, 0:128], 0.0,
                                                       ALU.add, ALU.bypass),
                        [R("bg2", 0, 128)], [R("bg2", 128, 256)])
                    top(lambda e: e.transpose(ps[4][:, 128:160], bg2[0:32, 128:256],
                                              identf[0:32, 0:32]),
                        [R("bg2", 128, 256)], [("ps4", 0, 512)])
                    vop(lambda e, tb=tb: e.tensor_copy(tokc[:, tb + 32:tb + 64],
                                                       ps[4][:, 128:160]),
                        [("ps4", 0, 512)], [R("tokc", tb + 32, tb + 64)])
                    vop(lambda e, tb=tb: e.tensor_scalar_mul(tokc[:, tb + 64:tb + 96],
                                                             tokc[:, tb:tb + 32], -1.0),
                        [R("tokc", tb, tb + 32)], [R("tokc", tb + 64, tb + 96)])
                    sop(lambda e, tb=tb: e.activation(tokc[:, tb + 96:tb + 128],
                                                      tokc[:, tb + 32:tb + 64], AF.Exp),
                        [R("tokc", tb + 32, tb + 64)], [R("tokc", tb + 96, tb + 128)])
                    vop(lambda e: e.tensor_scalar_mul(bigE[0:32, 0:128],
                                                      ones2d[0:32, 0:128],
                                                      bg2[0:32, 255:256]),
                        [R("ones2d", 0, 128), R("bg2", 128, 256)], [(E, 0, 128)])
                    top(lambda e: e.transpose(ps[4][:, 160:192], bigE[0:32, 0:128],
                                              identf[0:32, 0:32]),
                        [(E, 0, 128)], [("ps4", 0, 512)])
                    sop(lambda e, tb=tb: e.activation(tokc[:, tb + 160:tb + 192],
                                                      ps[4][:, 160:192], AF.Exp),
                        [("ps4", 0, 512)], [R("tokc", tb + 160, tb + 192)])
                    vop(lambda e, tb=tb: e.tensor_sub(bigE[:, 1536:1568],
                                                      ps[4][:, 160:192],
                                                      tokc[:, tb + 32:tb + 64]),
                        [("ps4", 0, 512), R("tokc", tb + 32, tb + 64)], [(E, 1536, 1568)])
                    sop(lambda e, tb=tb: e.activation(tokc[:, tb + 128:tb + 160],
                                                      bigE[:, 1536:1568], AF.Exp),
                        [(E, 1536, 1568)], [R("tokc", tb + 128, tb + 160)])
                    sop(lambda e, tb=tb: e.activation(bigE[:, 1536:1568],
                                                      tokc[:, tb:tb + 32], AF.Ln),
                        [R("tokc", tb, tb + 32)], [(E, 1536, 1568)])
                    vop(lambda e, tb=tb: e.tensor_sub(tokc[:, tb + 192:tb + 224],
                                                      bigE[:, 1536:1568],
                                                      tokc[:, tb + 32:tb + 64]),
                        [(E, 1536, 1568), R("tokc", tb + 32, tb + 64)],
                        [R("tokc", tb + 192, tb + 224)])
                for kt in range(32):
                    fac = (1.0 / DK) if kt < 16 else (DK ** -0.5)
                    tkey = (A, kt * 512, (kt + 1) * 512)
                    vop(lambda e, kt=kt: e.tensor_mul(wbt[:, 0:512],
                                                      bigA[:, kt * 512:(kt + 1) * 512],
                                                      bigA[:, kt * 512:(kt + 1) * 512]),
                        [tkey], [R("wbt", 0, 512)])
                    top(lambda e: e.matmul(ps[4][0:1, 0:512], onescb[:, 0:1],
                                           wbt[:, 0:512], start=True, stop=True),
                        [R("wbt", 0, 512), R("onescb", 0, 1)], [("ps4", 0, 512)])
                    vop(lambda e: e.tensor_scalar_mul(rowB[0:1, 0:512],
                                                      ps[4][0:1, 0:512], 1.0 / DK),
                        [("ps4", 0, 512)], [R("rowB", 0, 512)])
                    sop(lambda e: e.activation(rowB[0:1, 0:512], rowB[0:1, 0:512],
                                               AF.Ln, bias=cols[0:1, CI["rmseps"]:CI["rmseps"] + 1]),
                        [R("rowB", 0, 512), R("cols", CI["rmseps"], CI["rmseps"] + 1)],
                        [R("rowB", 0, 512)])
                    sop(lambda e: e.activation(rowB[0:1, 0:512], rowB[0:1, 0:512],
                                               AF.Exp, scale=-0.5),
                        [R("rowB", 0, 512)], [R("rowB", 0, 512)])
                    vop(lambda e, fac=fac: e.tensor_scalar_mul(rowB[0:1, 0:512],
                                                               rowB[0:1, 0:512], fac),
                        [R("rowB", 0, 512)], [R("rowB", 0, 512)])
                    top(lambda e: e.matmul(ps[5][:, 0:512], onesf[0:1, 0:128],
                                           rowB[0:1, 0:512], start=True, stop=True),
                        [R("rowB", 0, 512), R("onesf", 0, 128)], [("ps5", 0, 512)])
                    vop(lambda e, kt=kt: e.tensor_mul(bigA[:, kt * 512:(kt + 1) * 512],
                                                      bigA[:, kt * 512:(kt + 1) * 512],
                                                      ps[5][:, 0:512]),
                        [tkey, ("ps5", 0, 512)], [tkey])
                def mk_ccol(c):
                    tb = c * 256
                    return (lambda off, hh: tokc[:, tb + off + hh: tb + off + hh + 1],
                            lambda off, hh: ("tokc", tb + off + hh, tb + off + hh + 1))

                if True:
                    def r1_pair(hp, c, ccol_, ccolk, kksb):
                        kh = hp
                        kpar = kh % 2
                        kap, kkey = ktile(kh, c)
                        qap, qkey = qtile(kh, c)
                        top(lambda e, kap=kap, kpar=kpar: e.matmul(
                                ps[kpar][:, 256:384], kap, kap,
                                start=True, stop=True),
                            [kkey], [(f"ps{kpar}", 0, 512)])
                        yield
                        top(lambda e, kap=kap, qap=qap, kpar=kpar: e.matmul(
                                ps[kpar][:, 384:512], kap, qap,
                                start=True, stop=True),
                            [kkey, qkey], [(f"ps{kpar}", 0, 512)])
                        yield
                        kkt, kktk = nslot()
                        kqt, kqtk = nslot()
                        sop(lambda e, kkt=kkt, kpar=kpar: e.copy(
                                kkt, ps[kpar][:, 256:384]),
                            [(f"ps{kpar}", 0, 512)], [kktk])
                        yield
                        vop(lambda e, kqt=kqt, kpar=kpar: e.tensor_copy(
                                kqt, ps[kpar][:, 384:512]),
                            [(f"ps{kpar}", 0, 512)], [kqtk])
                        yield
                        pb, pbk = pslot()
                        top(lambda e, kap=kap, pb=pb: e.transpose(pb, kap, identb[:, :]),
                            [kkey, R("identb", 0, 128)], [pbk])
                        yield
                        ktlo = 4096 + (c % 2) * 2048 + kh * 128
                        vop(lambda e, pb=pb, ktlo=ktlo: e.tensor_copy(
                                bigD[:, ktlo: ktlo + 128], pb),
                            [pbk], [(Dk, ktlo, ktlo + 128)])
                        yield
                        ktokk = (Dk, ktlo, ktlo + 128)
                        for h in (2 * hp, 2 * hp + 1):
                            par = hp % 2
                            vap, vkey = vtile(h, c)
                            sV1, kV1 = fslot()
                            cap_cum = ccol_(32, h)
                            vop(lambda e, sV1=sV1, cap=cap_cum: e.tensor_scalar_mul(
                                    sV1, ones2d[:, :], cap),
                                [R("ones2d", 0, 128), ccolk(32, h)], [kV1])
                            yield
                            top(lambda e, sV1=sV1, par=par: e.transpose(
                                    ps[par][:, 0:128], sV1, identf[:, :]),
                                [kV1, R("identf", 0, 128)], [(f"ps{par}", 0, 512)])
                            yield
                            sM, kM = fslot()
                            vop(lambda e, sM=sM, cap=cap_cum, par=par: e.tensor_scalar(
                                    sM, ps[par][:, 0:128], cap, 0.0,
                                    ALU.subtract, ALU.min),
                                [(f"ps{par}", 0, 512), ccolk(32, h)], [kM])
                            yield
                            sE, kE = fslot()
                            sop(lambda e, sM=sM, sE=sE: e.activation(sE, sM, AF.Exp),
                                [kM], [kE])
                            yield
                            sV2, kV2 = fslot()
                            cap_cmb = ccol_(192, h)
                            vop(lambda e, sV2=sV2, cap=cap_cmb: e.tensor_scalar_mul(
                                    sV2, ones2d[:, :], cap),
                                [R("ones2d", 0, 128), ccolk(192, h)], [kV2])
                            yield
                            top(lambda e, sV2=sV2, par=par: e.transpose(
                                    ps[par][:, 128:256], sV2, identf[:, :]),
                                [kV2, R("identf", 0, 128)], [(f"ps{par}", 0, 512)])
                            yield
                            sM2, kM2 = fslot()
                            vop(lambda e, sM2=sM2, cap=cap_cum, par=par: e.tensor_scalar(
                                    sM2, ps[par][:, 128:256], cap, 0.0,
                                    ALU.add, ALU.min),
                                [(f"ps{par}", 0, 512), ccolk(32, h)], [kM2])
                            yield
                            sE2, kE2 = fslot()
                            sop(lambda e, sM2=sM2, sE2=sE2: e.activation(sE2, sM2, AF.Exp),
                                [kM2], [kE2])
                            yield
                            sA1, kA1 = fslot()
                            vop(lambda e, sA1=sA1, sE=sE, kkt=kkt: e.tensor_mul(sA1, sE, kkt),
                                [kE, kktk], [kA1])
                            yield
                            nT, nTk = nslot()
                            cap_nb = ccol_(64, h)
                            vop(lambda e, sA1=sA1, nT=nT, cap=cap_nb: e.scalar_tensor_tensor(
                                    nT, sA1, cap, mask_t[:, :], ALU.mult, ALU.mult),
                                [kA1, ccolk(64, h), R("mask_t", 0, 128)], [nTk])
                            yield
                            sA2, kA2 = fslot()
                            vop(lambda e, sA2=sA2, sE2=sE2, kkt=kkt: e.tensor_mul(sA2, sE2, kkt),
                                [kE2, kktk], [kA2])
                            yield
                            nS, nSk = nslot()
                            vop(lambda e, sA2=sA2, nS=nS: e.tensor_mul(nS, sA2,
                                                                       mask_sn[:, :]),
                                [kA2, R("mask_sn", 0, 128)], [nSk])
                            yield
                            sB1, kB1 = fslot()
                            vop(lambda e, sB1=sB1, sE=sE, kqt=kqt: e.tensor_mul(sB1, sE, kqt),
                                [kE, kqtk], [kB1])
                            yield
                            btk = (Dk, h * 128, (h + 1) * 128)
                            vop(lambda e, sB1=sB1, h=h: e.tensor_mul(
                                    bigD[:, h * 128:(h + 1) * 128], sB1, mask_i[:, :]),
                                [kB1, R("mask_i", 0, 128)], [btk])
                            yield
                            zzlo = h * 256
                            zzk = (C, zzlo, zzlo + 256)
                            pb2, pbk2 = pslot()
                            top(lambda e, vap=vap, pb2=pb2: e.transpose(pb2, vap, identb[:, :]),
                                [vkey, R("identb", 0, 128)], [pbk2])
                            yield
                            vop(lambda e, pb2=pb2, zzlo=zzlo: e.tensor_copy(
                                    bigC[:, zzlo:zzlo + 128], pb2),
                                [pbk2], [(C, zzlo, zzlo + 128)])
                            yield
                            cap_gam = ccol_(96, h)
                            vop(lambda e, zzlo=zzlo, ktlo=ktlo, cap=cap_gam: e.tensor_scalar_mul(
                                    bigC[:, zzlo + 128:zzlo + 256],
                                    bigD[:, ktlo: ktlo + 128],
                                    cap),
                                [ktokk, ccolk(96, h)], [(C, zzlo + 128, zzlo + 256)])
                            yield
                            if h % 2 == 0:
                                pairN = [(nT, nTk, nS, nSk)]
                                kksb[kh] = pairN
                            else:
                                kksb[kh].append((nT, nTk, nS, nSk))
                        (t0n, t0k, s0n, s0k), (t1n, t1k, s1n, s1k) = kksb[kh]
                        zzlo = 2 * hp * 256
                        zzk2 = (C, zzlo, zzlo + 512)
                        ab = 2 + (hp % 2)
                        q0 = hp % 2
                        q1 = hp % 2
                        for j in range(7):
                            top(lambda e, t0n=t0n, zzlo=zzlo, ab=ab: e.matmul(
                                    ps[ab][:, 0:256], t0n,
                                    bigC[:, zzlo:zzlo + 256], start=True, stop=True),
                                [t0k, zzk2], [(f"ps{ab}", 0, 512)])
                            yield
                            top(lambda e, t1n=t1n, zzlo=zzlo, ab=ab: e.matmul(
                                    ps[ab][:, 256:512], t1n,
                                    bigC[:, zzlo + 256:zzlo + 512], start=True, stop=True),
                                [t1k, zzk2], [(f"ps{ab}", 0, 512)])
                            yield
                            vop(lambda e, zzlo=zzlo, ab=ab: e.tensor_add(
                                    bigC[:, zzlo:zzlo + 512],
                                    bigC[:, zzlo:zzlo + 512], ps[ab][:, 0:512]),
                                [(f"ps{ab}", 0, 512), zzk2], [zzk2])
                            yield
                            if j < 6:
                                top(lambda e, s0n=s0n, t0n=t0n, q0=q0: e.matmul(
                                        ps[q0][:, 0:128], s0n, t0n,
                                        start=True, stop=True),
                                    [s0k, t0k], [(f"ps{q0}", 0, 512)])
                                yield
                                top(lambda e, s1n=s1n, t1n=t1n, q0=q0: e.matmul(
                                        ps[q0][:, 128:256], s1n, t1n,
                                        start=True, stop=True),
                                    [s1k, t1k], [(f"ps{q0}", 0, 512)])
                                yield
                                top(lambda e, s0n=s0n, t0n=t0n, q1=q1: e.matmul(
                                        ps[q1][:, 256:384], t0n, s0n,
                                        start=True, stop=True),
                                    [t0k, s0k], [(f"ps{q1}", 0, 512)])
                                yield
                                top(lambda e, s1n=s1n, t1n=t1n, q1=q1: e.matmul(
                                        ps[q1][:, 384:512], t1n, s1n,
                                        start=True, stop=True),
                                    [t1k, s1k], [(f"ps{q1}", 0, 512)])
                                yield
                                tp_, tpk = nslot2()
                                sp_, spk = nslot2()
                                sop(lambda e, tp_=tp_, q0=q0: e.copy(
                                        tp_, ps[q0][:, 0:256]),
                                    [(f"ps{q0}", 0, 512)], [tpk])
                                yield
                                vop(lambda e, sp_=sp_, q1=q1: e.tensor_copy(
                                        sp_, ps[q1][:, 256:512]),
                                    [(f"ps{q1}", 0, 512)], [spk])
                                yield
                                t0n, t1n = tp_[:, 0:128], tp_[:, 128:256]
                                s0n, s1n = sp_[:, 0:128], sp_[:, 128:256]
                                t0k = t1k = tpk
                                s0k = s1k = spk

                    def r2_head(h, c, ccol_, ccolk):
                        kh = h // 2
                        qap, qkey = qtile(kh, c)
                        zzlo = h * 256
                        zzk = (C, zzlo, zzlo + 256)
                        skey = ("S_sb", h * 128, (h + 1) * 128)
                        btk = (Dk, h * 128, (h + 1) * 128)
                        ktlo = 4096 + (c % 2) * 2048 + kh * 128
                        ktokk = (Dk, ktlo, ktlo + 128)
                        fb = 4 + (h % 2)
                        ob = 4 + (h % 2)
                        pcol = 32 + (h % 4) * 4
                        zr = bigB[:, 512 + (h % 4) * 128: 512 + (h % 4 + 1) * 128]
                        zrk = (Bk, 512 + (h % 4) * 128, 512 + (h % 4 + 1) * 128)
                        dma("y", zr, zscr[h * 65536:(h + 1) * 65536]
                            .rearrange("(p j) -> p j", p=128)[:, c * 128:(c + 1) * 128],
                            [("zscr", h * 65536, (h + 1) * 65536)], [zrk])
                        yield
                        if c > 0:
                            sbf = bigB[:, (h % 4) * 128:(h % 4 + 1) * 128]
                            sbfk = (Bk, (h % 4) * 128, (h % 4 + 1) * 128)
                            vop(lambda e, sbf=sbf, h=h: e.tensor_copy(
                                    sbf, S_sb[:, h * 128:(h + 1) * 128]),
                                [skey], [sbfk])
                            yield
                            ek, ekk = bslot()
                            pb3, pbk3 = pslot()
                            top(lambda e, pb3=pb3, zzlo=zzlo: e.transpose(
                                    pb3, bigC[:, zzlo + 128:zzlo + 256], identb[:, :]),
                                [zzk, R("identb", 0, 128)], [pbk3])
                            yield
                            vop(lambda e, ek=ek, pb3=pb3: e.tensor_copy(ek, pb3),
                                [pbk3], [ekk])
                            yield
                            top(lambda e, ek=ek, sbf=sbf, fb=fb: e.matmul(
                                    ps[fb][:, 0:128], ek, sbf, start=True, stop=True),
                                [ekk, sbfk], [(f"ps{fb}", 0, 512)])
                            yield
                            top(lambda e, qap=qap, sbf=sbf, fb=fb: e.matmul(
                                    ps[fb][:, 128:256], qap, sbf, start=True, stop=True),
                                [qkey, sbfk], [(f"ps{fb}", 0, 512)])
                            yield
                        sT, kT = fslot()
                        if c > 0:
                            vop(lambda e, sT=sT, zzlo=zzlo, fb=fb: e.tensor_sub(
                                    sT, bigC[:, zzlo:zzlo + 128], ps[fb][:, 0:128]),
                                [zzk, (f"ps{fb}", 0, 512)], [kT])
                        else:
                            vop(lambda e, sT=sT, zzlo=zzlo: e.tensor_copy(
                                    sT, bigC[:, zzlo:zzlo + 128]),
                                [zzk], [kT])
                        yield
                        dl, dlk = bslot()
                        cap_beta = ccol_(0, h)
                        vop(lambda e, dl=dl, sT=sT, cap=cap_beta: e.tensor_scalar_mul(
                                dl, sT, cap),
                            [kT, ccolk(0, h)], [dlk])
                        yield
                        top(lambda e, dl=dl, h=h, ob=ob: e.matmul(
                                ps[ob][:, 384:512], bigD[:, h * 128:(h + 1) * 128], dl,
                                start=True, stop=True),
                            [btk, dlk], [(f"ps{ob}", 0, 512)])
                        yield
                        sO, kO = fslot()
                        if c > 0:
                            cap_gam2 = ccol_(96, h)
                            vop(lambda e, sO=sO, cap=cap_gam2, fb=fb: e.tensor_scalar_mul(
                                    sO, ps[fb][:, 128:256], cap),
                                [(f"ps{fb}", 0, 512), ccolk(96, h)], [kO])
                            yield
                            vop(lambda e, sO=sO, ob=ob: e.tensor_add(
                                    sO, sO, ps[ob][:, 384:512]),
                                [(f"ps{ob}", 0, 512), kO], [kO])
                        else:
                            vop(lambda e, sO=sO, ob=ob: e.tensor_copy(
                                    sO, ps[ob][:, 384:512]),
                                [(f"ps{ob}", 0, 512)], [kO])
                        yield
                        sSq, kSq = fslot()
                        sop(lambda e, sO=sO, sSq=sSq, pcol=pcol: e.activation(
                                sSq, sO, AF.Square, accum_out=prob[:, pcol:pcol + 1]),
                            [kO], [kSq, R("prob", pcol, pcol + 1)])
                        yield
                        sop(lambda e, pcol=pcol: e.activation(
                                prob[:, pcol + 1:pcol + 2], prob[:, pcol:pcol + 1],
                                AF.Ln, scale=1.0 / DV,
                                bias=cols[:, CI["rmseps"]:CI["rmseps"] + 1]),
                            [R("prob", pcol, pcol + 1),
                             R("cols", CI["rmseps"], CI["rmseps"] + 1)],
                            [R("prob", pcol + 1, pcol + 2)])
                        yield
                        sop(lambda e, pcol=pcol: e.activation(
                                prob[:, pcol + 2:pcol + 3], prob[:, pcol + 1:pcol + 2],
                                AF.Exp, scale=-0.5),
                            [R("prob", pcol + 1, pcol + 2)],
                            [R("prob", pcol + 2, pcol + 3)])
                        yield
                        onb, onbk = bslot()
                        vop(lambda e, onb=onb, sO=sO, pcol=pcol: e.tensor_scalar_mul(
                                onb, sO, prob[:, pcol + 2:pcol + 3]),
                            [kO, R("prob", pcol + 2, pcol + 3)], [onbk])
                        yield
                        pb4, pbk4 = pslot()
                        top(lambda e, onb=onb, pb4=pb4: e.transpose(pb4, onb, identb[:, :]),
                            [onbk, R("identb", 0, 128)], [pbk4])
                        yield
                        gmap = (16 + h) if h < 16 else (h - 16)
                        glo = gmap * 512 + c * 128
                        vop(lambda e, pb4=pb4, glo=glo, zr=zr: e.tensor_mul(
                                bigA[:, glo:glo + 128], zr, pb4),
                            [pbk4, zrk], [(A, glo, glo + 128)])
                        yield
                        kb, kbk = bslot()
                        cap_dec = ccol_(128, h)
                        vop(lambda e, kb=kb, ktlo=ktlo, cap=cap_dec: e.tensor_scalar_mul(
                                kb, bigD[:, ktlo: ktlo + 128],
                                cap),
                            [ktokk, ccolk(128, h)], [kbk])
                        yield
                        top(lambda e, kb=kb, dl=dl, ob=ob: e.matmul(
                                ps[ob][:, 256:384], kb, dl, start=True, stop=True),
                            [kbk, dlk], [(f"ps{ob}", 0, 512)])
                        yield
                        if c == 0:
                            vop(lambda e, h=h, ob=ob: e.tensor_copy(
                                    S_sb[:, h * 128:(h + 1) * 128], ps[ob][:, 256:384]),
                                [(f"ps{ob}", 0, 512)], [skey])
                        else:
                            cap_gc = ccol_(160, h)
                            vop(lambda e, h=h, ob=ob, cap=cap_gc: e.scalar_tensor_tensor(
                                    S_sb[:, h * 128:(h + 1) * 128],
                                    S_sb[:, h * 128:(h + 1) * 128], cap,
                                    ps[ob][:, 256:384], ALU.mult, ALU.add),
                                [skey, ccolk(160, h), (f"ps{ob}", 0, 512)], [skey])
                        yield

                    def drive(pools):
                        actives = [[] for _ in pools]
                        idx = [0] * len(pools)
                        delay = [d for (_, _, d) in pools]
                        rnd = 0
                        done = False
                        while not done:
                            done = True
                            for pi, (gens_, win, _) in enumerate(pools):
                                if rnd < delay[pi]:
                                    if idx[pi] < len(gens_):
                                        done = False
                                    continue
                                act = actives[pi]
                                while len(act) < win and idx[pi] < len(gens_):
                                    act.append(gens_[idx[pi]])
                                    idx[pi] += 1
                                if act:
                                    done = False
                                for gg in list(act):
                                    try:
                                        next(gg)
                                    except StopIteration:
                                        act.remove(gg)
                            rnd += 1

                    cc0, cck0 = mk_ccol(0)
                    drive([([r1_pair(hp, 0, cc0, cck0, {}) for hp in range(NKH)], 2, 0)])
                    for c in range(NCH):
                        ccA, cckA = mk_ccol(c)
                        pools = [([r2_head(h, c, ccA, cckA) for h in range(NVH)], 2, 0)]
                        if c + 1 < NCH:
                            ccB, cckB = mk_ccol(c + 1)
                            pools.append(([r1_pair(hp, c + 1, ccB, cckB, {})
                                           for hp in range(NKH)], 2, 48))
                        drive(pools)
                for pp2 in range(16):
                    slot = pp2 % 2
                    wkey = (Bk, slot * 4096, slot * 4096 + 4096)
                    load_panel("L0_attn_W_out", pp2, bigB, slot * 4096, wkey)
                    j = pp2
                    bank = j % 2
                    for k in range(32):
                        gk = (16 + k) if k < 16 else (k - 16)
                        top(lambda e, slot=slot, k=k, bank=bank, gk=gk: e.matmul(
                                ps[bank][:, 0:512],
                                bigB[:, slot * 4096 + k * 128:
                                     slot * 4096 + k * 128 + 128],
                                bigA[:, gk * 512:(gk + 1) * 512],
                                start=(k == 0), stop=(k == 31)),
                            [wkey, (A, gk * 512, (gk + 1) * 512)],
                            [(f"ps{bank}", 0, 512)] if k == 0 else [],
                            group_n=32 if k == 0 else 1)
                    xk = ("xfst", 0, 512)
                    dma("y", xfst[:, 0:512],
                        src_hbm[j * 128:(j + 1) * 128, t0:t0 + 512],
                        [xiv(src_key, j, t0)], [xk])
                    ykey = ("xfst", 512, 1024)
                    vop(lambda e, bank=bank: e.tensor_add(
                            xfst[:, 512:1024], ps[bank][:, 0:512], xfst[:, 0:512]),
                        [(f"ps{bank}", 0, 512), xk], [ykey])
                    dma("y", dst_hbm[j * 128:(j + 1) * 128, t0:t0 + 512],
                        xfst[:, 512:1024], [ykey], [xiv(dst_key, j, t0)])

    setup_constants()
    setup_vectors()
    setup_cast()
    B.barrier()
    seqs = []
    if "ffn1" in parts:
        seqs.append(("ffn1", x_in, "x_in"))
    chain = {"ffn1": (xA, "xA"), "attn": (xB, "xB")}
    srcs = {"ffn1": (x_in, "x_in"), "attn": (xA, "xA"), "ffn2": (xB, "xB")}
    plist = [p for p in ("ffn1", "attn", "ffn2") if p in parts]
    for i, p in enumerate(plist):
        src_hbm, src_key = srcs[p]
        if i + 1 < len(plist):
            dst_hbm, dst_key = srcs[plist[i + 1]]
        else:
            dst_hbm, dst_key = out, "out"
        if p in ("ffn1", "ffn2"):
            ffn(src_hbm, src_key, dst_hbm, dst_key)
        else:
            attn(src_hbm, src_key, dst_hbm, dst_key)
        B.barrier()
    for s_ in list(B.sems):
        if s_ == "y":
            continue
        v = B.cnt[s_]
        if v > 0 and B.waited["y"].get(s_, 0) < v:
            B.waited["y"][s_] = v
            B.stream["y"].append(([(s_, v)], None, None, 0))
    B.finalize()
    mybir.codegen_inst_isa_subclasses(nc)
    return nc


def attn(*a, **k):
    raise NotImplementedError("attn emitted later")


def kernel(**inputs):
    x = inputs["L0_attn_x"]
    nc = build_program()
    in_maps = []
    for c in range(NCORE):
        m = {}
        xc = x[2 * c: 2 * c + 2]
        m["L0_attn_x"] = np.ascontiguousarray(xc.reshape(NTOK, D).T)
        for n in WMATS:
            m[n] = np.ascontiguousarray(inputs[n])
        for n in VECS:
            m[n] = np.ascontiguousarray(inputs[n]).reshape(-1)
        in_maps.append(m)
    res = run_bass_kernel_spmd(nc, in_maps, core_ids=list(range(NCORE)))
    outs = []
    for c in range(NCORE):
        o = res.results[c]["OUT"]
        outs.append(o.T.reshape(2, SEQ, D))
    return np.concatenate(outs, axis=0)
